# version 03 (final)
# baseline (speedup 1.0000x reference)
import contextlib
import math
import numpy as np
import ml_dtypes
import concourse.bass as bass
import concourse.mybir as mybir
from concourse.bass_utils import run_bass_kernel_spmd

F32 = mybir.dt.float32
BF16 = mybir.dt.bfloat16
U32 = mybir.dt.uint32
U8 = mybir.dt.uint8
AF = mybir.ActivationFunctionType
ALU = mybir.AluOpType
AX = mybir.AxisListType

NCORES = 8
NB = 2
T = 16384
D = 1024
TLOC = T // 4
ALPHA = 0.99

bfdt = ml_dtypes.bfloat16
ENGINES = ("sync", "gpsimd", "scalar", "vector", "tensor")


class Ph:

    def __init__(self, nc, S):
        self.nc = nc
        self.barsem = S(nc.semaphore("bar"))
        self.seq = []

    def tag(self, inst):
        self._tagcount += 1
        inst.then_inc(self.barsem, 16)
        return None

    def on(self, engine, fn, ndma=0):
        self.seq.append((engine, fn, ndma))

    def dma(self, engine, fn, ndma=1):
        self.seq.append((engine, fn, ndma))

    def bar(self):
        if self.seq and self.seq[-1] is not None:
            self.seq.append(None)

    def emit(self, S):
        self.bar()
        totals = []
        run = 0
        ndma_ph = 0
        for it in self.seq:
            if it is None:
                run += 16 * ndma_ph + 5
                totals.append(run)
                ndma_ph = 0
            else:
                ndma_ph += it[2]

        seq = self.seq
        block = S(self.nc.Block())
        ph = self

        def mk(eng_name):
            def body(eng):
                bi = 0
                lastinst = None
                declared = 0
                ph._tagcount = 0
                for it in seq:
                    if it is None:
                        assert ph._tagcount == declared, (
                            f"{eng_name} phase {bi}: tagged {ph._tagcount} != declared {declared}")
                        if lastinst is None:
                            eng.nop().then_inc(ph.barsem, 1)
                        else:
                            lastinst.then_inc(ph.barsem, 1)
                        eng.wait_ge(ph.barsem, totals[bi])
                        bi += 1
                        lastinst = None
                        declared = 0
                        ph._tagcount = 0
                    elif it[0] == eng_name:
                        declared += it[2]
                        r = it[1](eng)
                        if r is not None:
                            lastinst = r
            return body

        block.sync(mk("sync"))
        block.gpsimd(mk("gpsimd"))
        block.scalar(mk("scalar"))
        block.vector(mk("vector"))
        block.tensor(mk("tensor"))


def cdiv(a, b):
    return (a + b - 1) // b


def fmT(x):
    return np.ascontiguousarray(np.asarray(x).T, dtype=np.float32)


def cont(x, dt=np.float32):
    return np.ascontiguousarray(np.asarray(x), dtype=dt)


def col128(v):
    v = np.asarray(v, dtype=np.float32)
    return np.ascontiguousarray(v.reshape(-1, 128).T)


def setup_identity(ph, ident_f32, ident_bf):
    def f_io(e):
        return e.iota(ident_f32[:, :], pattern=[[-1, 128]], base=0,
                      channel_multiplier=1, allow_small_or_imprecise_dtypes=True)
    ph.on("gpsimd", f_io)
    ph.bar()

    def f_eq(e):
        e.tensor_scalar(ident_f32[:, :], ident_f32[:, :], 0.0, None, ALU.is_equal)
        return e.tensor_copy(ident_bf[:, :], ident_f32[:, :])
    ph.on("vector", f_eq)
    ph.bar()


def ln_tm(ph, x_tm, out, sq, mcol, vcol, rcol, dm, eps=1e-5):
    ph.on("scalar", lambda e: e.activation(sq, x_tm, AF.Square))
    ph.bar()

    def red(e):
        e.reduce_sum(mcol, x_tm, axis=AX.X)
        return e.reduce_sum(vcol, sq, axis=AX.X)
    ph.on("vector", red)
    ph.bar()

    def stats(e):
        e.tensor_scalar(mcol, mcol, 1.0 / dm, None, ALU.mult)
        return e.tensor_scalar(vcol, vcol, 1.0 / dm, None, ALU.mult)
    ph.on("vector", stats)
    ph.bar()
    ph.on("vector", lambda e: e.tensor_mul(sq[:, 0:1], mcol, mcol))
    ph.bar()
    ph.on("vector", lambda e: e.scalar_tensor_tensor(vcol, sq[:, 0:1], -1.0, vcol, ALU.mult, ALU.add))
    ph.bar()
    ph.on("vector", lambda e: e.tensor_scalar(vcol, vcol, float(eps), None, ALU.add))
    ph.bar()
    ph.on("scalar", lambda e: e.activation(rcol, vcol, AF.Ln))
    ph.bar()
    ph.on("scalar", lambda e: e.activation(rcol, rcol, AF.Exp, scale=-0.5))
    ph.bar()
    ph.on("vector", lambda e: e.tensor_scalar(out, x_tm, mcol, rcol, ALU.subtract, ALU.mult))
    ph.bar()


L0_CH = 512
L0_HALO = 4096


def build_l01():
    nc = bass.Bass("TRN2", target_bir_lowering=False, num_devices=NCORES)
    CH = L0_CH
    NHC = L0_HALO // CH
    NOC = TLOC // CH
    LNA = float(np.log(ALPHA))
    AL = float(ALPHA ** CH)

    xh = nc.dram_tensor("xh_fm", [D, L0_HALO + TLOC], F32, kind="ExternalInput").ap()
    x_tm = nc.dram_tensor("x_tm", [TLOC, D], F32, kind="ExternalInput").ap()
    wqkvT = nc.dram_tensor("wqkvT", [D, 3 * D], F32, kind="ExternalInput").ap()
    bqkv = nc.dram_tensor("bqkv", [128, 24], F32, kind="ExternalInput").ap()
    hvalid = nc.dram_tensor("hvalid", [128, 1], F32, kind="ExternalInput").ap()
    out = nc.dram_tensor("y_tm", [TLOC, D], F32, kind="ExternalOutput").ap()

    with contextlib.ExitStack() as es:
        S = es.enter_context
        w_bf = S(nc.sbuf_tensor("w_bf", [128, 8 * 3 * D], BF16))
        bq = S(nc.sbuf_tensor("bq", [128, 24], F32))
        bq_q = S(nc.sbuf_tensor("bq_q", [128, 8], F32))
        x_bf = S(nc.sbuf_tensor("x_bf", [128, 8 * CH], BF16))
        kfm = S(nc.sbuf_tensor("kfm", [128, 8 * CH], BF16))
        vfm = S(nc.sbuf_tensor("vfm", [128, 8 * CH], BF16))
        qfm = S(nc.sbuf_tensor("qfm", [128, 8 * CH], BF16))
        kT = S(nc.sbuf_tensor("kT", [128, 4 * D], BF16))
        vT = S(nc.sbuf_tensor("vT", [128, 4 * D], BF16))
        sT = S(nc.sbuf_tensor("sT", [128, 4 * CH], BF16))
        S_sb = S(nc.sbuf_tensor("S_sb", [128, 8 * D], F32))
        S_bf = S(nc.sbuf_tensor("S_bf", [128, 8 * D], BF16))
        o_tm = S(nc.sbuf_tensor("o_tm", [128, 4 * D], F32))
        res_tm = S(nc.sbuf_tensor("res_tm", [128, 4 * D], F32))
        mask = S(nc.sbuf_tensor("maskT", [128, 4 * CH], F32))
        iot = S(nc.sbuf_tensor("iot", [128, 4 * CH], F32))
        dec_i = S(nc.sbuf_tensor("dec_i", [128, 4], F32))
        kdec = S(nc.sbuf_tensor("kdec", [128, 4], F32))
        kdec_h = S(nc.sbuf_tensor("kdec_h", [128, 4], F32))
        hv = S(nc.sbuf_tensor("hv", [128, 1], F32))
        ident_f = S(nc.sbuf_tensor("ident_f", [128, 128], F32))
        ident = S(nc.sbuf_tensor("ident", [128, 128], BF16))
        ln_sq = S(nc.sbuf_tensor("ln_sq", [128, D], F32))
        ln_m = S(nc.sbuf_tensor("ln_m", [128, 1], F32))
        ln_v = S(nc.sbuf_tensor("ln_v", [128, 1], F32))
        ln_r = S(nc.sbuf_tensor("ln_r", [128, 1], F32))

        pMM = S(nc.psum_tensor("pMM", [128, 512], F32))
        pO = S(nc.psum_tensor("pO", [128, 1024], F32))
        pI = S(nc.psum_tensor("pI", [128, 1024], F32))
        pS = S(nc.psum_tensor("pS", [128, 512], F32))
        pT = S(nc.psum_tensor("pT", [128, 512], BF16))

        ph = Ph(nc, S)

        def ld_w(e):
            ph.tag(e.dma_start(out=bq[:, :], in_=bqkv))
            ph.tag(e.dma_start(out=hv[:, :], in_=hvalid))
        ph.on("sync", ld_w, ndma=2)

        def ld_wbf(e):
            ph.tag(e.dma_start(out=w_bf.ap().rearrange("p (a o) -> p a o", a=8),
                               in_=wqkvT.rearrange("(a p) o -> p a o", p=128)))
        ph.on("gpsimd", ld_wbf, ndma=1)

        def mk_iota(e):
            r = None
            for jt in range(4):
                r = e.iota(iot[:, jt * CH:(jt + 1) * CH], pattern=[[1, CH]],
                           base=-(jt * 128), channel_multiplier=-1,
                           allow_small_or_imprecise_dtypes=True)
            for it in range(4):
                r = e.iota(dec_i[:, it:it + 1], pattern=[[0, 1]], base=it * 128 + 1,
                           channel_multiplier=1, allow_small_or_imprecise_dtypes=True)
            for jt in range(4):
                r = e.iota(kdec[:, jt:jt + 1], pattern=[[0, 1]], base=CH - 1 - jt * 128,
                           channel_multiplier=-1, allow_small_or_imprecise_dtypes=True)
            r = e.iota(ident_f[:, :], pattern=[[-1, 128]], base=0,
                       channel_multiplier=1, allow_small_or_imprecise_dtypes=True)
            return r
        ph.on("gpsimd", mk_iota)
        ph.bar()

        def mk_masks(e):
            e.activation(mask[:, :], iot[:, :], AF.Exp, scale=LNA)
            e.activation(dec_i[:, :], dec_i[:, :], AF.Exp, scale=LNA)
            return e.activation(kdec[:, :], kdec[:, :], AF.Exp, scale=LNA)
        ph.on("scalar", mk_masks)
        ph.bar()

        def mk_mask2(e):
            e.tensor_scalar(iot[:, :], iot[:, :], 0.0, None, ALU.is_ge)
            e.tensor_scalar(ident_f[:, :], ident_f[:, :], 0.0, None, ALU.is_equal)
            e.tensor_scalar(bq_q[:, :], bq[:, 0:8], 1.0 / 32.0, None, ALU.mult)
            e.tensor_scalar(kdec_h[:, :], kdec[:, :], hv[:, 0:1], None, ALU.mult)
            return e.memset(S_sb[:, :], 0.0)
        ph.on("vector", mk_mask2)
        ph.bar()

        def mk_mask3(e):
            e.tensor_mul(mask[:, :], mask[:, :], iot[:, :])
            return e.tensor_copy(ident[:, :], ident_f[:, :])
        ph.on("vector", mk_mask3)
        ph.bar()

        for ci in range(NHC + NOC):
            own = ci >= NHC
            c0 = ci * CH
            oc0 = (ci - NHC) * CH
            kcol = kdec if own else kdec_h

            def ldx(e, c0=c0):
                ph.tag(e.dma_start(out=x_bf.ap().rearrange("p (a t) -> p a t", a=8),
                                   in_=xh.rearrange("(a p) t -> p a t", p=128)[:, :, c0:c0 + CH]))
            ph.on("gpsimd", ldx, ndma=1)
            if own:
                def ldres(e, oc0=oc0):
                    ph.tag(e.dma_start(out=res_tm.ap().rearrange("p (a d) -> p a d", a=4),
                                       in_=x_tm[oc0:oc0 + CH, :].rearrange("(a p) d -> p a d", p=128)))
                ph.on("sync", ldres, ndma=1)
            ph.bar()

            plist = [(8, kfm), (16, vfm)] + ([(0, qfm)] if own else [])
            for base, dst in plist:
                for ot in range(8):
                    po = base + ot

                    def mm1(e, po=po):
                        r = None
                        for i in range(8):
                            r = e.matmul(pMM[:, :], w_bf[:, i * 3 * D + po * 128:i * 3 * D + po * 128 + 128],
                                         x_bf[:, i * CH:(i + 1) * CH], start=(i == 0), stop=(i == 7))
                        return r
                    ph.on("tensor", mm1)
                    ph.bar()
                    if base == 0:
                        def ev1(e, dst=dst, ot=ot, po=po):
                            return e.tensor_scalar(dst[:, ot * CH:(ot + 1) * CH], pMM[:, :],
                                                   bq[:, po:po + 1], 1.0 / 32.0, ALU.add, ALU.mult)
                    else:
                        def ev1(e, dst=dst, ot=ot, po=po):
                            return e.tensor_scalar(dst[:, ot * CH:(ot + 1) * CH], pMM[:, :],
                                                   bq[:, po:po + 1], None, ALU.add)
                    ph.on("vector", ev1)
                    ph.bar()

            for src, dst, scaled in ((kfm, kT, True), (vfm, vT, False)):
                for tt in range(4):
                    for half in range(2):
                        def trp(e, src=src, tt=tt, half=half):
                            r = None
                            for dd in range(4):
                                d_t = half * 4 + dd
                                r = e.transpose(pT[:, dd * 128:(dd + 1) * 128],
                                                src[:, d_t * CH + tt * 128:d_t * CH + tt * 128 + 128],
                                                ident[:, :])
                            return r
                        ph.on("tensor", trp)
                        ph.bar()
                        if scaled:
                            def evt(e, dst=dst, tt=tt, half=half, kcol=kcol):
                                return e.tensor_scalar(dst[:, tt * D + half * 512:tt * D + half * 512 + 512],
                                                       pT[:, :], kcol[:, tt:tt + 1], None, ALU.mult)
                        else:
                            def evt(e, dst=dst, tt=tt, half=half):
                                return e.tensor_copy(dst[:, tt * D + half * 512:tt * D + half * 512 + 512],
                                                     pT[:, :])
                        ph.on("vector", evt)
                        ph.bar()

            if own:
                for jt in range(4):
                    def mm_sc(e, jt=jt):
                        r = None
                        for i in range(8):
                            r = e.matmul(pMM[:, :], kfm[:, i * CH + jt * 128:i * CH + jt * 128 + 128],
                                         qfm[:, i * CH:(i + 1) * CH], start=(i == 0), stop=(i == 7))
                        return r
                    ph.on("tensor", mm_sc)
                    ph.bar()

                    def ev_sc(e, jt=jt):
                        return e.tensor_mul(sT[:, jt * CH:(jt + 1) * CH], pMM[:, :],
                                            mask[:, jt * CH:(jt + 1) * CH])
                    ph.on("vector", ev_sc)
                    ph.bar()

                for it in range(4):
                    def mm_oi(e, it=it):
                        r = None
                        for half in range(2):
                            for jt in range(4):
                                r = e.matmul(pO[:, half * 512:(half + 1) * 512],
                                             sT[:, jt * CH + it * 128:jt * CH + it * 128 + 128],
                                             vT[:, jt * D + half * 512:jt * D + half * 512 + 512],
                                             start=(jt == 0), stop=(jt == 3))
                        for half in range(2):
                            for dkt in range(8):
                                r = e.matmul(pI[:, half * 512:(half + 1) * 512],
                                             qfm[:, dkt * CH + it * 128:dkt * CH + it * 128 + 128],
                                             S_bf[:, dkt * D + half * 512:dkt * D + half * 512 + 512],
                                             start=(dkt == 0), stop=(dkt == 7))
                        return r
                    ph.on("tensor", mm_oi)
                    ph.bar()

                    def ev_oi(e, it=it):
                        e.tensor_copy(o_tm[:, it * D:it * D + 1024], pO[:, :])
                        return e.scalar_tensor_tensor(o_tm[:, it * D:it * D + 1024], pI[:, :],
                                                      dec_i[:, it:it + 1], o_tm[:, it * D:it * D + 1024],
                                                      ALU.mult, ALU.add)
                    ph.on("vector", ev_oi)
                    ph.bar()

            for dkt in range(8):
                for half in range(2):
                    def mm_s(e, dkt=dkt, half=half):
                        r = None
                        for tt in range(4):
                            r = e.matmul(pS[:, :], kT[:, tt * D + dkt * 128:tt * D + dkt * 128 + 128],
                                         vT[:, tt * D + half * 512:tt * D + half * 512 + 512],
                                         start=(tt == 0), stop=(tt == 3))
                        return r
                    ph.on("tensor", mm_s)
                    ph.bar()

                    def ev_s(e, dkt=dkt, half=half):
                        sl = S_sb[:, dkt * D + half * 512:dkt * D + half * 512 + 512]
                        return e.scalar_tensor_tensor(sl, sl, AL, pS[:, :], ALU.mult, ALU.add)
                    ph.on("vector", ev_s)
                    ph.bar()
            def cast_s(e):
                return e.tensor_copy(S_bf[:, :], S_sb[:, :])
            ph.on("vector", cast_s)
            ph.bar()

            if own:
                def addres(e):
                    return e.tensor_add(o_tm[:, :], o_tm[:, :], res_tm[:, :])
                ph.on("vector", addres)
                ph.bar()
                for it in range(4):
                    ln_tm(ph, o_tm[:, it * D:(it + 1) * D], o_tm[:, it * D:(it + 1) * D],
                          ln_sq[:, :], ln_m[:, :], ln_v[:, :], ln_r[:, :], D)

                def store(e, oc0=oc0):
                    ph.tag(e.dma_start(out=out[oc0:oc0 + CH, :].rearrange("(a p) d -> p a d", p=128),
                                       in_=o_tm.ap().rearrange("p (a d) -> p a d", a=4)))
                ph.on("sync", store, ndma=1)
                ph.bar()

        ph.emit(S)
    return nc


def l01_inmaps(x0, wqkv, bqkv):
    wT = fmT(wqkv)
    bq = col128(bqkv)
    ins = []
    for c in range(NCORES):
        b = c // 4
        q = c % 4
        t0 = q * TLOC
        xb = x0[b]
        own_fm = fmT(xb[t0:t0 + TLOC])
        if q == 0:
            halo = np.zeros((D, L0_HALO), np.float32)
            hvalid = np.zeros((128, 1), np.float32)
        else:
            halo = fmT(xb[t0 - L0_HALO:t0])
            hvalid = np.ones((128, 1), np.float32)
        ins.append({
            "xh_fm": np.concatenate([halo, own_fm], axis=1),
            "x_tm": cont(xb[t0:t0 + TLOC]),
            "wqkvT": wT,
            "bqkv": bq,
            "hvalid": hvalid,
        })
    return ins


def kernel(**inputs):
    inp = inputs
    cores = list(range(NCORES))

    def launch(nc, ins):
        last = None
        for _ in range(3):
            try:
                return run_bass_kernel_spmd(nc, ins, core_ids=cores)
            except Exception as ex:
                last = ex
        raise last

    def assemble_tm(res, key="y_tm"):
        out = np.empty((NB, T, D), np.float32)
        for c in cores:
            b, q = c // 4, c % 4
            out[b, q * TLOC:(q + 1) * TLOC] = np.asarray(res.results[c][key])
        return out

    def fm_full(x_bt):
        return [np.ascontiguousarray(x_bt[b].T) for b in range(NB)]

    def glob_fm(res, key, rows, trim=3):
        g = np.empty((rows, NB * T), np.float32)
        for c in cores:
            b, q = c // 4, c % 4
            cs = b * T + q * TLOC
            g[:, cs:cs + TLOC] = np.asarray(res.results[c][key])[:, trim:trim + TLOC]
        return g

    def glob_rows(res, key, rows):
        g = np.empty((rows, NB * T), np.float32)
        for c in cores:
            b, q = c // 4, c % 4
            cs = b * T + q * TLOC
            g[:, cs:cs + TLOC] = np.asarray(res.results[c][key])
        return g

    def revb(g):
        r = np.empty_like(g)
        for b in range(NB):
            r[:, b * T:(b + 1) * T] = g[:, b * T:(b + 1) * T][:, ::-1]
        return np.ascontiguousarray(r)

    x0 = np.ascontiguousarray(np.asarray(inp["L0_attn_x"], np.float32))
    res = launch(build_l01(),
                               l01_inmaps(x0, inp["L0_attn_W_qkv"], inp["L0_attn_b_qkv"]))
    x1 = assemble_tm(res)

    res = launch(build_l02(),
                               l02_inmaps(fm_full(x1), inp["L0_ffn_W_r"], inp["L0_ffn_W_k"],
                                          inp["L0_ffn_W_v"], inp["L0_ffn_mu_r"], inp["L0_ffn_mu_k"]))
    x2 = assemble_tm(res)
    x2fm = fm_full(x2)

    res = launch(build_l03(), l03_inmaps(x2fm, inp))
    za_g = glob_fm(res, "z_act", D)
    uf_g = glob_fm(res, "u_f", D)
    ub_g = glob_fm(res, "u_b", D)
    df_g = glob_fm(res, "delta_f", D)
    db_g = glob_fm(res, "delta_b", D)
    bcf_g = glob_fm(res, "bc_f", 32)
    bcb_g = glob_fm(res, "bc_b", 32)

    nsc = build_scan(1)
    res = launch(nsc,
                               scan_inmaps(df_g, uf_g, bcf_g[:16], bcf_g[16:], inp["L1_attn_A_log"]))
    yf_g = np.concatenate([np.asarray(res.results[c]["y_sl"]) for c in cores], axis=0)
    res = launch(nsc,
                               scan_inmaps(revb(db_g), revb(ub_g), revb(bcb_g[:16]),
                                           revb(bcb_g[16:]), inp["L1_attn_A_log_b"]))
    yb_g = revb(np.concatenate([np.asarray(res.results[c]["y_sl"]) for c in cores], axis=0))

    res = launch(build_l06(),
                               l06_inmaps(x2fm, yf_g, yb_g, uf_g, ub_g, za_g, inp))
    x3 = assemble_tm(res)
    lg = glob_rows(res, "lg_sl", 32)

    x3tm = np.ascontiguousarray(x3.reshape(-1, D))
    res = launch(build_l07(), l07_inmaps(lg, x3tm, inp))
    ybuf_g = np.concatenate([np.asarray(res.results[c]["ybuf_sl"]) for c in cores], axis=0)
    slotc = np.asarray(res.results[0]["slotc"])
    gatec = np.asarray(res.results[0]["gatec"])

    res = launch(build_l08(),
                               l08_inmaps(fm_full(x3), ybuf_g, slotc, gatec))
    x4 = assemble_tm(res)
    x4fm = fm_full(x4)

    res = launch(build_l09(), l09_inmaps(x4fm, inp))
    r2_g = glob_fm(res, "res_act", 2048)
    u2_g = glob_fm(res, "u", 2048)
    d2_g = glob_fm(res, "delta", 2048)
    bc2_g = glob_fm(res, "bc", 32)

    res = launch(build_scan(2),
                               scan_inmaps(d2_g, u2_g, bc2_g[:16], bc2_g[16:], inp["L2_attn_A_log"]))
    y2_g = np.concatenate([np.asarray(res.results[c]["y_sl"]) for c in cores], axis=0)

    res = launch(build_l11(),
                               l11_inmaps(x4fm, y2_g, u2_g, r2_g, inp))
    x5 = assemble_tm(res)

    res = launch(build_l12(), l12_inmaps(fm_full(x5), inp))
    x6 = assemble_tm(res)
    x6fm = fm_full(x6)

    res = launch(build_l13a(), l13a_inmaps(x6fm, inp))
    x7fm = [np.concatenate([np.asarray(res.results[b * 4 + q]["x7_fm"]) for q in range(4)], axis=1)
            for b in range(NB)]
    res = launch(build_l13b(), l13b_inmaps(x7fm, x6fm, inp))
    x8 = assemble_tm(res)
    lg8 = glob_rows(res, "lg8", 8)

    x8tm = np.ascontiguousarray(x8.reshape(-1, D))
    res = launch(build_l14(), l14_inmaps(lg8, x8tm, inp))
    ybuf3 = np.concatenate([np.asarray(res.results[c]["ybuf3_sl"]) for c in cores], axis=0)
    slotc3 = np.asarray(res.results[0]["slotc3"])

    res = launch(build_l15(), l15_inmaps(fm_full(x8), ybuf3, slotc3))
    return assemble_tm(res)


def build_l02():
    nc = bass.Bass("TRN2", target_bir_lowering=False, num_devices=NCORES)
    CH = 512
    NOC = TLOC // CH
    xh = nc.dram_tensor("xh_fm", [D, 1 + TLOC], F32, kind="ExternalInput").ap()
    x_tm = nc.dram_tensor("x_tm", [TLOC, D], F32, kind="ExternalInput").ap()
    wrT = nc.dram_tensor("wrT", [D, D], F32, kind="ExternalInput").ap()
    wkT = nc.dram_tensor("wkT", [D, D], F32, kind="ExternalInput").ap()
    wvT = nc.dram_tensor("wvT", [D, D], F32, kind="ExternalInput").ap()
    mu_r = nc.dram_tensor("mu_r", [128, 8], F32, kind="ExternalInput").ap()
    mu_k = nc.dram_tensor("mu_k", [128, 8], F32, kind="ExternalInput").ap()
    out = nc.dram_tensor("y_tm", [TLOC, D], F32, kind="ExternalOutput").ap()

    with contextlib.ExitStack() as es:
        S = es.enter_context
        wr_bf = S(nc.sbuf_tensor("wr_bf", [128, 8 * D], BF16))
        wk_bf = S(nc.sbuf_tensor("wk_bf", [128, 8 * D], BF16))
        wv_bf = S(nc.sbuf_tensor("wv_bf", [128, 8 * D], BF16))
        mur = S(nc.sbuf_tensor("mur", [128, 8], F32))
        muk = S(nc.sbuf_tensor("muk", [128, 8], F32))
        murn = S(nc.sbuf_tensor("murn", [128, 8], F32))
        mukn = S(nc.sbuf_tensor("mukn", [128, 8], F32))
        x_f = S(nc.sbuf_tensor("x_f", [128, 8 * CH], F32))
        xp_f = S(nc.sbuf_tensor("xp_f", [128, 8 * CH], F32))
        xr_bf = S(nc.sbuf_tensor("xr_bf", [128, 8 * CH], BF16))
        xk_bf = S(nc.sbuf_tensor("xk_bf", [128, 8 * CH], BF16))
        tmp = S(nc.sbuf_tensor("tmp", [128, 8 * CH], F32))
        kact = S(nc.sbuf_tensor("kact", [128, 8 * CH], BF16))
        r_tm = S(nc.sbuf_tensor("r_tm", [128, 4 * D], F32))
        o_tm = S(nc.sbuf_tensor("o_tm", [128, 4 * D], F32))
        res_tm = S(nc.sbuf_tensor("res_tm", [128, 4 * D], F32))
        ln_sq = S(nc.sbuf_tensor("ln_sq", [128, D], F32))
        ln_m = S(nc.sbuf_tensor("ln_m", [128, 1], F32))
        ln_v = S(nc.sbuf_tensor("ln_v", [128, 1], F32))
        ln_r = S(nc.sbuf_tensor("ln_r", [128, 1], F32))
        pR = S(nc.psum_tensor("pR", [128, 1024], F32))
        pK = S(nc.psum_tensor("pK", [128, 512], F32))
        pV = S(nc.psum_tensor("pV", [128, 1024], F32))

        ph = Ph(nc, S)

        def ld_w(e):
            ph.tag(e.dma_start(out=wr_bf.ap().rearrange("p (a o) -> p a o", a=8),
                               in_=wrT.rearrange("(a p) o -> p a o", p=128)))
            ph.tag(e.dma_start(out=wk_bf.ap().rearrange("p (a o) -> p a o", a=8),
                               in_=wkT.rearrange("(a p) o -> p a o", p=128)))
            ph.tag(e.dma_start(out=wv_bf.ap().rearrange("p (a o) -> p a o", a=8),
                               in_=wvT.rearrange("(a p) o -> p a o", p=128)))
        ph.on("gpsimd", ld_w, ndma=3)

        def ld_mu(e):
            ph.tag(e.dma_start(out=mur[:, :], in_=mu_r))
            ph.tag(e.dma_start(out=muk[:, :], in_=mu_k))
        ph.on("sync", ld_mu, ndma=2)
        ph.bar()

        def neg_mu(e):
            e.tensor_scalar(murn[:, :], mur[:, :], -1.0, 1.0, ALU.mult, ALU.add)
            return e.tensor_scalar(mukn[:, :], muk[:, :], -1.0, 1.0, ALU.mult, ALU.add)
        ph.on("vector", neg_mu)
        ph.bar()

        for ci in range(NOC):
            c0 = ci * CH

            def ldx(e, c0=c0):
                ph.tag(e.dma_start(out=x_f.ap().rearrange("p (a t) -> p a t", a=8),
                                   in_=xh.rearrange("(a p) t -> p a t", p=128)[:, :, c0 + 1:c0 + 1 + CH]))
                ph.tag(e.dma_start(out=xp_f.ap().rearrange("p (a t) -> p a t", a=8),
                                   in_=xh.rearrange("(a p) t -> p a t", p=128)[:, :, c0:c0 + CH]))
                ph.tag(e.dma_start(out=res_tm.ap().rearrange("p (a d) -> p a d", a=4),
                                   in_=x_tm[c0:c0 + CH, :].rearrange("(a p) d -> p a d", p=128)))
            ph.on("sync", ldx, ndma=3)
            ph.bar()

            def mk_xr1(e):
                r = None
                for dt in range(8):
                    sl = slice(dt * CH, (dt + 1) * CH)
                    r = e.tensor_scalar(tmp[:, sl], x_f[:, sl], mur[:, dt:dt + 1], None, ALU.mult)
                return r
            ph.on("vector", mk_xr1)
            ph.bar()

            def mk_xr2(e):
                r = None
                for dt in range(8):
                    sl = slice(dt * CH, (dt + 1) * CH)
                    r = e.scalar_tensor_tensor(xr_bf[:, sl], xp_f[:, sl], murn[:, dt:dt + 1],
                                               tmp[:, sl], ALU.mult, ALU.add)
                return r
            ph.on("vector", mk_xr2)
            ph.bar()

            def mk_xk1(e):
                r = None
                for dt in range(8):
                    sl = slice(dt * CH, (dt + 1) * CH)
                    r = e.tensor_scalar(tmp[:, sl], x_f[:, sl], muk[:, dt:dt + 1], None, ALU.mult)
                return r
            ph.on("vector", mk_xk1)
            ph.bar()

            def mk_xk2(e):
                r = None
                for dt in range(8):
                    sl = slice(dt * CH, (dt + 1) * CH)
                    r = e.scalar_tensor_tensor(xk_bf[:, sl], xp_f[:, sl], mukn[:, dt:dt + 1],
                                               tmp[:, sl], ALU.mult, ALU.add)
                return r
            ph.on("vector", mk_xk2)
            ph.bar()

            for tt in range(4):
                def mm_r(e, tt=tt):
                    r = None
                    for half in range(2):
                        for i in range(8):
                            r = e.matmul(pR[:, half * 512:(half + 1) * 512],
                                         xr_bf[:, i * CH + tt * 128:i * CH + tt * 128 + 128],
                                         wr_bf[:, i * D + half * 512:i * D + half * 512 + 512],
                                         start=(i == 0), stop=(i == 7))
                    return r
                ph.on("tensor", mm_r)
                ph.bar()
                ph.on("scalar", lambda e, tt=tt: e.activation(r_tm[:, tt * D:(tt + 1) * D], pR[:, :], AF.Sigmoid))
                ph.bar()

            for ot in range(8):
                def mm_k(e, ot=ot):
                    r = None
                    for i in range(8):
                        r = e.matmul(pK[:, :], wk_bf[:, i * D + ot * 128:i * D + ot * 128 + 128],
                                     xk_bf[:, i * CH:(i + 1) * CH], start=(i == 0), stop=(i == 7))
                    return r
                ph.on("tensor", mm_k)
                ph.bar()
                ph.on("scalar", lambda e, ot=ot: e.activation(kact[:, ot * CH:(ot + 1) * CH], pK[:, :], AF.Relu))
                ph.bar()
            def sq_k(e):
                return e.tensor_mul(kact[:, :], kact[:, :], kact[:, :])
            ph.on("vector", sq_k)
            ph.bar()

            for tt in range(4):
                def mm_v(e, tt=tt):
                    r = None
                    for half in range(2):
                        for i in range(8):
                            r = e.matmul(pV[:, half * 512:(half + 1) * 512],
                                         kact[:, i * CH + tt * 128:i * CH + tt * 128 + 128],
                                         wv_bf[:, i * D + half * 512:i * D + half * 512 + 512],
                                         start=(i == 0), stop=(i == 7))
                    return r
                ph.on("tensor", mm_v)
                ph.bar()

                def ev_v(e, tt=tt):
                    return e.tensor_mul(o_tm[:, tt * D:(tt + 1) * D], r_tm[:, tt * D:(tt + 1) * D], pV[:, :])
                ph.on("vector", ev_v)
                ph.bar()

            ph.on("vector", lambda e: e.tensor_add(o_tm[:, :], o_tm[:, :], res_tm[:, :]))
            ph.bar()
            for it in range(4):
                ln_tm(ph, o_tm[:, it * D:(it + 1) * D], o_tm[:, it * D:(it + 1) * D],
                      ln_sq[:, :], ln_m[:, :], ln_v[:, :], ln_r[:, :], D)

            def store(e, c0=c0):
                ph.tag(e.dma_start(out=out[c0:c0 + CH, :].rearrange("(a p) d -> p a d", p=128),
                                   in_=o_tm.ap().rearrange("p (a d) -> p a d", a=4)))
            ph.on("sync", store, ndma=1)
            ph.bar()

        ph.emit(S)
    return nc


def l02_inmaps(x1_fm_full, w_r, w_k, w_v, mu_r, mu_k):
    ins = []
    for c in range(NCORES):
        b, q = c // 4, c % 4
        t0 = q * TLOC
        xb = x1_fm_full[b]
        if q == 0:
            prevcol = np.zeros((D, 1), np.float32)
        else:
            prevcol = xb[:, t0 - 1:t0]
        ins.append({
            "xh_fm": np.ascontiguousarray(np.concatenate([prevcol, xb[:, t0:t0 + TLOC]], axis=1)),
            "x_tm": cont(xb[:, t0:t0 + TLOC].T),
            "wrT": fmT(w_r), "wkT": fmT(w_k), "wvT": fmT(w_v),
            "mu_r": col128(mu_r), "mu_k": col128(mu_k),
        })
    return ins


def ln_tm_batch(ph, xv, outv, sq, mc, vc, rc, nt, dm, eps=1e-5):
    def f_sq(e):
        r = None
        for i in range(nt):
            r = e.activation(sq[:, i * dm:(i + 1) * dm], xv(i), AF.Square)
        return r
    ph.on("scalar", f_sq)
    ph.bar()

    def f_red(e):
        r = None
        for i in range(nt):
            e.reduce_sum(mc[:, i:i + 1], xv(i), axis=AX.X)
            r = e.reduce_sum(vc[:, i:i + 1], sq[:, i * dm:(i + 1) * dm], axis=AX.X)
        return r
    ph.on("vector", f_red)
    ph.bar()

    def f_sc(e):
        e.tensor_scalar(mc[:, :nt], mc[:, :nt], 1.0 / dm, None, ALU.mult)
        return e.tensor_scalar(vc[:, :nt], vc[:, :nt], 1.0 / dm, None, ALU.mult)
    ph.on("vector", f_sc)
    ph.bar()
    ph.on("vector", lambda e: e.tensor_mul(rc[:, :nt], mc[:, :nt], mc[:, :nt]))
    ph.bar()
    ph.on("vector", lambda e: e.scalar_tensor_tensor(vc[:, :nt], rc[:, :nt], -1.0, vc[:, :nt], ALU.mult, ALU.add))
    ph.bar()
    ph.on("vector", lambda e: e.tensor_scalar(vc[:, :nt], vc[:, :nt], float(eps), None, ALU.add))
    ph.bar()
    ph.on("scalar", lambda e: e.activation(rc[:, :nt], vc[:, :nt], AF.Ln))
    ph.bar()
    ph.on("scalar", lambda e: e.activation(rc[:, :nt], rc[:, :nt], AF.Exp, scale=-0.5))
    ph.bar()

    def f_norm(e):
        r = None
        for i in range(nt):
            r = e.tensor_scalar(outv(i), xv(i), mc[:, i:i + 1], rc[:, i:i + 1], ALU.subtract, ALU.mult)
        return r
    ph.on("vector", f_norm)
    ph.bar()


TEXT = TLOC + 128


def build_l03():
    nc = bass.Bass("TRN2", target_bir_lowering=False, num_devices=NCORES)
    x_tme = nc.dram_tensor("x_tme", [TEXT, D], F32, kind="ExternalInput").ap()
    ipwT = nc.dram_tensor("ipwT", [D, 2 * D], F32, kind="ExternalInput").ap()
    ln_g = nc.dram_tensor("ln_g", [128, 8], F32, kind="ExternalInput").ap()
    ln_b = nc.dram_tensor("ln_b", [128, 8], F32, kind="ExternalInput").ap()
    cwf = nc.dram_tensor("cwf", [128, 32], F32, kind="ExternalInput").ap()
    cbf = nc.dram_tensor("cbf", [128, 8], F32, kind="ExternalInput").ap()
    cwb = nc.dram_tensor("cwb", [128, 32], F32, kind="ExternalInput").ap()
    cbb = nc.dram_tensor("cbb", [128, 8], F32, kind="ExternalInput").ap()
    xpwfT = nc.dram_tensor("xpwfT", [D, 96], F32, kind="ExternalInput").ap()
    xpwbT = nc.dram_tensor("xpwbT", [D, 96], F32, kind="ExternalInput").ap()
    dtwfT = nc.dram_tensor("dtwfT", [64, D], F32, kind="ExternalInput").ap()
    dtwbT = nc.dram_tensor("dtwbT", [64, D], F32, kind="ExternalInput").ap()
    dtbf = nc.dram_tensor("dtbf", [128, 8], F32, kind="ExternalInput").ap()
    dtbb = nc.dram_tensor("dtbb", [128, 8], F32, kind="ExternalInput").ap()
    hlv = nc.dram_tensor("hlv", [128, 1], F32, kind="ExternalInput").ap()
    hrv = nc.dram_tensor("hrv", [128, 1], F32, kind="ExternalInput").ap()

    o_z = nc.dram_tensor("z_act", [D, TEXT], F32, kind="ExternalOutput").ap()
    o_uf = nc.dram_tensor("u_f", [D, TEXT], F32, kind="ExternalOutput").ap()
    o_ub = nc.dram_tensor("u_b", [D, TEXT], F32, kind="ExternalOutput").ap()
    o_df = nc.dram_tensor("delta_f", [D, TEXT], F32, kind="ExternalOutput").ap()
    o_db = nc.dram_tensor("delta_b", [D, TEXT], F32, kind="ExternalOutput").ap()
    o_bcf = nc.dram_tensor("bc_f", [32, TEXT], F32, kind="ExternalOutput").ap()
    o_bcb = nc.dram_tensor("bc_b", [32, TEXT], F32, kind="ExternalOutput").ap()

    NCH = TEXT // 512 + (1 if TEXT % 512 else 0)

    with contextlib.ExitStack() as es:
        S = es.enter_context
        ipw_bf = S(nc.sbuf_tensor("ipw_bf", [128, 8 * 2 * D], BF16))
        XW = TEXT + 6
        xzx = S(nc.sbuf_tensor("xzx", [128, 8 * XW], BF16))
        g_c = S(nc.sbuf_tensor("g_c", [128, 8], F32))
        b_c = S(nc.sbuf_tensor("b_c", [128, 8], F32))
        cw_f = S(nc.sbuf_tensor("cw_f", [128, 32], F32))
        cb_f = S(nc.sbuf_tensor("cb_f", [128, 8], F32))
        cw_b = S(nc.sbuf_tensor("cw_b", [128, 32], F32))
        cb_b = S(nc.sbuf_tensor("cb_b", [128, 8], F32))
        xpf_bf = S(nc.sbuf_tensor("xpf_bf", [128, 8 * 96], BF16))
        xpb_bf = S(nc.sbuf_tensor("xpb_bf", [128, 8 * 96], BF16))
        dtwf_bf = S(nc.sbuf_tensor("dtwf_bf", [64, D], BF16))
        dtwb_bf = S(nc.sbuf_tensor("dtwb_bf", [64, D], BF16))
        dtb_f = S(nc.sbuf_tensor("dtb_f", [128, 8], F32))
        dtb_b = S(nc.sbuf_tensor("dtb_b", [128, 8], F32))
        x_tm_t = S(nc.sbuf_tensor("x_tm_t", [128, 4 * D], F32))
        n_bf = S(nc.sbuf_tensor("n_bf", [128, 4 * D], BF16))
        hsn = S(nc.sbuf_tensor("hsn", [128, 8 * 512], BF16))
        u_ch = S(nc.sbuf_tensor("u_ch", [128, 8 * 512], F32))
        u_bf = S(nc.sbuf_tensor("u_bf", [128, 8 * 512], BF16))
        xdt = S(nc.sbuf_tensor("xdt", [64, 512], BF16))
        ident_f = S(nc.sbuf_tensor("ident_f", [128, 128], F32))
        ident = S(nc.sbuf_tensor("ident", [128, 128], BF16))
        ln_sq = S(nc.sbuf_tensor("ln_sq", [128, 4 * D], F32))
        dl1 = x_tm_t
        dl2 = ln_sq
        ln_m = S(nc.sbuf_tensor("ln_m", [128, 4], F32))
        ln_v = S(nc.sbuf_tensor("ln_v", [128, 4], F32))
        ln_r = S(nc.sbuf_tensor("ln_r", [128, 4], F32))
        pMM = S(nc.psum_tensor("pMM", [128, 512], F32))
        pT = S(nc.psum_tensor("pT", [128, 512], BF16))

        ph = Ph(nc, S)

        def ld1(e):
            ph.tag(e.dma_start(out=ipw_bf.ap().rearrange("p (a o) -> p a o", a=8),
                               in_=ipwT.rearrange("(a p) o -> p a o", p=128)))
            ph.tag(e.dma_start(out=xpf_bf.ap().rearrange("p (a o) -> p a o", a=8),
                               in_=xpwfT.rearrange("(a p) o -> p a o", p=128)))
            ph.tag(e.dma_start(out=xpb_bf.ap().rearrange("p (a o) -> p a o", a=8),
                               in_=xpwbT.rearrange("(a p) o -> p a o", p=128)))
            ph.tag(e.dma_start(out=dtwf_bf[:, :], in_=dtwfT))
            ph.tag(e.dma_start(out=dtwb_bf[:, :], in_=dtwbT))
        ph.on("gpsimd", ld1, ndma=5)

        hl_c = S(nc.sbuf_tensor("hl_c", [128, 1], F32))
        hr_c = S(nc.sbuf_tensor("hr_c", [128, 1], F32))

        def ld2(e):
            for dst, srcd in ((g_c, ln_g), (b_c, ln_b), (cw_f, cwf), (cb_f, cbf),
                              (cw_b, cwb), (cb_b, cbb), (dtb_f, dtbf), (dtb_b, dtbb),
                              (hl_c, hlv), (hr_c, hrv)):
                ph.tag(e.dma_start(out=dst[:, :], in_=srcd))
        ph.on("sync", ld2, ndma=10)

        def mkid2(e):
            return e.iota(ident_f[:, :], pattern=[[-1, 128]], base=0,
                          channel_multiplier=1, allow_small_or_imprecise_dtypes=True)
        ph.on("gpsimd", mkid2)
        ph.on("vector", lambda e: e.memset(xzx[:, :], 0.0))
        ph.bar()

        def mkid3(e):
            return e.tensor_scalar(ident_f[:, :], ident_f[:, :], 0.0, None, ALU.is_equal)
        ph.on("vector", mkid3)
        ph.bar()
        ph.on("vector", lambda e: e.tensor_copy(ident[:, :], ident_f[:, :]))
        ph.bar()

        chs = [(i * 512, 512) for i in range(TEXT // 512)]
        if TEXT % 512:
            chs.append((TEXT // 512 * 512, TEXT % 512))
        for c0, W in chs:
            TT = W // 128

            def ldc(e, c0=c0, W=W, TT=TT):
                ph.tag(e.dma_start(out=x_tm_t.ap().rearrange("p (a d) -> p a d", a=4)[:, 0:TT, :],
                                   in_=x_tme[c0:c0 + W, :].rearrange("(a p) d -> p a d", p=128)))
            ph.on("sync", ldc, ndma=1)
            ph.bar()
            ln_tm_batch(ph, lambda i: x_tm_t[:, i * D:(i + 1) * D],
                        lambda i: n_bf[:, i * D:(i + 1) * D],
                        ln_sq, ln_m, ln_v, ln_r, TT, D)

            for dt in range(8):
                def trp(e, dt=dt, TT=TT):
                    r = None
                    for tt in range(TT):
                        r = e.transpose(pT[:, tt * 128:(tt + 1) * 128],
                                        n_bf[:, tt * D + dt * 128:tt * D + dt * 128 + 128],
                                        ident[:, :])
                    return r
                ph.on("tensor", trp)
                ph.bar()

                def evt(e, dt=dt, W=W):
                    return e.tensor_scalar(hsn[:, dt * 512:dt * 512 + W], pT[:, 0:W],
                                           g_c[:, dt:dt + 1], b_c[:, dt:dt + 1], ALU.mult, ALU.add)
                ph.on("vector", evt)
                ph.bar()

            for ot in range(16):
                def mmip(e, ot=ot, W=W):
                    r = None
                    for i in range(8):
                        r = e.matmul(pMM[:, 0:W], ipw_bf[:, i * 2 * D + ot * 128:i * 2 * D + ot * 128 + 128],
                                     hsn[:, i * 512:i * 512 + W], start=(i == 0), stop=(i == 7))
                    return r
                ph.on("tensor", mmip)
                ph.bar()
                if ot < 8:
                    def evip(e, ot=ot, c0=c0, W=W):
                        return e.tensor_copy(xzx[:, ot * XW + c0 + 3:ot * XW + c0 + 3 + W], pMM[:, 0:W])
                    ph.on("vector", evip)
                    ph.bar()
                else:
                    def evz(e, ot=ot, W=W):
                        return e.activation(u_ch[:, (ot - 8) * 512:(ot - 8) * 512 + W], pMM[:, 0:W], AF.Silu)
                    ph.on("scalar", evz)
                    ph.bar()
            def stz(e, c0=c0, W=W):
                ph.tag(e.dma_start(out=o_z.rearrange("(a p) t -> p a t", p=128)[:, :, c0:c0 + W],
                                   in_=u_ch.ap().rearrange("p (a t) -> p a t", a=8)[:, :, 0:W]))
            ph.on("sync", stz, ndma=1)
            ph.bar()

        def zhalo(e):
            xv = xzx.ap().rearrange("p (a t) -> p a t", a=8)
            e.tensor_scalar(xv[:, :, 3:6], xv[:, :, 3:6], hl_c[:, 0:1], None, ALU.mult)
            return e.tensor_scalar(xv[:, :, TLOC + 6:TLOC + 9], xv[:, :, TLOC + 6:TLOC + 9],
                                   hr_c[:, 0:1], None, ALU.mult)
        ph.on("vector", zhalo)
        ph.bar()

        for br in range(2):
            cw, cb = (cw_f, cb_f) if br == 0 else (cw_b, cb_b)
            xp = xpf_bf if br == 0 else xpb_bf
            dtw = dtwf_bf if br == 0 else dtwb_bf
            dtb = dtb_f if br == 0 else dtb_b
            o_u = o_uf if br == 0 else o_ub
            o_d = o_df if br == 0 else o_db
            o_bc = o_bcf if br == 0 else o_bcb
            for c0, W in chs:
                TT = W // 128
                def cvi(e, c0=c0, W=W, br=br, cw=cw):
                    r = None
                    for dt in range(8):
                        for k in range(4):
                            off = (c0 + k) if br == 0 else (c0 + 6 - k)
                            src = xzx[:, dt * XW + off:dt * XW + off + W]
                            wcol = cw[:, dt * 4 + k:dt * 4 + k + 1]
                            if k == 0:
                                r = e.tensor_scalar(u_ch[:, dt * 512:dt * 512 + W], src, wcol, None, ALU.mult)
                            else:
                                r = e.scalar_tensor_tensor(u_ch[:, dt * 512:dt * 512 + W], src, wcol,
                                                           u_ch[:, dt * 512:dt * 512 + W], ALU.mult, ALU.add)
                    return r
                ph.on("vector", cvi)
                ph.bar()

                def cvb(e, cb=cb, W=W):
                    r = None
                    for dt in range(8):
                        r = e.tensor_scalar(u_ch[:, dt * 512:dt * 512 + W],
                                            u_ch[:, dt * 512:dt * 512 + W], cb[:, dt:dt + 1], None, ALU.add)
                    return r
                ph.on("vector", cvb)
                ph.bar()

                def usil(e, W=W):
                    r = None
                    for dt in range(8):
                        r = e.activation(u_ch[:, dt * 512:dt * 512 + W], u_ch[:, dt * 512:dt * 512 + W], AF.Silu)
                    return r
                ph.on("scalar", usil)
                ph.bar()
                ph.on("vector", lambda e, W=W: e.tensor_copy(u_bf[:, :], u_ch[:, :]))
                ph.bar()

                def stu(e, c0=c0, W=W, o_u=o_u):
                    ph.tag(e.dma_start(out=o_u.rearrange("(a p) t -> p a t", p=128)[:, :, c0:c0 + W],
                                       in_=u_ch.ap().rearrange("p (a t) -> p a t", a=8)[:, :, 0:W]))
                ph.on("sync", stu, ndma=1)

                def mmxd(e, xp=xp, W=W):
                    r = None
                    for i in range(8):
                        r = e.matmul(pMM[0:96, 0:W], xp[:, i * 96:(i + 1) * 96],
                                     u_bf[:, i * 512:i * 512 + W], start=(i == 0), stop=(i == 7))
                    return r
                ph.on("tensor", mmxd)
                ph.bar()

                def evxd(e, W=W):
                    return e.tensor_copy(xdt[:, 0:W], pMM[0:64, 0:W])
                ph.on("vector", evxd)

                ph.bar()

                def evbc(e, W=W):
                    return e.tensor_copy(ln_sq[0:32, 0:W], pMM[64:96, 0:W])
                ph.on("vector", evbc)
                ph.bar()

                def stbc2(e, c0=c0, W=W, o_bc=o_bc):
                    ph.tag(e.dma_start(out=o_bc[:, c0:c0 + W], in_=ln_sq[0:32, 0:W]))
                ph.on("sync", stbc2, ndma=1)

                for ot in range(8):
                    def mmdt(e, ot=ot, dtw=dtw, W=W):
                        return e.matmul(pMM[:, 0:W], dtw[:, ot * 128:(ot + 1) * 128],
                                        xdt[:, 0:W], start=True, stop=True)
                    ph.on("tensor", mmdt)
                    ph.bar()

                    def evdt(e, ot=ot, dtb=dtb, W=W):
                        return e.tensor_scalar(dl1[:, ot * 512:ot * 512 + W], pMM[:, 0:W],
                                               dtb[:, ot:ot + 1], None, ALU.add)
                    ph.on("vector", evdt)
                    ph.bar()
                def sp1(e, W=W):
                    r = None
                    for dt in range(8):
                        r = e.activation(dl2[:, dt * 512:dt * 512 + W], dl1[:, dt * 512:dt * 512 + W], AF.Abs)
                    return r
                ph.on("scalar", sp1)
                ph.bar()

                def sp2(e, W=W):
                    r = None
                    for dt in range(8):
                        r = e.activation(dl2[:, dt * 512:dt * 512 + W], dl2[:, dt * 512:dt * 512 + W], AF.Exp, scale=-1.0)
                    return r
                ph.on("scalar", sp2)
                ph.bar()
                ph.on("vector", lambda e, W=W: e.tensor_scalar(dl2[:, :], dl2[:, :], 1.0, None, ALU.add))
                ph.bar()

                def sp3(e, W=W):
                    r = None
                    for dt in range(8):
                        r = e.activation(dl2[:, dt * 512:dt * 512 + W], dl2[:, dt * 512:dt * 512 + W], AF.Ln)
                    return r
                ph.on("scalar", sp3)
                ph.bar()

                def sp4(e, W=W):
                    r = None
                    for dt in range(8):
                        r = e.activation(dl1[:, dt * 512:dt * 512 + W], dl1[:, dt * 512:dt * 512 + W], AF.Relu)
                    return r
                ph.on("scalar", sp4)
                ph.bar()
                ph.on("vector", lambda e, W=W: e.tensor_add(dl1[:, :], dl1[:, :], dl2[:, :]))
                ph.bar()

                def std(e, c0=c0, W=W, o_d=o_d):
                    ph.tag(e.dma_start(out=o_d.rearrange("(a p) t -> p a t", p=128)[:, :, c0:c0 + W],
                                       in_=dl1.ap().rearrange("p (a t) -> p a t", a=8)[:, :, 0:W]))
                ph.on("sync", std, ndma=1)
                ph.bar()

        ph.emit(S)
    return nc


def l03_inmaps(x2_fm_full, inp):
    ins = []
    ipwT = fmT(inp["L1_attn_in_proj_w"])
    args = dict(
        ipwT=ipwT,
        ln_g=col128(inp["L1_attn_ln_g"]), ln_b=col128(inp["L1_attn_ln_b"]),
        cwf=cont(inp["L1_attn_conv_w"]).reshape(8, 128, 4).transpose(1, 0, 2).reshape(128, 32).copy(),
        cbf=col128(inp["L1_attn_conv_b"]),
        cwb=cont(inp["L1_attn_conv_w_b"]).reshape(8, 128, 4).transpose(1, 0, 2).reshape(128, 32).copy(),
        cbb=col128(inp["L1_attn_conv_b_b"]),
        xpwfT=fmT(inp["L1_attn_x_proj_w"]), xpwbT=fmT(inp["L1_attn_x_proj_w_b"]),
        dtwfT=fmT(inp["L1_attn_dt_proj_w"]), dtwbT=fmT(inp["L1_attn_dt_proj_w_b"]),
        dtbf=col128(inp["L1_attn_dt_proj_b"]), dtbb=col128(inp["L1_attn_dt_proj_b_b"]),
    )
    for c in range(NCORES):
        b, q = c // 4, c % 4
        t0 = q * TLOC
        xb = x2_fm_full[b]
        ext = np.zeros((TEXT, D), np.float32)
        lo = t0 - 3
        hi = t0 + TLOC + 3
        slo = max(lo, 0)
        shi = min(hi, T)
        ext[slo - lo:slo - lo + (shi - slo)] = xb[:, slo:shi].T
        m = dict(args)
        m["x_tme"] = ext
        m["hlv"] = np.full((128, 1), 0.0 if q == 0 else 1.0, np.float32)
        m["hrv"] = np.full((128, 1), 0.0 if q == 3 else 1.0, np.float32)
        ins.append(m)
    return ins


def build_scan(NT, TB=T, NBT=NB, TC=2048):
    nc = bass.Bass("TRN2", target_bir_lowering=False, num_devices=NCORES)
    ROWS = 128 * NT
    TTOT = NBT * TB
    d_sl = nc.dram_tensor("delta_sl", [ROWS, TTOT], F32, kind="ExternalInput").ap()
    u_sl = nc.dram_tensor("u_sl", [ROWS, TTOT], F32, kind="ExternalInput").ap()
    bc = nc.dram_tensor("bc", [32, TTOT], F32, kind="ExternalInput").ap()
    alog = nc.dram_tensor("alog", [128, NT * 16], F32, kind="ExternalInput").ap()
    y_sl = nc.dram_tensor("y_sl", [ROWS, TTOT], F32, kind="ExternalOutput").ap()

    NPC = TC // 512

    with contextlib.ExitStack() as es:
        S = es.enter_context
        acol = S(nc.sbuf_tensor("acol", [128, NT * 16], F32))
        sel = S(nc.sbuf_tensor("sel", [32, 32 * 128], F32))
        dch = S(nc.sbuf_tensor("dch", [128, TC], F32))
        uch = S(nc.sbuf_tensor("uch", [128, TC], F32))
        duch = S(nc.sbuf_tensor("duch", [128, TC], F32))
        bcch = S(nc.sbuf_tensor("bcch", [32, TC], F32))
        dA = S(nc.sbuf_tensor("dA", [128, TC], F32))
        dbu = S(nc.sbuf_tensor("dbu", [128, TC], F32))
        hch = S(nc.sbuf_tensor("hch", [128, TC], F32))
        zch = S(nc.sbuf_tensor("zch", [128, TC], F32))
        yacc = S(nc.sbuf_tensor("yacc", [128, TC], F32))
        carry = S(nc.sbuf_tensor("carry", [128, 16 * NT], F32))
        pB = S(nc.psum_tensor("pB", [128, TC], F32))
        pC = S(nc.psum_tensor("pC", [128, TC], F32))

        ph = Ph(nc, S)

        def ld0(e):
            ph.tag(e.dma_start(out=acol[:, :], in_=alog))
        ph.on("sync", ld0, ndma=1)
        ph.on("gpsimd", lambda e: e.iota(sel[:, :], pattern=[[-1, 32], [0, 128]], base=0,
                                         channel_multiplier=1, allow_small_or_imprecise_dtypes=True))
        ph.bar()
        ph.on("vector", lambda e: e.tensor_scalar(sel[:, :], sel[:, :], 0.0, None, ALU.is_equal))
        ph.bar()
        ph.on("scalar", lambda e: e.activation(acol[:, :], acol[:, :], AF.Exp))
        ph.bar()
        ph.on("vector", lambda e: e.tensor_scalar(acol[:, :], acol[:, :], -1.0, None, ALU.mult))
        ph.bar()

        for dt in range(NT):
            r0 = dt * 128
            for b in range(NBT):
                ph.on("vector", lambda e, dt=dt: e.memset(carry[:, dt * 16:(dt + 1) * 16], 0.0))
                ph.bar()
                for ci in range(TB // TC):
                    c0 = b * TB + ci * TC

                    def ldc(e, r0=r0, c0=c0):
                        ph.tag(e.dma_start(out=dch[:, :], in_=d_sl[r0:r0 + 128, c0:c0 + TC]))
                        ph.tag(e.dma_start(out=uch[:, :], in_=u_sl[r0:r0 + 128, c0:c0 + TC]))
                        ph.tag(e.dma_start(out=bcch[:, :], in_=bc[:, c0:c0 + TC]))
                    ph.on("sync", ldc, ndma=3)
                    ph.bar()
                    ph.on("vector", lambda e: e.tensor_mul(duch[:, :], dch[:, :], uch[:, :]))
                    ph.bar()

                    for n in range(16):
                        def mkda(e, dt=dt, n=n):
                            return e.activation(dA[:, :], dch[:, :], AF.Exp,
                                                scale=acol[:, dt * 16 + n:dt * 16 + n + 1])
                        ph.on("scalar", mkda)

                        def bbc(e, n=n):
                            r = None
                            for pc in range(NPC):
                                r = e.matmul(pB[:, pc * 512:(pc + 1) * 512],
                                             sel[:, n * 128:(n + 1) * 128],
                                             bcch[:, pc * 512:(pc + 1) * 512],
                                             start=True, stop=True)
                                r = e.matmul(pC[:, pc * 512:(pc + 1) * 512],
                                             sel[:, (16 + n) * 128:(17 + n) * 128],
                                             bcch[:, pc * 512:(pc + 1) * 512],
                                             start=True, stop=True)
                            return r
                        ph.on("tensor", bbc)
                        ph.bar()

                        def mkdbu(e):
                            r = None
                            for pc in range(NPC):
                                sl = slice(pc * 512, (pc + 1) * 512)
                                r = e.tensor_mul(dbu[:, sl], duch[:, sl], pB[:, sl])
                            return r
                        ph.on("vector", mkdbu)
                        ph.bar()

                        def scan(e, dt=dt, n=n):
                            return e.tensor_tensor_scan(hch[:, :], dA[:, :], dbu[:, :],
                                                        carry[:, dt * 16 + n:dt * 16 + n + 1],
                                                        ALU.mult, ALU.add)
                        ph.on("vector", scan)
                        ph.bar()

                        def upc(e, dt=dt, n=n):
                            return e.tensor_copy(carry[:, dt * 16 + n:dt * 16 + n + 1], hch[:, TC - 1:TC])
                        ph.on("vector", upc)

                        if n == 0:
                            def mkz(e):
                                r = None
                                for pc in range(NPC):
                                    sl = slice(pc * 512, (pc + 1) * 512)
                                    r = e.tensor_mul(yacc[:, sl], hch[:, sl], pC[:, sl])
                                return r
                            ph.on("vector", mkz)
                            ph.bar()
                        else:
                            def mkz(e):
                                r = None
                                for pc in range(NPC):
                                    sl = slice(pc * 512, (pc + 1) * 512)
                                    r = e.tensor_mul(zch[:, sl], hch[:, sl], pC[:, sl])
                                return r
                            ph.on("vector", mkz)
                            ph.bar()
                            ph.on("vector", lambda e: e.tensor_add(yacc[:, :], yacc[:, :], zch[:, :]))
                            ph.bar()

                    def sty(e, r0=r0, c0=c0):
                        ph.tag(e.dma_start(out=y_sl[r0:r0 + 128, c0:c0 + TC], in_=yacc[:, :]))
                    ph.on("sync", sty, ndma=1)
                    ph.bar()

        ph.emit(S)
    return nc


def scan_inmaps(delta_g, u_g, b_g, c_g, a_log):
    Dn = delta_g.shape[0]
    NT = Dn // (128 * NCORES)
    bcm = np.concatenate([b_g, c_g], axis=0).astype(np.float32)
    bcm = np.ascontiguousarray(bcm)
    ins = []
    for c in range(NCORES):
        r0 = c * 128 * NT
        sl = slice(r0, r0 + 128 * NT)
        al = a_log[sl]
        alc = np.ascontiguousarray(al.reshape(NT, 128, 16).transpose(1, 0, 2).reshape(128, NT * 16))
        ins.append({
            "delta_sl": np.ascontiguousarray(delta_g[sl]),
            "u_sl": np.ascontiguousarray(u_g[sl]),
            "bc": bcm,
            "alog": alc.astype(np.float32),
        })
    return ins


def build_l06():
    nc = bass.Bass("TRN2", target_bir_lowering=False, num_devices=NCORES)
    CH = 512
    x_tm = nc.dram_tensor("x_tm", [TLOC, D], F32, kind="ExternalInput").ap()
    yf = nc.dram_tensor("yf", [D, TLOC], F32, kind="ExternalInput").ap()
    yb = nc.dram_tensor("yb", [D, TLOC], F32, kind="ExternalInput").ap()
    uf = nc.dram_tensor("uf", [D, TLOC], F32, kind="ExternalInput").ap()
    ub = nc.dram_tensor("ub", [D, TLOC], F32, kind="ExternalInput").ap()
    za = nc.dram_tensor("za", [D, TLOC], F32, kind="ExternalInput").ap()
    dpf = nc.dram_tensor("dpf", [128, 8], F32, kind="ExternalInput").ap()
    dpb = nc.dram_tensor("dpb", [128, 8], F32, kind="ExternalInput").ap()
    outwT = nc.dram_tensor("outwT", [D, D], F32, kind="ExternalInput").ap()
    fcwT = nc.dram_tensor("fcwT", [D, D], F32, kind="ExternalInput").ap()
    fcb = nc.dram_tensor("fcb", [1, D], F32, kind="ExternalInput").ap()
    wrt = nc.dram_tensor("wrt", [D, 32], F32, kind="ExternalInput").ap()
    out = nc.dram_tensor("y_tm", [TLOC, D], F32, kind="ExternalOutput").ap()
    o_lg = nc.dram_tensor("lg_sl", [32, TLOC], F32, kind="ExternalOutput").ap()

    with contextlib.ExitStack() as es:
        S = es.enter_context
        ow_bf = S(nc.sbuf_tensor("ow_bf", [128, 8 * D], BF16))
        fw_bf = S(nc.sbuf_tensor("fw_bf", [128, 8 * D], BF16))
        fb_row = S(nc.sbuf_tensor("fb_row", [1, D], BF16))
        onesr = S(nc.sbuf_tensor("onesr", [1, 512], BF16))
        dp_f = S(nc.sbuf_tensor("dp_f", [128, 8], F32))
        dp_b = S(nc.sbuf_tensor("dp_b", [128, 8], F32))
        t_yf = S(nc.sbuf_tensor("t_yf", [128, 8 * CH], F32))
        t_yb = S(nc.sbuf_tensor("t_yb", [128, 8 * CH], F32))
        t_uf = S(nc.sbuf_tensor("t_uf", [128, 8 * CH], F32))
        t_ub = S(nc.sbuf_tensor("t_ub", [128, 8 * CH], F32))
        t_za = S(nc.sbuf_tensor("t_za", [128, 8 * CH], F32))
        g_bf = S(nc.sbuf_tensor("g_bf", [128, 8 * CH], BF16))
        mid_bf = S(nc.sbuf_tensor("mid_bf", [128, 8 * CH], BF16))
        o_tm = S(nc.sbuf_tensor("o_tm", [128, 4 * D], F32))
        res_tm = S(nc.sbuf_tensor("res_tm", [128, 4 * D], F32))
        ln_sq = S(nc.sbuf_tensor("ln_sq", [128, 4 * D], F32))
        ln_m = S(nc.sbuf_tensor("ln_m", [128, 4], F32))
        ln_v = S(nc.sbuf_tensor("ln_v", [128, 4], F32))
        ln_r = S(nc.sbuf_tensor("ln_r", [128, 4], F32))
        wr_bf = S(nc.sbuf_tensor("wr_bf", [128, 8 * 32], BF16))
        xn_bf = S(nc.sbuf_tensor("xn_bf", [128, 4 * D], BF16))
        xf_bf = S(nc.sbuf_tensor("xf_bf", [128, 8 * CH], BF16))
        lg_sb = S(nc.sbuf_tensor("lg_sb", [32, CH], F32))
        ident_f = S(nc.sbuf_tensor("ident_f6", [128, 128], F32))
        ident = S(nc.sbuf_tensor("ident6", [128, 128], BF16))
        pMM = S(nc.psum_tensor("pMM", [128, 512], F32))
        pO = S(nc.psum_tensor("pO", [128, 1024], F32))
        pT = S(nc.psum_tensor("pT", [128, 512], BF16))

        ph = Ph(nc, S)

        def ldw(e):
            ph.tag(e.dma_start(out=ow_bf.ap().rearrange("p (a o) -> p a o", a=8),
                               in_=outwT.rearrange("(a p) o -> p a o", p=128)))
            ph.tag(e.dma_start(out=fw_bf.ap().rearrange("p (a o) -> p a o", a=8),
                               in_=fcwT.rearrange("(a p) o -> p a o", p=128)))
            ph.tag(e.dma_start(out=fb_row[:, :], in_=fcb))
            ph.tag(e.dma_start(out=wr_bf.ap().rearrange("p (a o) -> p a o", a=8),
                               in_=wrt.rearrange("(a p) o -> p a o", p=128)))
        ph.on("gpsimd", ldw, ndma=4)
        ph.on("gpsimd", lambda e: e.iota(ident_f[:, :], pattern=[[-1, 128]], base=0,
                                         channel_multiplier=1, allow_small_or_imprecise_dtypes=True))
        ph.bar()
        ph.on("vector", lambda e: e.tensor_scalar(ident_f[:, :], ident_f[:, :], 0.0, None, ALU.is_equal))
        ph.bar()
        ph.on("vector", lambda e: e.tensor_copy(ident[:, :], ident_f[:, :]))
        ph.bar()

        def ldc0(e):
            ph.tag(e.dma_start(out=dp_f[:, :], in_=dpf))
            ph.tag(e.dma_start(out=dp_b[:, :], in_=dpb))
        ph.on("sync", ldc0, ndma=2)
        ph.on("vector", lambda e: e.memset(onesr[:, :], 1.0))
        ph.bar()

        for ci in range(TLOC // CH):
            c0 = ci * CH

            def ldc(e, c0=c0):
                for dst, srcd in ((t_yf, yf), (t_yb, yb), (t_uf, uf), (t_ub, ub), (t_za, za)):
                    ph.tag(e.dma_start(out=dst.ap().rearrange("p (a t) -> p a t", a=8),
                                       in_=srcd.rearrange("(a p) t -> p a t", p=128)[:, :, c0:c0 + CH]))
                ph.tag(e.dma_start(out=res_tm.ap().rearrange("p (a d) -> p a d", a=4),
                                   in_=x_tm[c0:c0 + CH, :].rearrange("(a p) d -> p a d", p=128)))
            ph.on("sync", ldc, ndma=6)
            ph.bar()

            ph.on("vector", lambda e: e.tensor_add(t_yf[:, :], t_yf[:, :], t_yb[:, :]))
            ph.bar()

            def dterm1(e):
                r = None
                for dt in range(8):
                    sl = slice(dt * CH, (dt + 1) * CH)
                    r = e.scalar_tensor_tensor(t_yf[:, sl], t_uf[:, sl], dp_f[:, dt:dt + 1],
                                               t_yf[:, sl], ALU.mult, ALU.add)
                return r
            ph.on("vector", dterm1)
            ph.bar()

            def dterm2(e):
                r = None
                for dt in range(8):
                    sl = slice(dt * CH, (dt + 1) * CH)
                    r = e.scalar_tensor_tensor(t_yf[:, sl], t_ub[:, sl], dp_b[:, dt:dt + 1],
                                               t_yf[:, sl], ALU.mult, ALU.add)
                return r
            ph.on("vector", dterm2)
            ph.bar()
            ph.on("vector", lambda e: e.tensor_mul(g_bf[:, :], t_yf[:, :], t_za[:, :]))
            ph.bar()

            for ot in range(8):
                def mm1(e, ot=ot):
                    r = None
                    for i in range(8):
                        r = e.matmul(pMM[:, :], ow_bf[:, i * D + ot * 128:i * D + ot * 128 + 128],
                                     g_bf[:, i * CH:(i + 1) * CH], start=(i == 0), stop=(i == 7))
                    return r
                ph.on("tensor", mm1)
                ph.bar()
                ph.on("vector", lambda e, ot=ot: e.tensor_copy(mid_bf[:, ot * CH:(ot + 1) * CH], pMM[:, :]))
                ph.bar()

            for tt in range(4):
                def mm2(e, tt=tt):
                    r = None
                    for half in range(2):
                        psl = pO[:, half * 512:(half + 1) * 512]
                        for i in range(8):
                            e.matmul(psl, mid_bf[:, i * CH + tt * 128:i * CH + tt * 128 + 128],
                                     fw_bf[:, i * D + half * 512:i * D + half * 512 + 512],
                                     start=(i == 0), stop=False)
                        r = e.matmul(psl, onesr[:, tt * 128:(tt + 1) * 128],
                                     fb_row[:, half * 512:(half + 1) * 512], start=False, stop=True)
                    return r
                ph.on("tensor", mm2)
                ph.bar()

                def evo(e, tt=tt):
                    return e.scalar_tensor_tensor(o_tm[:, tt * D:(tt + 1) * D],
                                                  res_tm[:, tt * D:(tt + 1) * D], 2.0,
                                                  pO[:, :], ALU.mult, ALU.add)
                ph.on("vector", evo)
                ph.bar()

            ln_tm_batch(ph, lambda i: o_tm[:, i * D:(i + 1) * D], lambda i: o_tm[:, i * D:(i + 1) * D],
                        ln_sq, ln_m, ln_v, ln_r, 4, D)

            def store(e, c0=c0):
                ph.tag(e.dma_start(out=out[c0:c0 + CH, :].rearrange("(a p) d -> p a d", p=128),
                                   in_=o_tm.ap().rearrange("p (a d) -> p a d", a=4)))
            ph.on("sync", store, ndma=1)
            ph.on("vector", lambda e: e.tensor_copy(xn_bf[:, :], o_tm[:, :]))
            ph.bar()

            for dt in range(8):
                def trp(e, dt=dt):
                    r = None
                    for tt in range(4):
                        r = e.transpose(pT[:, tt * 128:(tt + 1) * 128],
                                        xn_bf[:, tt * D + dt * 128:tt * D + dt * 128 + 128],
                                        ident[:, :])
                    return r
                ph.on("tensor", trp)
                ph.bar()
                ph.on("vector", lambda e, dt=dt: e.tensor_copy(xf_bf[:, dt * CH:(dt + 1) * CH], pT[:, :]))
                ph.bar()

            def mmlg(e):
                r = None
                for i in range(8):
                    r = e.matmul(pMM[0:32, :], wr_bf[:, i * 32:(i + 1) * 32],
                                 xf_bf[:, i * CH:(i + 1) * CH], start=(i == 0), stop=(i == 7))
                return r
            ph.on("tensor", mmlg)
            ph.bar()
            ph.on("vector", lambda e: e.tensor_copy(lg_sb[:, :], pMM[0:32, :]))
            ph.bar()

            def stlg(e, c0=c0):
                ph.tag(e.dma_start(out=o_lg[:, c0:c0 + CH], in_=lg_sb[:, :]))
            ph.on("sync", stlg, ndma=1)
            ph.bar()

        ph.emit(S)
    return nc


def l06_inmaps(x2_fm_full, yf_g, yb_g, uf_g, ub_g, za_g, inp):
    args = dict(
        dpf=col128(inp["L1_attn_Dp"]), dpb=col128(inp["L1_attn_Dp_b"]),
        outwT=fmT(inp["L1_attn_out_proj_w"]), fcwT=fmT(inp["L1_attn_fc_w"]),
        fcb=cont(inp["L1_attn_fc_b"]).reshape(1, D),
        wrt=cont(inp["L1_ffn_w_router"]),
    )
    ins = []
    for c in range(NCORES):
        b, q = c // 4, c % 4
        cs = b * T + q * TLOC
        m = dict(args)
        m["x_tm"] = cont(x2_fm_full[b][:, q * TLOC:(q + 1) * TLOC].T)
        for nmk, g in (("yf", yf_g), ("yb", yb_g), ("uf", uf_g), ("ub", ub_g), ("za", za_g)):
            m[nmk] = np.ascontiguousarray(g[:, cs:cs + TLOC])
        ins.append(m)
    return ins


L1F_C = 1280
L1F_CP = 1536
L1F_BIG = float(1 << 20)


def build_l07():
    nc = bass.Bass("TRN2", target_bir_lowering=False, num_devices=NCORES)
    TTK = NB * T
    NCH = TTK // 512
    lg = nc.dram_tensor("lg", [32, TTK], F32, kind="ExternalInput").ap()
    x_tm = nc.dram_tensor("x3_tm", [TTK, D], F32, kind="ExternalInput").ap()
    wg_sl = nc.dram_tensor("wg_sl", [4 * D, D], F32, kind="ExternalInput").ap()
    wu_sl = nc.dram_tensor("wu_sl", [4 * D, D], F32, kind="ExternalInput").ap()
    wd_sl = nc.dram_tensor("wd_sl", [4 * D, D], F32, kind="ExternalInput").ap()
    eoff = nc.dram_tensor("eoff", [1, 1], F32, kind="ExternalInput").ap()
    ybuf = nc.dram_tensor("ybuf_sl", [4 * L1F_CP, D], BF16, kind="ExternalOutput").ap()
    o_slotc = nc.dram_tensor("slotc", [128, 256], U32, kind="ExternalOutput").ap()
    o_gatec = nc.dram_tensor("gatec", [128, 256], F32, kind="ExternalOutput").ap()
    iv = nc.dram_tensor("iv", [4 * L1F_CP + 128, 4], U32).ap()

    with contextlib.ExitStack() as es:
        S = es.enter_context
        lgch = S(nc.sbuf_tensor("lgch", [32, 512], F32))
        lgT = S(nc.sbuf_tensor("lgT", [128, 4 * 32], F32))
        mx8 = S(nc.sbuf_tensor("mx8", [128, 8], F32))
        zeq = S(nc.sbuf_tensor("zeq", [128, 32], F32))
        zin = S(nc.sbuf_tensor("zin", [128, 32], F32))
        ones32r = S(nc.sbuf_tensor("ones32r", [128, 32], F32))
        ecol = S(nc.sbuf_tensor("ecol", [128, 4], F32))
        gcol = S(nc.sbuf_tensor("gcol", [128, 4], F32))
        den = S(nc.sbuf_tensor("den", [128, 4], F32))
        etmp = S(nc.sbuf_tensor("etmp", [128, 4 * 32], F32))
        erow = S(nc.sbuf_tensor("erow", [1, 512], F32))
        oh_tm = S(nc.sbuf_tensor("oh_tm", [128, 4 * 32], F32))
        iota32r = S(nc.sbuf_tensor("iota32r", [128, 32], F32))
        ones32 = S(nc.sbuf_tensor("ones32", [32, 512], F32))
        onescol = S(nc.sbuf_tensor("onescol", [32, 1], F32))
        oh = S(nc.sbuf_tensor("oh", [32, 512], F32))
        rk = S(nc.sbuf_tensor("rk", [32, 512], F32))
        car32 = S(nc.sbuf_tensor("car32", [32, 1], F32))
        posr = S(nc.sbuf_tensor("posr", [1, 512], F32))
        slotr = S(nc.sbuf_tensor("slotr", [1, 512], F32))
        slotr2 = S(nc.sbuf_tensor("slotr2", [1, 512], F32))
        slocr = S(nc.sbuf_tensor("slocr", [1, 512], F32))
        slocr2 = S(nc.sbuf_tensor("slocr2", [1, 512], F32))
        slocc2 = S(nc.sbuf_tensor("slocc2", [128, 4], F32))
        bigr = S(nc.sbuf_tensor("bigr", [1, 512], F32))
        mskr = S(nc.sbuf_tensor("mskr", [1, 512], U8))
        slotg_c = S(nc.sbuf_tensor("slotg_c", [128, 4], U32))
        slcols = S(nc.sbuf_tensor("slcols", [128, 256], U32))
        ids = S(nc.sbuf_tensor("ids", [128, 4 * 256], U32))
        ivcols = S(nc.sbuf_tensor("ivcols", [128, 16], U32))
        eoffc = S(nc.sbuf_tensor("eoffc", [1, 1], F32))
        zu = S(nc.sbuf_tensor("zu", [49, 4 * 128], U32))
        trash4 = S(nc.sbuf_tensor("trash4", [128, 4], F32))
        slocc = S(nc.sbuf_tensor("slocc", [128, 4], F32))
        mskc = S(nc.sbuf_tensor("mskc", [128, 4], U8))
        xg_tm = S(nc.sbuf_tensor("xg_tm", [128, 4 * D], F32))
        xg_bf = S(nc.sbuf_tensor("xg_bf", [128, 4 * D], BF16))
        x_fm = S(nc.sbuf_tensor("x_fm", [128, 8 * 512], BF16))
        wg_bf = S(nc.sbuf_tensor("wg_bf", [128, 8 * D], BF16))
        wu_bf = S(nc.sbuf_tensor("wu_bf", [128, 8 * D], BF16))
        wd_bf = S(nc.sbuf_tensor("wd_bf", [128, 8 * D], BF16))
        g_bf = S(nc.sbuf_tensor("g_bf", [128, 8 * 512], BF16))
        u_bf2 = S(nc.sbuf_tensor("u_bf2", [128, 8 * 512], BF16))
        y_bf = S(nc.sbuf_tensor("y_bf", [128, 8 * 512], BF16))
        y_tmo = S(nc.sbuf_tensor("y_tmo", [128, 4 * D], BF16))
        ident_f = S(nc.sbuf_tensor("ident_f7", [128, 128], F32))
        identb = S(nc.sbuf_tensor("ident7b", [128, 128], BF16))
        pMM = S(nc.psum_tensor("pMM", [128, 512], F32))
        pT = S(nc.psum_tensor("pT7", [128, 512], F32))
        pTb = S(nc.psum_tensor("pT7b", [128, 512], BF16))

        ph = Ph(nc, S)

        def setup_io(e):
            e.iota(iota32r[:, :], pattern=[[1, 32]], base=0, channel_multiplier=0,
                   allow_small_or_imprecise_dtypes=True)
            e.iota(ids[:, :], pattern=[[128, 256], [0, 4]], base=0, channel_multiplier=1)
            e.iota(trash4[:, :], pattern=[[0, 4]], base=4 * L1F_CP, channel_multiplier=1,
                   allow_small_or_imprecise_dtypes=True)
            return e.iota(ident_f[:, :], pattern=[[-1, 128]], base=0, channel_multiplier=1,
                          allow_small_or_imprecise_dtypes=True)
        ph.on("gpsimd", setup_io)

        def setup_ms(e):
            e.memset(ones32[:, :], 1.0)
            e.memset(ones32r[:, :], 1.0)
            e.memset(onescol[:, :], 1.0)
            e.memset(bigr[:, :], L1F_BIG)
            e.memset(zu[:, :], 0)
            return e.memset(car32[:, :], 0.0)
        ph.on("vector", setup_ms)

        def ld_eoff(e):
            ph.tag(e.dma_start(out=eoffc[:, :], in_=eoff))
        ph.on("sync", ld_eoff, ndma=1)
        ph.bar()
        ph.on("vector", lambda e: e.tensor_scalar(ident_f[:, :], ident_f[:, :], 0.0, None, ALU.is_equal))
        ph.bar()
        ph.on("vector", lambda e: e.tensor_copy(identb[:, :], ident_f[:, :]))
        ph.bar()

        def iv_init(e):
            ph.tag(e.dma_start(out=iv.rearrange("(a p) c -> a (p c)", p=128), in_=zu[:, :]))
        ph.on("sync", iv_init, ndma=1)
        ph.bar()

        for ci in range(NCH):
            c0 = ci * 512

            def ldlg(e, c0=c0):
                ph.tag(e.dma_start(out=lgch[:, :], in_=lg[:, c0:c0 + 512]))
            ph.on("sync", ldlg, ndma=1)
            ph.bar()

            def trlg(e):
                r = None
                for tt in range(4):
                    r = e.transpose(pT[:, tt * 32:(tt + 1) * 32],
                                    lgch[:, tt * 128:(tt + 1) * 128], ident_f[0:32, 0:32])
                return r
            ph.on("tensor", trlg)
            ph.bar()
            ph.on("vector", lambda e: e.tensor_copy(lgT[:, :], pT[:, 0:128]))
            ph.bar()
            for tt in range(4):
                sl32 = slice(tt * 32, (tt + 1) * 32)
                ph.on("vector", lambda e, sl32=sl32: e.max(mx8[:, :], lgT[:, sl32]))
                ph.bar()
                ph.on("vector", lambda e, sl32=sl32: e.tensor_scalar(zeq[:, :], lgT[:, sl32],
                                                                     mx8[:, 0:1], None, ALU.is_equal))
                ph.bar()
                ph.on("vector", lambda e: e.tensor_tensor_scan(zin[:, :], ones32r[:, :], zeq[:, :],
                                                               0.0, ALU.mult, ALU.add))
                ph.bar()
                ph.on("vector", lambda e: e.tensor_scalar(zin[:, :], zin[:, :], 1.0, None, ALU.is_equal))
                ph.bar()
                ph.on("vector", lambda e, tt=tt: e.tensor_mul(oh_tm[:, tt * 32:(tt + 1) * 32], zeq[:, :], zin[:, :]))
                ph.bar()
                ph.on("vector", lambda e, tt=tt: e.tensor_mul(zeq[:, :], oh_tm[:, tt * 32:(tt + 1) * 32], iota32r[:, :]))
                ph.bar()
                ph.on("vector", lambda e, tt=tt: e.reduce_sum(ecol[:, tt:tt + 1], zeq[:, :], axis=AX.X))
                ph.on("vector", lambda e, tt=tt: e.tensor_copy(den[:, tt:tt + 1], mx8[:, 0:1]))
                ph.bar()

                def ex(e, tt=tt, sl32=sl32):
                    return e.tensor_scalar(etmp[:, sl32], lgT[:, sl32],
                                           den[:, tt:tt + 1], None, ALU.subtract)
                ph.on("vector", ex)
                ph.bar()
                ph.on("scalar", lambda e, sl32=sl32: e.activation(etmp[:, sl32], etmp[:, sl32], AF.Exp))
                ph.bar()
                ph.on("vector", lambda e, tt=tt, sl32=sl32: e.reduce_sum(den[:, tt:tt + 1],
                                                                         etmp[:, sl32], axis=AX.X))
                ph.bar()
            ph.on("vector", lambda e: e.reciprocal(gcol[:, :], den[:, :]))
            ph.bar()

            def stg(e, ci=ci):
                ph.tag(e.dma_start(out=o_gatec[:, ci * 4:(ci + 1) * 4], in_=gcol[:, :]))
            ph.on("sync", stg, ndma=1)

            def troh(e):
                r = None
                for tt in range(4):
                    r = e.transpose(pT[0:32, tt * 128:(tt + 1) * 128],
                                    oh_tm[:, tt * 32:(tt + 1) * 32], ident_f[:, :])
                for tt in range(4):
                    r = e.transpose(pMM[0:1, tt * 128:(tt + 1) * 128],
                                    ecol[:, tt:tt + 1], ident_f[:, :])
                return r
            ph.on("tensor", troh)
            ph.bar()

            def evoh(e):
                e.tensor_copy(oh[:, :], pT[0:32, 0:512])
                return e.tensor_copy(erow[:, :], pMM[0:1, :])
            ph.on("vector", evoh)
            ph.bar()
            ph.on("vector", lambda e: e.tensor_tensor_scan(rk[:, :], ones32[:, :], oh[:, :],
                                                           car32[:, 0:1], ALU.mult, ALU.add))
            ph.bar()
            ph.on("vector", lambda e: e.tensor_copy(car32[:, 0:1], rk[:, 511:512]))
            ph.on("vector", lambda e: e.tensor_mul(oh[:, :], oh[:, :], rk[:, :]))
            ph.bar()
            ph.on("tensor", lambda e: e.matmul(pMM[0:1, :], onescol[:, :], oh[:, :], start=True, stop=True))
            ph.bar()
            ph.on("vector", lambda e: e.tensor_scalar(posr[:, :], pMM[0:1, :], 1.0, None, ALU.subtract))
            ph.bar()
            ph.on("vector", lambda e: e.scalar_tensor_tensor(slotr[:, :], erow[:, :], float(L1F_CP),
                                                             posr[:, :], ALU.mult, ALU.add))
            ph.on("vector", lambda e: e.tensor_scalar(mskr[:, :], posr[:, :], float(L1F_C), None, ALU.is_lt))
            ph.bar()
            ph.on("vector", lambda e: e.select(slotr2[:, :], mskr[:, :], slotr[:, :], bigr[:, :]))
            ph.bar()
            ph.on("vector", lambda e: e.tensor_scalar(slocr[:, :], slotr2[:, :], eoffc[0:1, 0:1], None, ALU.subtract))
            ph.bar()
            ph.on("vector", lambda e: e.tensor_scalar(mskr[:, :], slocr[:, :], 0.0, None, ALU.is_ge))
            ph.bar()
            ph.on("vector", lambda e: e.select(slocr2[:, :], mskr[:, :], slocr[:, :], bigr[:, :]))
            ph.bar()

            def trslot(e):
                r = None
                for tt in range(4):
                    r = e.transpose(pT[:, tt:tt + 1], slotr2[0:1, tt * 128:(tt + 1) * 128], ident_f[0:1, 0:1])
                for tt in range(4):
                    r = e.transpose(pT[:, 8 + tt:8 + tt + 1], slocr2[0:1, tt * 128:(tt + 1) * 128], ident_f[0:1, 0:1])
                return r
            ph.on("tensor", trslot)
            ph.bar()

            def evslot(e, ci=ci):
                e.tensor_copy(slotg_c[:, :], pT[:, 0:4])
                return e.tensor_copy(slocc[:, :], pT[:, 8:12])
            ph.on("vector", evslot)
            ph.bar()
            ph.on("vector", lambda e: e.tensor_scalar(mskc[:, :], slocc[:, :], float(4 * L1F_CP), None, ALU.is_lt))
            ph.bar()
            ph.on("vector", lambda e: e.select(slocc2[:, :], mskc[:, :], slocc[:, :], trash4[:, :]))
            ph.bar()
            ph.on("vector", lambda e, ci=ci: e.tensor_copy(slcols[:, ci * 4:(ci + 1) * 4], slocc2[:, :]))
            ph.bar()

            def stslot(e, ci=ci):
                ph.tag(e.dma_start(out=o_slotc[:, ci * 4:(ci + 1) * 4], in_=slotg_c[:, :]))
            ph.on("sync", stslot, ndma=1)
            ph.bar()

        def sct_all(e):
            for g in range(256):
                ph.tag(e.indirect_dma_start(
                    out=iv, out_offset=bass.IndirectOffsetOnAxis(ap=slcols[:, g:g + 1], axis=0),
                    in_=ids[:, g * 4:(g + 1) * 4], in_offset=None))
        ph.on("gpsimd", sct_all, ndma=256)
        ph.bar()

        for el in range(4):
            def ldwe(e, el=el):
                for dst, srcd in ((wg_bf, wg_sl), (wu_bf, wu_sl), (wd_bf, wd_sl)):
                    ph.tag(e.dma_start(out=dst.ap().rearrange("p (a o) -> p a o", a=8),
                                       in_=srcd[el * D:(el + 1) * D, :].rearrange("(a p) o -> p a o", p=128)))
            ph.on("gpsimd", ldwe, ndma=3)
            ph.bar()
            for grp in range(3):
                r0 = el * L1F_CP + grp * 512

                def ldiv(e, r0=r0):
                    ph.tag(e.dma_start(out=ivcols[:, :],
                                       in_=iv.rearrange("(r a) c -> r (a c)", a=4)[r0 // 4:r0 // 4 + 128, :]))
                ph.on("sync", ldiv, ndma=1)
                ph.on("vector", lambda e: e.memset(xg_tm[:, :], 0.0))
                ph.bar()

                def gat(e):
                    for a in range(4):
                        ph.tag(e.indirect_dma_start(
                            out=xg_tm[:, a * D:(a + 1) * D], out_offset=None,
                            in_=x_tm, in_offset=bass.IndirectOffsetOnAxis(ap=ivcols[:, a * 4:a * 4 + 1], axis=0)))
                ph.on("gpsimd", gat, ndma=4)
                ph.bar()
                ph.on("vector", lambda e: e.tensor_copy(xg_bf[:, :], xg_tm[:, :]))
                ph.bar()

                for dt in range(8):
                    def trp(e, dt=dt):
                        r = None
                        for a in range(4):
                            r = e.transpose(pTb[:, a * 128:(a + 1) * 128],
                                            xg_bf[:, a * D + dt * 128:a * D + dt * 128 + 128], identb[:, :])
                        return r
                    ph.on("tensor", trp)
                    ph.bar()
                    ph.on("vector", lambda e, dt=dt: e.tensor_copy(x_fm[:, dt * 512:(dt + 1) * 512], pTb[:, :]))
                    ph.bar()

                for dst, wsb, act in ((g_bf, wg_bf, AF.Silu), (u_bf2, wu_bf, None)):
                    for ft in range(8):
                        def mmg(e, wsb=wsb, ft=ft):
                            r = None
                            for i in range(8):
                                r = e.matmul(pMM[:, :], wsb[:, i * D + ft * 128:i * D + ft * 128 + 128],
                                             x_fm[:, i * 512:(i + 1) * 512], start=(i == 0), stop=(i == 7))
                            return r
                        ph.on("tensor", mmg)
                        ph.bar()
                        if act is None:
                            ph.on("vector", lambda e, dst=dst, ft=ft: e.tensor_copy(dst[:, ft * 512:(ft + 1) * 512], pMM[:, :]))
                        else:
                            ph.on("scalar", lambda e, dst=dst, ft=ft, act=act: e.activation(dst[:, ft * 512:(ft + 1) * 512], pMM[:, :], act))
                        ph.bar()
                ph.on("vector", lambda e: e.tensor_mul(g_bf[:, :], g_bf[:, :], u_bf2[:, :]))
                ph.bar()

                for ht in range(8):
                    def mmy(e, ht=ht):
                        r = None
                        for i in range(8):
                            r = e.matmul(pMM[:, :], wd_bf[:, i * D + ht * 128:i * D + ht * 128 + 128],
                                         g_bf[:, i * 512:(i + 1) * 512], start=(i == 0), stop=(i == 7))
                        return r
                    ph.on("tensor", mmy)
                    ph.bar()
                    ph.on("vector", lambda e, ht=ht: e.tensor_copy(y_bf[:, ht * 512:(ht + 1) * 512], pMM[:, :]))
                    ph.bar()

                for dt in range(8):
                    def trpy(e, dt=dt):
                        r = None
                        for a in range(4):
                            r = e.transpose(pTb[:, a * 128:(a + 1) * 128],
                                            y_bf[:, dt * 512 + a * 128:dt * 512 + a * 128 + 128], identb[:, :])
                        return r
                    ph.on("tensor", trpy)
                    ph.bar()

                    def evy(e, dt=dt):
                        return e.tensor_copy(
                            y_tmo.ap().rearrange("p (a d) -> p a d", a=4)[:, :, dt * 128:(dt + 1) * 128],
                            pTb.ap().rearrange("p (a t) -> p a t", a=4))
                    ph.on("vector", evy)
                    ph.bar()

                def sty(e, r0=r0):
                    ph.tag(e.dma_start(out=ybuf[r0:r0 + 512, :].rearrange("(p a) d -> p a d", a=4),
                                       in_=y_tmo.ap().rearrange("p (a d) -> p a d", a=4)))
                ph.on("sync", sty, ndma=1)
                ph.bar()

        ph.emit(S)
    return nc


def l07_inmaps(lg_g, x3_tm_g, inp):
    ins = []
    for c in range(NCORES):
        ins.append({
            "lg": cont(lg_g),
            "x3_tm": cont(x3_tm_g),
            "wg_sl": cont(inp["L1_ffn_w_gate"][4 * c:4 * c + 4]).reshape(4 * D, D),
            "wu_sl": cont(inp["L1_ffn_w_up"][4 * c:4 * c + 4]).reshape(4 * D, D),
            "wd_sl": cont(inp["L1_ffn_w_down"][4 * c:4 * c + 4]).reshape(4 * D, D),
            "eoff": np.full((1, 1), 4 * c * L1F_CP, np.float32),
        })
    return ins


def build_l08():
    nc = bass.Bass("TRN2", target_bir_lowering=False, num_devices=NCORES)
    CH = 512
    x_tm = nc.dram_tensor("x_tm", [TLOC, D], F32, kind="ExternalInput").ap()
    ybuf = nc.dram_tensor("ybuf", [32 * L1F_CP, D], BF16, kind="ExternalInput").ap()
    slot = nc.dram_tensor("slot", [128, TLOC // 128], U32, kind="ExternalInput").ap()
    gate = nc.dram_tensor("gate", [128, TLOC // 128], F32, kind="ExternalInput").ap()
    out = nc.dram_tensor("y_tm", [TLOC, D], F32, kind="ExternalOutput").ap()

    with contextlib.ExitStack() as es:
        S = es.enter_context
        slc = S(nc.sbuf_tensor("slc", [128, 32], U32))
        slcf = S(nc.sbuf_tensor("slcf", [128, 32], F32))
        mk8 = S(nc.sbuf_tensor("mk8", [128, 32], F32))
        slc2 = S(nc.sbuf_tensor("slc2", [128, 32], U32))
        gc = S(nc.sbuf_tensor("gc", [128, 32], F32))
        yg = S(nc.sbuf_tensor("yg", [128, 4 * D], BF16))
        o_tm = S(nc.sbuf_tensor("o_tm", [128, 4 * D], F32))
        res_tm = S(nc.sbuf_tensor("res_tm", [128, 4 * D], F32))
        ln_sq = S(nc.sbuf_tensor("ln_sq", [128, 4 * D], F32))
        ln_m = S(nc.sbuf_tensor("ln_m", [128, 4], F32))
        ln_v = S(nc.sbuf_tensor("ln_v", [128, 4], F32))
        ln_r = S(nc.sbuf_tensor("ln_r", [128, 4], F32))

        ph = Ph(nc, S)

        def ld0(e):
            ph.tag(e.dma_start(out=slc[:, :], in_=slot))
            ph.tag(e.dma_start(out=gc[:, :], in_=gate))
        ph.on("sync", ld0, ndma=2)
        ph.bar()
        ph.on("vector", lambda e: e.tensor_copy(slcf[:, :], slc[:, :]))
        ph.bar()
        ph.on("vector", lambda e: e.tensor_scalar(mk8[:, :], slcf[:, :], float(32 * L1F_CP), None, ALU.is_lt))
        ph.bar()

        def msk2(e):
            e.tensor_mul(gc[:, :], gc[:, :], mk8[:, :])
            return e.tensor_mul(slcf[:, :], slcf[:, :], mk8[:, :])
        ph.on("vector", msk2)
        ph.bar()
        ph.on("vector", lambda e: e.tensor_copy(slc2[:, :], slcf[:, :]))
        ph.bar()

        for ci in range(TLOC // CH):
            c0 = ci * CH

            def ldres(e, c0=c0):
                ph.tag(e.dma_start(out=res_tm.ap().rearrange("p (a d) -> p a d", a=4),
                                   in_=x_tm[c0:c0 + CH, :].rearrange("(a p) d -> p a d", p=128)))
            ph.on("sync", ldres, ndma=1)
            ph.on("vector", lambda e: e.memset(yg[:, :], 0.0))
            ph.bar()

            def gat(e, ci=ci):
                for a in range(4):
                    ph.tag(e.indirect_dma_start(
                        out=yg[:, a * D:(a + 1) * D], out_offset=None,
                        in_=ybuf, in_offset=bass.IndirectOffsetOnAxis(ap=slc2[:, ci * 4 + a:ci * 4 + a + 1], axis=0)))
            ph.on("gpsimd", gat, ndma=4)
            ph.bar()

            def comb(e, ci=ci):
                r = None
                for a in range(4):
                    r = e.tensor_scalar(o_tm[:, a * D:(a + 1) * D], yg[:, a * D:(a + 1) * D],
                                        gc[:, ci * 4 + a:ci * 4 + a + 1], None, ALU.mult)
                return r
            ph.on("vector", comb)
            ph.bar()
            ph.on("vector", lambda e: e.scalar_tensor_tensor(o_tm[:, :], res_tm[:, :], 2.0, o_tm[:, :],
                                                             ALU.mult, ALU.add))
            ph.bar()
            ln_tm_batch(ph, lambda i: o_tm[:, i * D:(i + 1) * D], lambda i: o_tm[:, i * D:(i + 1) * D],
                        ln_sq, ln_m, ln_v, ln_r, 4, D)

            def store(e, c0=c0):
                ph.tag(e.dma_start(out=out[c0:c0 + CH, :].rearrange("(a p) d -> p a d", p=128),
                                   in_=o_tm.ap().rearrange("p (a d) -> p a d", a=4)))
            ph.on("sync", store, ndma=1)
            ph.bar()

        ph.emit(S)
    return nc


def l08_inmaps(x3_fm_full, ybuf_g, slotc_g, gatec_g):
    ins = []
    for c in range(NCORES):
        b, q = c // 4, c % 4
        g0 = (b * T + q * TLOC) // 128
        ins.append({
            "x_tm": cont(x3_fm_full[b][:, q * TLOC:(q + 1) * TLOC].T),
            "ybuf": np.ascontiguousarray(ybuf_g),
            "slot": np.ascontiguousarray(slotc_g[:, g0:g0 + TLOC // 128]),
            "gate": np.ascontiguousarray(gatec_g[:, g0:g0 + TLOC // 128]),
        })
    return ins


def build_l09():
    nc = bass.Bass("TRN2", target_bir_lowering=False, num_devices=NCORES)
    DI = 2048
    x_tme = nc.dram_tensor("x_tme", [TEXT, D], F32, kind="ExternalInput").ap()
    inwT = nc.dram_tensor("inwT", [D, 2 * DI], F32, kind="ExternalInput").ap()
    cw = nc.dram_tensor("cw", [128, 16 * 4], F32, kind="ExternalInput").ap()
    cb = nc.dram_tensor("cb", [128, 16], F32, kind="ExternalInput").ap()
    xpwT = nc.dram_tensor("xpwT", [DI, 96], F32, kind="ExternalInput").ap()
    dtwT = nc.dram_tensor("dtwT", [64, DI], F32, kind="ExternalInput").ap()
    dtb = nc.dram_tensor("dtb", [128, 16], F32, kind="ExternalInput").ap()
    hlv = nc.dram_tensor("hlv", [128, 1], F32, kind="ExternalInput").ap()
    o_res = nc.dram_tensor("res_act", [DI, TEXT], F32, kind="ExternalOutput").ap()
    o_u = nc.dram_tensor("u", [DI, TEXT], F32, kind="ExternalOutput").ap()
    o_d = nc.dram_tensor("delta", [DI, TEXT], F32, kind="ExternalOutput").ap()
    o_bc = nc.dram_tensor("bc", [32, TEXT], F32, kind="ExternalOutput").ap()

    NT16 = 16
    with contextlib.ExitStack() as es:
        S = es.enter_context
        XW = TEXT + 6
        inw_bf = S(nc.sbuf_tensor("inw_bf", [128, 8 * 2 * DI], BF16))
        xx = S(nc.sbuf_tensor("xx9", [128, 16 * XW // 4], BF16)) if False else None
        xzx = S(nc.sbuf_tensor("xzx9", [128, 8 * XW], BF16))
        cw_c = S(nc.sbuf_tensor("cw_c9", [128, 64], F32))
        cb_c = S(nc.sbuf_tensor("cb_c9", [128, 16], F32))
        dtb_c = S(nc.sbuf_tensor("dtb_c9", [128, 16], F32))
        hl_c = S(nc.sbuf_tensor("hl_c9", [128, 1], F32))
        xpw_bf = S(nc.sbuf_tensor("xpw_bf9", [128, 16 * 96], BF16))
        dtw_bf = S(nc.sbuf_tensor("dtw_bf9", [64, DI], BF16))
        x_tm_t = S(nc.sbuf_tensor("x_tm_t9", [128, 4 * D], F32))
        n_bf = S(nc.sbuf_tensor("n_bf9", [128, 4 * D], BF16))
        u_ch = S(nc.sbuf_tensor("u_ch9", [128, 8 * 512], F32))
        u_bf = S(nc.sbuf_tensor("u_bf9", [128, 8 * 512], BF16))
        xdt = S(nc.sbuf_tensor("xdt9", [64, 512], BF16))
        ident_f = S(nc.sbuf_tensor("ident_f9", [128, 128], F32))
        ident = S(nc.sbuf_tensor("ident9", [128, 128], BF16))
        ln_sq = S(nc.sbuf_tensor("ln_sq9", [128, 4 * D], F32))
        dl1 = x_tm_t
        dl2 = ln_sq
        ln_m = S(nc.sbuf_tensor("ln_m9", [128, 4], F32))
        ln_v = S(nc.sbuf_tensor("ln_v9", [128, 4], F32))
        ln_r = S(nc.sbuf_tensor("ln_r9", [128, 4], F32))
        pMM = S(nc.psum_tensor("pMM9", [128, 512], F32))
        pT = S(nc.psum_tensor("pT9", [128, 512], BF16))

        ph = Ph(nc, S)

        def ld1(e):
            ph.tag(e.dma_start(out=inw_bf.ap().rearrange("p (a o) -> p a o", a=8),
                               in_=inwT.rearrange("(a p) o -> p a o", p=128)))
            ph.tag(e.dma_start(out=xpw_bf.ap().rearrange("p (a o) -> p a o", a=16),
                               in_=xpwT.rearrange("(a p) o -> p a o", p=128)))
            ph.tag(e.dma_start(out=dtw_bf[:, :], in_=dtwT))
        ph.on("gpsimd", ld1, ndma=3)

        def ld2(e):
            for dst, srcd in ((cw_c, cw), (cb_c, cb), (dtb_c, dtb), (hl_c, hlv)):
                ph.tag(e.dma_start(out=dst[:, :], in_=srcd))
        ph.on("sync", ld2, ndma=4)

        def mkid(e):
            return e.iota(ident_f[:, :], pattern=[[-1, 128]], base=0,
                          channel_multiplier=1, allow_small_or_imprecise_dtypes=True)
        ph.on("gpsimd", mkid)
        ph.bar()
        ph.on("vector", lambda e: e.tensor_scalar(ident_f[:, :], ident_f[:, :], 0.0, None, ALU.is_equal))
        ph.bar()
        ph.on("vector", lambda e: e.tensor_copy(ident[:, :], ident_f[:, :]))
        ph.bar()

        chs = [(i * 512, 512) for i in range(TEXT // 512)]
        if TEXT % 512:
            chs.append((TEXT // 512 * 512, TEXT % 512))

        for half in range(2):
            for c0, W in chs:
                TT = W // 128

                def ldc(e, c0=c0, W=W, TT=TT):
                    ph.tag(e.dma_start(out=x_tm_t.ap().rearrange("p (a d) -> p a d", a=4)[:, 0:TT, :],
                                       in_=x_tme[c0:c0 + W, :].rearrange("(a p) d -> p a d", p=128)))
                ph.on("sync", ldc, ndma=1)
                ph.bar()
                ph.on("vector", lambda e, TT=TT: e.tensor_copy(n_bf[:, 0:TT * D], x_tm_t[:, 0:TT * D]))
                ph.bar()
                for dt in range(8):
                    def trp(e, dt=dt, TT=TT):
                        r = None
                        for tt in range(TT):
                            r = e.transpose(pT[:, tt * 128:(tt + 1) * 128],
                                            n_bf[:, tt * D + dt * 128:tt * D + dt * 128 + 128],
                                            ident[:, :])
                        return r
                    ph.on("tensor", trp)
                    ph.bar()
                    ph.on("vector", lambda e, dt=dt, W=W: e.tensor_copy(u_ch[:, dt * 512:dt * 512 + W], pT[:, 0:W]))
                    ph.bar()
                ph.on("vector", lambda e: e.tensor_copy(u_bf[:, :], u_ch[:, :]))
                ph.bar()

                for ot in range(8):
                    go = half * 8 + ot

                    def mmip(e, go=go, W=W):
                        r = None
                        for i in range(8):
                            r = e.matmul(pMM[:, 0:W], inw_bf[:, i * 2 * DI + go * 128:i * 2 * DI + go * 128 + 128],
                                         u_bf[:, i * 512:i * 512 + W], start=(i == 0), stop=(i == 7))
                        return r
                    ph.on("tensor", mmip)
                    ph.bar()

                    def evip(e, ot=ot, c0=c0, W=W):
                        return e.tensor_copy(xzx[:, ot * XW + c0 + 3:ot * XW + c0 + 3 + W], pMM[:, 0:W])
                    ph.on("vector", evip)
                    ph.bar()
                for ot in range(8):
                    go = 16 + half * 8 + ot

                    def mmrs(e, go=go, W=W):
                        r = None
                        for i in range(8):
                            r = e.matmul(pMM[:, 0:W], inw_bf[:, i * 2 * DI + go * 128:i * 2 * DI + go * 128 + 128],
                                         u_bf[:, i * 512:i * 512 + W], start=(i == 0), stop=(i == 7))
                        return r
                    ph.on("tensor", mmrs)
                    ph.bar()

                    def evrs(e, ot=ot, W=W):
                        return e.activation(u_ch[:, ot * 512:ot * 512 + W], pMM[:, 0:W], AF.Silu)
                    ph.on("scalar", evrs)
                    ph.bar()

                def strs(e, c0=c0, W=W, half=half):
                    ph.tag(e.dma_start(
                        out=o_res.rearrange("(h a p) t -> h p a t", h=2, p=128)[half][:, :, c0:c0 + W],
                        in_=u_ch.ap().rearrange("p (a t) -> p a t", a=8)[:, :, 0:W]))
                ph.on("sync", strs, ndma=1)
                ph.bar()


            for c0, W in chs:
                def cvi(e, c0=c0, W=W, half=half):
                    r = None
                    for dt in range(8):
                        gdt = half * 8 + dt
                        for k in range(4):
                            off = c0 + k
                            srcv = xzx[:, dt * XW + off:dt * XW + off + W]
                            wcol = cw_c[:, gdt * 4 + k:gdt * 4 + k + 1]
                            if k == 0:
                                r = e.tensor_scalar(u_ch[:, dt * 512:dt * 512 + W], srcv, wcol, None, ALU.mult)
                            else:
                                r = e.scalar_tensor_tensor(u_ch[:, dt * 512:dt * 512 + W], srcv, wcol,
                                                           u_ch[:, dt * 512:dt * 512 + W], ALU.mult, ALU.add)
                    return r
                ph.on("vector", cvi)
                ph.bar()

                def cvb(e, W=W, half=half):
                    r = None
                    for dt in range(8):
                        gdt = half * 8 + dt
                        r = e.tensor_scalar(u_ch[:, dt * 512:dt * 512 + W],
                                            u_ch[:, dt * 512:dt * 512 + W], cb_c[:, gdt:gdt + 1], None, ALU.add)
                    return r
                ph.on("vector", cvb)
                ph.bar()

                def usil(e, W=W):
                    r = None
                    for dt in range(8):
                        r = e.activation(u_ch[:, dt * 512:dt * 512 + W], u_ch[:, dt * 512:dt * 512 + W], AF.Silu)
                    return r
                ph.on("scalar", usil)
                ph.bar()

                def stu(e, c0=c0, W=W, half=half):
                    ph.tag(e.dma_start(
                        out=o_u.rearrange("(h a p) t -> h p a t", h=2, p=128)[half][:, :, c0:c0 + W],
                        in_=u_ch.ap().rearrange("p (a t) -> p a t", a=8)[:, :, 0:W]))
                ph.on("sync", stu, ndma=1)
                ph.bar()

        for c0, W in chs:
            def ldu(e, c0=c0, W=W):
                ph.tag(e.dma_start(out=u_bf.ap().rearrange("p (a t) -> p a t", a=8)[:, :, 0:W],
                                   in_=o_u.rearrange("(a p) t -> p a t", p=128)[:, 0:8, c0:c0 + W]))
                ph.tag(e.dma_start(out=n_bf.ap().rearrange("p (a t) -> p a t", a=8)[:, :, 0:W],
                                   in_=o_u.rearrange("(a p) t -> p a t", p=128)[:, 8:16, c0:c0 + W]))
            ph.on("gpsimd", ldu, ndma=2)
            ph.bar()

            def mmxd(e, W=W):
                r = None
                for i in range(8):
                    r = e.matmul(pMM[0:96, 0:W], xpw_bf[:, i * 96:(i + 1) * 96],
                                 u_bf[:, i * 512:i * 512 + W], start=(i == 0), stop=False)
                for i in range(8):
                    r = e.matmul(pMM[0:96, 0:W], xpw_bf[:, (8 + i) * 96:(9 + i) * 96],
                                 n_bf[:, i * 512:i * 512 + W], start=False, stop=(i == 7))
                return r
            ph.on("tensor", mmxd)
            ph.bar()

            def evxd(e, W=W):
                return e.tensor_copy(xdt[:, 0:W], pMM[0:64, 0:W])
            ph.on("vector", evxd)
            ph.bar()

            def evbc(e, W=W):
                return e.tensor_copy(ln_sq[0:32, 0:W], pMM[64:96, 0:W])
            ph.on("vector", evbc)
            ph.bar()

            def stbc2(e, c0=c0, W=W):
                ph.tag(e.dma_start(out=o_bc[:, c0:c0 + W], in_=ln_sq[0:32, 0:W]))
            ph.on("sync", stbc2, ndma=1)
            ph.bar()

            for grp in range(2):
                for ot in range(8):
                    got = grp * 8 + ot

                    def mmdt(e, got=got, W=W):
                        return e.matmul(pMM[:, 0:W], dtw_bf[:, got * 128:(got + 1) * 128],
                                        xdt[:, 0:W], start=True, stop=True)
                    ph.on("tensor", mmdt)
                    ph.bar()

                    def evdt(e, ot=ot, got=got, W=W):
                        return e.tensor_scalar(dl1[:, ot * 512:ot * 512 + W], pMM[:, 0:W],
                                               dtb_c[:, got:got + 1], None, ALU.add)
                    ph.on("vector", evdt)
                    ph.bar()

                def sp1(e, W=W):
                    r = None
                    for dt in range(8):
                        r = e.activation(dl2[:, dt * 512:dt * 512 + W], dl1[:, dt * 512:dt * 512 + W], AF.Abs)
                    return r
                ph.on("scalar", sp1)
                ph.bar()

                def sp2(e, W=W):
                    r = None
                    for dt in range(8):
                        r = e.activation(dl2[:, dt * 512:dt * 512 + W], dl2[:, dt * 512:dt * 512 + W], AF.Exp, scale=-1.0)
                    return r
                ph.on("scalar", sp2)
                ph.bar()
                ph.on("vector", lambda e: e.tensor_scalar(dl2[:, :], dl2[:, :], 1.0, None, ALU.add))
                ph.bar()

                def sp3(e, W=W):
                    r = None
                    for dt in range(8):
                        r = e.activation(dl2[:, dt * 512:dt * 512 + W], dl2[:, dt * 512:dt * 512 + W], AF.Ln)
                    return r
                ph.on("scalar", sp3)
                ph.bar()

                def sp4(e, W=W):
                    r = None
                    for dt in range(8):
                        r = e.activation(dl1[:, dt * 512:dt * 512 + W], dl1[:, dt * 512:dt * 512 + W], AF.Relu)
                    return r
                ph.on("scalar", sp4)
                ph.bar()
                ph.on("vector", lambda e: e.tensor_add(dl1[:, :], dl1[:, :], dl2[:, :]))
                ph.bar()

                def std(e, c0=c0, W=W, grp=grp):
                    ph.tag(e.dma_start(
                        out=o_d.rearrange("(g a p) t -> g p a t", g=2, p=128)[grp][:, :, c0:c0 + W],
                        in_=dl1.ap().rearrange("p (a t) -> p a t", a=8)[:, :, 0:W]))
                ph.on("sync", std, ndma=1)
                ph.bar()

        ph.emit(S)
    return nc


def l09_inmaps(x4_fm_full, inp):
    args = dict(
        inwT=fmT(inp["L2_attn_in_w"]),
        cw=cont(inp["L2_attn_conv_w"]).reshape(2048, 4).reshape(16, 128, 4).transpose(1, 0, 2).reshape(128, 64).copy(),
        cb=col128(inp["L2_attn_conv_b"]),
        xpwT=fmT(inp["L2_attn_xproj_w"]),
        dtwT=fmT(inp["L2_attn_dt_w"]),
        dtb=col128(inp["L2_attn_dt_b"]),
    )
    ins = []
    for c in range(NCORES):
        b, q = c // 4, c % 4
        t0 = q * TLOC
        xb = x4_fm_full[b]
        ext = np.zeros((TEXT, D), np.float32)
        lo = t0 - 3
        hi = t0 + TLOC + 3
        slo = max(lo, 0)
        shi = min(hi, T)
        ext[slo - lo:slo - lo + (shi - slo)] = xb[:, slo:shi].T
        m = dict(args)
        m["x_tme"] = ext
        m["hlv"] = np.full((128, 1), 0.0 if q == 0 else 1.0, np.float32)
        ins.append(m)
    return ins


def build_l11():
    nc = bass.Bass("TRN2", target_bir_lowering=False, num_devices=NCORES)
    DI = 2048
    CH = 512
    x_tm = nc.dram_tensor("x_tm", [TLOC, D], F32, kind="ExternalInput").ap()
    yd = nc.dram_tensor("y", [DI, TLOC], F32, kind="ExternalInput").ap()
    ud = nc.dram_tensor("u", [DI, TLOC], F32, kind="ExternalInput").ap()
    rd = nc.dram_tensor("res_act", [DI, TLOC], F32, kind="ExternalInput").ap()
    dpc = nc.dram_tensor("dpc", [128, 16], F32, kind="ExternalInput").ap()
    outwT = nc.dram_tensor("outwT", [DI, D], F32, kind="ExternalInput").ap()
    out = nc.dram_tensor("y_tm", [TLOC, D], F32, kind="ExternalOutput").ap()

    with contextlib.ExitStack() as es:
        S = es.enter_context
        ow_bf = S(nc.sbuf_tensor("ow_bf11", [128, 16 * D], BF16))
        dp_c = S(nc.sbuf_tensor("dp_c11", [128, 16], F32))
        t_y = S(nc.sbuf_tensor("t_y11", [128, 16 * CH], F32))
        t_u = S(nc.sbuf_tensor("t_u11", [128, 16 * CH], F32))
        t_r = S(nc.sbuf_tensor("t_r11", [128, 16 * CH], F32))
        g_bf = S(nc.sbuf_tensor("g_bf11", [128, 16 * CH], BF16))
        o_tm = S(nc.sbuf_tensor("o_tm11", [128, 4 * D], F32))
        res_tm = S(nc.sbuf_tensor("res_tm11", [128, 4 * D], F32))
        ln_sq = S(nc.sbuf_tensor("ln_sq11", [128, 4 * D], F32))
        ln_m = S(nc.sbuf_tensor("ln_m11", [128, 4], F32))
        ln_v = S(nc.sbuf_tensor("ln_v11", [128, 4], F32))
        ln_r = S(nc.sbuf_tensor("ln_r11", [128, 4], F32))
        pO = S(nc.psum_tensor("pO11", [128, 1024], F32))

        ph = Ph(nc, S)

        def ldw(e):
            ph.tag(e.dma_start(out=ow_bf.ap().rearrange("p (a o) -> p a o", a=16),
                               in_=outwT.rearrange("(a p) o -> p a o", p=128)))
        ph.on("gpsimd", ldw, ndma=1)

        def ldc0(e):
            ph.tag(e.dma_start(out=dp_c[:, :], in_=dpc))
        ph.on("sync", ldc0, ndma=1)
        ph.bar()

        for ci in range(TLOC // CH):
            c0 = ci * CH

            def ldc(e, c0=c0):
                for dst, srcd in ((t_y, yd), (t_u, ud), (t_r, rd)):
                    ph.tag(e.dma_start(out=dst.ap().rearrange("p (a t) -> p a t", a=16),
                                       in_=srcd.rearrange("(a p) t -> p a t", p=128)[:, :, c0:c0 + CH]))
                ph.tag(e.dma_start(out=res_tm.ap().rearrange("p (a d) -> p a d", a=4),
                                   in_=x_tm[c0:c0 + CH, :].rearrange("(a p) d -> p a d", p=128)))
            ph.on("sync", ldc, ndma=4)
            ph.bar()

            def dterm(e):
                r = None
                for dt in range(16):
                    sl = slice(dt * CH, (dt + 1) * CH)
                    r = e.scalar_tensor_tensor(t_y[:, sl], t_u[:, sl], dp_c[:, dt:dt + 1],
                                               t_y[:, sl], ALU.mult, ALU.add)
                return r
            ph.on("vector", dterm)
            ph.bar()
            ph.on("vector", lambda e: e.tensor_mul(g_bf[:, :], t_y[:, :], t_r[:, :]))
            ph.bar()

            for tt in range(4):
                def mm2(e, tt=tt):
                    r = None
                    for half in range(2):
                        psl = pO[:, half * 512:(half + 1) * 512]
                        for i in range(16):
                            r = e.matmul(psl, g_bf[:, i * CH + tt * 128:i * CH + tt * 128 + 128],
                                         ow_bf[:, i * D + half * 512:i * D + half * 512 + 512],
                                         start=(i == 0), stop=(i == 15))
                    return r
                ph.on("tensor", mm2)
                ph.bar()

                def evo(e, tt=tt):
                    return e.tensor_add(o_tm[:, tt * D:(tt + 1) * D], res_tm[:, tt * D:(tt + 1) * D], pO[:, :])
                ph.on("vector", evo)
                ph.bar()

            ln_tm_batch(ph, lambda i: o_tm[:, i * D:(i + 1) * D], lambda i: o_tm[:, i * D:(i + 1) * D],
                        ln_sq, ln_m, ln_v, ln_r, 4, D)

            def store(e, c0=c0):
                ph.tag(e.dma_start(out=out[c0:c0 + CH, :].rearrange("(a p) d -> p a d", p=128),
                                   in_=o_tm.ap().rearrange("p (a d) -> p a d", a=4)))
            ph.on("sync", store, ndma=1)
            ph.bar()

        ph.emit(S)
    return nc


def l11_inmaps(x4_fm_full, y_g, u_g, r_g, inp):
    args = dict(dpc=col128(inp["L2_attn_D"]), outwT=fmT(inp["L2_attn_out_w"]))
    ins = []
    for c in range(NCORES):
        b, q = c // 4, c % 4
        cs = b * T + q * TLOC
        m = dict(args)
        m["x_tm"] = cont(x4_fm_full[b][:, q * TLOC:(q + 1) * TLOC].T)
        m["y"] = np.ascontiguousarray(y_g[:, cs:cs + TLOC])
        m["u"] = np.ascontiguousarray(u_g[:, cs:cs + TLOC])
        m["res_act"] = np.ascontiguousarray(r_g[:, cs:cs + TLOC])
        ins.append(m)
    return ins


def build_l12():
    nc = bass.Bass("TRN2", target_bir_lowering=False, num_devices=NCORES)
    CH = 512
    x_tm = nc.dram_tensor("x_tm", [TLOC, D], F32, kind="ExternalInput").ap()
    x_fm = nc.dram_tensor("x_fm", [D, TLOC], F32, kind="ExternalInput").ap()
    wg = nc.dram_tensor("wg", [D, D], F32, kind="ExternalInput").ap()
    bg_r = nc.dram_tensor("bg_r", [1, D], F32, kind="ExternalInput").ap()
    we = nc.dram_tensor("we", [8 * D, D], F32, kind="ExternalInput").ap()
    be_r = nc.dram_tensor("be_r", [1, 8 * D], F32, kind="ExternalInput").ap()
    out = nc.dram_tensor("y_tm", [TLOC, D], F32, kind="ExternalOutput").ap()
    o_part = nc.dram_tensor("o_part", [TLOC, D], F32).ap()

    with contextlib.ExitStack() as es:
        S = es.enter_context
        wg_bf = S(nc.sbuf_tensor("wg_bf12", [128, 8 * D], BF16))
        we_bf = S(nc.sbuf_tensor("we_bf12", [128, 32 * D], BF16))
        bg_row = S(nc.sbuf_tensor("bg_row12", [1, D], BF16))
        be_row = S(nc.sbuf_tensor("be_row12", [1, 8 * D], BF16))
        onesr = S(nc.sbuf_tensor("onesr12", [1, 512], BF16))
        x_bf = S(nc.sbuf_tensor("x_bf12", [128, 8 * CH], BF16))
        lg_tm = S(nc.sbuf_tensor("lg_tm12", [128, 4 * D], F32))
        sq = S(nc.sbuf_tensor("sq12", [128, 4 * D], F32))
        zeq = S(nc.sbuf_tensor("zeq12", [128, D], F32))
        zin = S(nc.sbuf_tensor("zin12", [128, D], F32))
        ones1k = S(nc.sbuf_tensor("ones1k12", [128, D], F32))
        i8row = S(nc.sbuf_tensor("i8row12", [128, D], F32))
        mxc = S(nc.sbuf_tensor("mxc12", [128, 4], F32))
        dnc = S(nc.sbuf_tensor("dnc12", [128, 4], F32))
        pmc = S(nc.sbuf_tensor("pmc12", [128, 4], F32))
        evc = S(nc.sbuf_tensor("evc12", [128, 4], F32))
        msk8 = S(nc.sbuf_tensor("msk812", [128, 4 * 8], F32))
        h_tm = S(nc.sbuf_tensor("h_tm12", [128, 4 * D], F32))
        o_tm = S(nc.sbuf_tensor("o_tm12", [128, 4 * D], F32))
        res_tm = S(nc.sbuf_tensor("res_tm12", [128, 4 * D], F32))
        ln_m = S(nc.sbuf_tensor("ln_m12", [128, 4], F32))
        ln_v = S(nc.sbuf_tensor("ln_v12", [128, 4], F32))
        ln_r = S(nc.sbuf_tensor("ln_r12", [128, 4], F32))
        pMM = S(nc.psum_tensor("pMM12", [128, 1024], F32))

        ph = Ph(nc, S)

        def ldw(e):
            ph.tag(e.dma_start(out=wg_bf.ap().rearrange("p (a o) -> p a o", a=8),
                               in_=wg.rearrange("(a p) o -> p a o", p=128)))
            ph.tag(e.dma_start(out=bg_row[:, :], in_=bg_r))
            ph.tag(e.dma_start(out=be_row[:, :], in_=be_r))
        ph.on("gpsimd", ldw, ndma=3)
        ph.on("gpsimd", lambda e: e.iota(i8row[:, :], pattern=[[0, 128], [1, 8]], base=0,
                                         channel_multiplier=0, allow_small_or_imprecise_dtypes=True))

        def ms(e):
            e.memset(onesr[:, :], 1.0)
            return e.memset(ones1k[:, :], 1.0)
        ph.on("vector", ms)
        ph.bar()

        for grp in range(2):
            def ldwe(e, grp=grp):
                ph.tag(e.dma_start(out=we_bf.ap().rearrange("p (a o) -> p a o", a=32),
                                   in_=we[grp * 4 * D:(grp + 1) * 4 * D, :].rearrange("(a p) o -> p a o", p=128)))
            ph.on("gpsimd", ldwe, ndma=1)
            ph.bar()

            for ci in range(TLOC // CH):
                c0 = ci * CH

                def ldc(e, c0=c0):
                    ph.tag(e.dma_start(out=x_bf.ap().rearrange("p (a t) -> p a t", a=8),
                                       in_=x_fm.rearrange("(a p) t -> p a t", p=128)[:, :, c0:c0 + CH]))
                ph.on("gpsimd", ldc, ndma=1)
                if grp == 1:
                    def ldres(e, c0=c0):
                        ph.tag(e.dma_start(out=res_tm.ap().rearrange("p (a d) -> p a d", a=4),
                                           in_=x_tm[c0:c0 + CH, :].rearrange("(a p) d -> p a d", p=128)))
                    ph.on("sync", ldres, ndma=1)

                    def ldpart(e, c0=c0):
                        ph.tag(e.dma_start(out=o_tm.ap().rearrange("p (a d) -> p a d", a=4),
                                           in_=o_part[c0:c0 + CH, :].rearrange("(a p) d -> p a d", p=128)))
                    ph.on("sync", ldpart, ndma=1)
                ph.bar()

                for tt in range(4):
                    def mmlg(e, tt=tt):
                        r = None
                        for half in range(2):
                            psl = pMM[:, half * 512:(half + 1) * 512]
                            for i in range(8):
                                e.matmul(psl, x_bf[:, i * CH + tt * 128:i * CH + tt * 128 + 128],
                                         wg_bf[:, i * D + half * 512:i * D + half * 512 + 512],
                                         start=(i == 0), stop=False)
                            r = e.matmul(psl, onesr[:, tt * 128:(tt + 1) * 128],
                                         bg_row[:, half * 512:(half + 1) * 512], start=False, stop=True)
                        return r
                    ph.on("tensor", mmlg)
                    ph.bar()
                    ph.on("vector", lambda e, tt=tt: e.tensor_copy(lg_tm[:, tt * D:(tt + 1) * D], pMM[:, :]))
                    ph.bar()

                def red1(e):
                    r = None
                    for tt in range(4):
                        r = e.reduce_max(mxc[:, tt:tt + 1], lg_tm[:, tt * D:(tt + 1) * D], axis=AX.X)
                    return r
                ph.on("vector", red1)
                ph.bar()
                if grp == 1:
                    def esub(e):
                        r = None
                        for tt in range(4):
                            r = e.tensor_scalar(sq[:, tt * D:(tt + 1) * D], lg_tm[:, tt * D:(tt + 1) * D],
                                                mxc[:, tt:tt + 1], None, ALU.subtract)
                        return r
                    ph.on("vector", esub)
                    ph.bar()
                    ph.on("scalar", lambda e: e.activation(sq[:, :], sq[:, :], AF.Exp))
                    ph.bar()

                    def red2(e):
                        r = None
                        for tt in range(4):
                            r = e.reduce_sum(dnc[:, tt:tt + 1], sq[:, tt * D:(tt + 1) * D], axis=AX.X)
                        return r
                    ph.on("vector", red2)
                    ph.bar()
                    ph.on("vector", lambda e: e.reciprocal(pmc[:, :], dnc[:, :]))
                    ph.bar()

                for tt in range(4):
                    sl = slice(tt * D, (tt + 1) * D)
                    ph.on("vector", lambda e, sl=sl, tt=tt: e.tensor_scalar(zeq[:, :], lg_tm[:, sl],
                                                                            mxc[:, tt:tt + 1], None, ALU.is_equal))
                    ph.bar()
                    ph.on("vector", lambda e: e.tensor_tensor_scan(zin[:, :], ones1k[:, :], zeq[:, :],
                                                                   0.0, ALU.mult, ALU.add))
                    ph.bar()
                    ph.on("vector", lambda e: e.tensor_scalar(zin[:, :], zin[:, :], 1.0, None, ALU.is_equal))
                    ph.bar()
                    ph.on("vector", lambda e: e.tensor_mul(zeq[:, :], zeq[:, :], zin[:, :]))
                    ph.bar()
                    ph.on("vector", lambda e: e.tensor_mul(zeq[:, :], zeq[:, :], i8row[:, :]))
                    ph.bar()
                    ph.on("vector", lambda e, tt=tt: e.reduce_sum(evc[:, tt:tt + 1], zeq[:, :], axis=AX.X))
                    ph.bar()

                def mkmask(e, grp=grp):
                    r = None
                    for tt in range(4):
                        for exl in range(4):
                            r = e.tensor_scalar(msk8[:, tt * 8 + exl:tt * 8 + exl + 1], evc[:, tt:tt + 1],
                                                float(grp * 4 + exl), None, ALU.is_equal)
                    return r
                ph.on("vector", mkmask)
                ph.bar()
                if grp == 0:
                    ph.on("vector", lambda e: e.memset(o_tm[:, :], 0.0))
                    ph.bar()

                for exl in range(4):
                    ex = grp * 4 + exl
                    for tt in range(4):
                        def mme(e, exl=exl, ex=ex, tt=tt):
                            r = None
                            for half in range(2):
                                psl = pMM[:, half * 512:(half + 1) * 512]
                                for i in range(8):
                                    e.matmul(psl, x_bf[:, i * CH + tt * 128:i * CH + tt * 128 + 128],
                                             we_bf[:, (exl * 8 + i) * D + half * 512:(exl * 8 + i) * D + half * 512 + 512],
                                             start=(i == 0), stop=False)
                                r = e.matmul(psl, onesr[:, tt * 128:(tt + 1) * 128],
                                             be_row[:, ex * D + half * 512:ex * D + half * 512 + 512],
                                             start=False, stop=True)
                            return r
                        ph.on("tensor", mme)
                        ph.bar()
                        ph.on("scalar", lambda e, tt=tt: e.activation(h_tm[:, tt * D:(tt + 1) * D], pMM[:, :], AF.Relu))
                        ph.bar()

                        def acc(e, exl=exl, tt=tt):
                            return e.scalar_tensor_tensor(o_tm[:, tt * D:(tt + 1) * D], h_tm[:, tt * D:(tt + 1) * D],
                                                          msk8[:, tt * 8 + exl:tt * 8 + exl + 1],
                                                          o_tm[:, tt * D:(tt + 1) * D], ALU.mult, ALU.add)
                        ph.on("vector", acc)
                        ph.bar()

                if grp == 0:
                    def stpart(e, c0=c0):
                        ph.tag(e.dma_start(out=o_part[c0:c0 + CH, :].rearrange("(a p) d -> p a d", p=128),
                                           in_=o_tm.ap().rearrange("p (a d) -> p a d", a=4)))
                    ph.on("sync", stpart, ndma=1)
                    ph.bar()
                else:
                    def fin(e):
                        r = None
                        for tt in range(4):
                            r = e.tensor_scalar(o_tm[:, tt * D:(tt + 1) * D], o_tm[:, tt * D:(tt + 1) * D],
                                                pmc[:, tt:tt + 1], None, ALU.mult)
                        return r
                    ph.on("vector", fin)
                    ph.bar()
                    ph.on("vector", lambda e: e.tensor_add(o_tm[:, :], o_tm[:, :], res_tm[:, :]))
                    ph.bar()
                    ln_tm_batch(ph, lambda i: o_tm[:, i * D:(i + 1) * D], lambda i: o_tm[:, i * D:(i + 1) * D],
                                sq, ln_m, ln_v, ln_r, 4, D)

                    def store(e, c0=c0):
                        ph.tag(e.dma_start(out=out[c0:c0 + CH, :].rearrange("(a p) d -> p a d", p=128),
                                           in_=o_tm.ap().rearrange("p (a d) -> p a d", a=4)))
                    ph.on("sync", store, ndma=1)
                    ph.bar()

        ph.emit(S)
    return nc


def l12_inmaps(x5_fm_full, inp):
    args = dict(
        wg=cont(inp["L2_ffn_Wg"]),
        bg_r=cont(inp["L2_ffn_bg"]).reshape(1, D),
        we=cont(inp["L2_ffn_We"]).reshape(8 * D, D),
        be_r=cont(inp["L2_ffn_be"]).reshape(1, 8 * D),
    )
    ins = []
    for c in range(NCORES):
        b, q = c // 4, c % 4
        m = dict(args)
        m["x_tm"] = cont(x5_fm_full[b][:, q * TLOC:(q + 1) * TLOC].T)
        m["x_fm"] = np.ascontiguousarray(x5_fm_full[b][:, q * TLOC:(q + 1) * TLOC])
        ins.append(m)
    return ins


TE3 = TLOC + 256


def build_l13a():
    nc = bass.Bass("TRN2", target_bir_lowering=False, num_devices=NCORES)
    x_tme = nc.dram_tensor("x_tme", [TE3, D], F32, kind="ExternalInput").ap()
    x_fme = nc.dram_tensor("x_fme", [D, TE3], F32, kind="ExternalInput").ap()
    ipwT = nc.dram_tensor("ipwT", [D, 128], F32, kind="ExternalInput").ap()
    ldec = nc.dram_tensor("ldec", [64, 1], F32, kind="ExternalInput").ap()
    freq = nc.dram_tensor("freq", [64, 1], F32, kind="ExternalInput").ap()
    coupT = nc.dram_tensor("coupT", [8, 16, 16], F32, kind="ExternalInput").ap()
    opwT = nc.dram_tensor("opwT", [128, D], F32, kind="ExternalInput").ap()
    gwT = nc.dram_tensor("gwT", [D, D], F32, kind="ExternalInput").ap()
    gb = nc.dram_tensor("gb", [128, 8], F32, kind="ExternalInput").ap()
    n1g = nc.dram_tensor("n1g", [128, 8], F32, kind="ExternalInput").ap()
    n1b = nc.dram_tensor("n1b", [128, 8], F32, kind="ExternalInput").ap()
    hlv = nc.dram_tensor("hlv", [128, 1], F32, kind="ExternalInput").ap()
    o_x7 = nc.dram_tensor("x7_fm", [D, TLOC], F32, kind="ExternalOutput").ap()

    with contextlib.ExitStack() as es:
        S = es.enter_context
        ipw_bf = S(nc.sbuf_tensor("ipw13", [128, 8 * 128], BF16))
        opw_bf = S(nc.sbuf_tensor("opw13", [128, D], BF16))
        gw_bf = S(nc.sbuf_tensor("gw13", [128, 8 * D], BF16))
        gb_c = S(nc.sbuf_tensor("gb13", [128, 8], F32))
        n1g_c = S(nc.sbuf_tensor("n1g13", [128, 8], F32))
        n1b_c = S(nc.sbuf_tensor("n1b13", [128, 8], F32))
        hl_c = S(nc.sbuf_tensor("hl13", [128, 1], F32))
        bigc = S(nc.sbuf_tensor("bigc13", [128, 128], BF16))
        bigcf = S(nc.sbuf_tensor("bigcf13", [128, 128], F32))
        mcol = S(nc.sbuf_tensor("mcol13", [64, 1], F32))
        thcol = S(nc.sbuf_tensor("thcol13", [64, 1], F32))
        csn = S(nc.sbuf_tensor("csn13", [64, 4], F32))
        Rbr = S(nc.sbuf_tensor("Rbr13", [64, 1], F32))
        Rbi = S(nc.sbuf_tensor("Rbi13", [64, 1], F32))
        R5r = S(nc.sbuf_tensor("R5r13", [64, 1], F32))
        R5i = S(nc.sbuf_tensor("R5i13", [64, 1], F32))
        Bsr = S(nc.sbuf_tensor("Bsr13", [64, 1], F32))
        Bsi = S(nc.sbuf_tensor("Bsi13", [64, 1], F32))
        tA = S(nc.sbuf_tensor("tA13", [64, 1], F32))
        tB = S(nc.sbuf_tensor("tB13", [64, 1], F32))
        tC = S(nc.sbuf_tensor("tC13", [64, 1], F32))
        mbc = S(nc.sbuf_tensor("mbc13", [64, 512], F32))
        Rr = S(nc.sbuf_tensor("Rr13", [64, 512], F32))
        Ri = S(nc.sbuf_tensor("Ri13", [64, 512], F32))
        Rcr = S(nc.sbuf_tensor("Rcr13", [64, 512], F32))
        Rci = S(nc.sbuf_tensor("Rci13", [64, 512], F32))
        t1 = S(nc.sbuf_tensor("t113", [64, 512], F32))
        t2 = S(nc.sbuf_tensor("t213", [64, 512], F32))
        bpr = S(nc.sbuf_tensor("bpr13", [64, 512], F32))
        bpi = S(nc.sbuf_tensor("bpi13", [64, 512], F32))
        gsr = S(nc.sbuf_tensor("gsr13", [64, 512], F32))
        gsi = S(nc.sbuf_tensor("gsi13", [64, 512], F32))
        carr = S(nc.sbuf_tensor("carr13", [64, 1], F32))
        cari = S(nc.sbuf_tensor("cari13", [64, 1], F32))
        cc = S(nc.sbuf_tensor("cc13", [128, 512], BF16))
        x_tm_t = S(nc.sbuf_tensor("xtm13", [128, 4 * D], F32))
        n_bf = S(nc.sbuf_tensor("nbf13", [128, 4 * D], BF16))
        xn_fm = S(nc.sbuf_tensor("xnfm13", [128, 8 * 512], BF16))
        x_fm = S(nc.sbuf_tensor("xfm13", [128, 8 * 512], F32))
        betar = S(nc.sbuf_tensor("betar13", [64, 512], F32))
        betai = S(nc.sbuf_tensor("betai13", [64, 512], F32))
        eig_bf = S(nc.sbuf_tensor("eig13", [128, 512], BF16))
        o_fm = S(nc.sbuf_tensor("ofm13", [128, 8 * 512], F32))
        x7_fm = S(nc.sbuf_tensor("x7fm13", [128, 8 * 512], F32))
        ident_f = S(nc.sbuf_tensor("idf13", [128, 128], F32))
        ident = S(nc.sbuf_tensor("id13", [128, 128], BF16))
        ln_sq = S(nc.sbuf_tensor("lnsq13", [128, 4 * D], F32))
        ln_m = S(nc.sbuf_tensor("lnm13", [128, 4], F32))
        ln_v = S(nc.sbuf_tensor("lnv13", [128, 4], F32))
        ln_r = S(nc.sbuf_tensor("lnr13", [128, 4], F32))
        pMM = S(nc.psum_tensor("pMM13", [128, 512], F32))
        pT = S(nc.psum_tensor("pT13", [128, 512], BF16))

        ph = Ph(nc, S)

        def ldw(e):
            ph.tag(e.dma_start(out=ipw_bf.ap().rearrange("p (a o) -> p a o", a=8),
                               in_=ipwT.rearrange("(a p) o -> p a o", p=128)))
            ph.tag(e.dma_start(out=opw_bf[:, :], in_=opwT))
            ph.tag(e.dma_start(out=gw_bf.ap().rearrange("p (a o) -> p a o", a=8),
                               in_=gwT.rearrange("(a p) o -> p a o", p=128)))
        ph.on("gpsimd", ldw, ndma=3)

        def ldc(e):
            for dst, srcd in ((gb_c, gb), (n1g_c, n1g), (n1b_c, n1b), (hl_c, hlv),
                              (mcol, ldec), (thcol, freq)):
                ph.tag(e.dma_start(out=dst[:, :], in_=srcd))
        ph.on("sync", ldc, ndma=6)

        def mkid(e):
            return e.iota(ident_f[:, :], pattern=[[-1, 128]], base=0,
                          channel_multiplier=1, allow_small_or_imprecise_dtypes=True)
        ph.on("gpsimd", mkid)
        ph.bar()
        ph.on("vector", lambda e: e.tensor_scalar(ident_f[:, :], ident_f[:, :], 0.0, None, ALU.is_equal))
        ph.on("vector", lambda e: e.memset(bigcf[:, :], 0.0))
        ph.bar()
        ph.on("vector", lambda e: e.tensor_copy(ident[:, :], ident_f[:, :]))
        ph.bar()

        def ldcoup(e):
            for g in range(8):
                ph.tag(e.dma_start(out=bigcf[g * 16:(g + 1) * 16, g * 16:(g + 1) * 16], in_=coupT[g]))
        ph.on("gpsimd", ldcoup, ndma=8)
        ph.bar()
        ph.on("vector", lambda e: e.tensor_copy(bigc[:, :], bigcf[:, :]))
        ph.on("scalar", lambda e: e.activation(mcol[:, :], mcol[:, :], AF.Sigmoid))
        ph.bar()
        ph.on("vector", lambda e: e.tensor_scalar(csn[:, 0:1], thcol[:, :], float(np.pi), None, ALU.is_gt))
        ph.bar()
        ph.on("vector", lambda e: e.scalar_tensor_tensor(thcol[:, :], csn[:, 0:1], float(-2 * np.pi),
                                                         thcol[:, :], ALU.mult, ALU.add))
        ph.bar()
        ph.on("scalar", lambda e: e.activation(csn[:, 0:1], thcol[:, :], AF.Sin, scale=0.5))
        ph.bar()
        ph.on("vector", lambda e: e.tensor_mul(csn[:, 1:2], csn[:, 0:1], csn[:, 0:1]))
        ph.bar()
        ph.on("vector", lambda e: e.tensor_scalar(csn[:, 2:3], csn[:, 1:2], -1.0, 1.0, ALU.mult, ALU.add))
        ph.bar()
        ph.on("scalar", lambda e: e.activation(csn[:, 2:3], csn[:, 2:3], AF.Sqrt))
        ph.bar()

        def mk_trig(e):
            e.tensor_mul(csn[:, 3:4], csn[:, 0:1], csn[:, 2:3])
            return e.tensor_scalar(csn[:, 1:2], csn[:, 1:2], -2.0, 1.0, ALU.mult, ALU.add)
        ph.on("vector", mk_trig)
        ph.bar()
        ph.on("vector", lambda e: e.tensor_scalar(csn[:, 3:4], csn[:, 3:4], -2.0, None, ALU.mult))
        ph.bar()

        def rinit(e):
            e.memset(Rr[:, 0:1], 1.0)
            e.memset(Ri[:, 0:1], 0.0)
            e.tensor_copy(Rbr[:, :], csn[:, 1:2])
            e.tensor_copy(Rbi[:, :], csn[:, 3:4])
            e.memset(carr[:, :], 0.0)
            e.memset(cari[:, :], 0.0)
            e.memset(mbc[:, :], 1.0)
            e.memset(Bsr[:, :], 1.0)
            return e.memset(Bsi[:, :], 0.0)
        ph.on("vector", rinit)
        ph.bar()
        ph.on("vector", lambda e: e.tensor_scalar(mbc[:, :], mbc[:, :], mcol[:, 0:1], None, ALU.mult))
        ph.bar()
        n = 1
        while n < 512:
            m2 = min(n, 512 - n)

            def dbl(e, n=n, m2=m2):
                e.tensor_scalar(t1[:, 0:m2], Rr[:, 0:m2], Rbr[:, 0:1], None, ALU.mult)
                e.tensor_scalar(t2[:, 0:m2], Ri[:, 0:m2], Rbi[:, 0:1], None, ALU.mult)
                e.tensor_scalar(gsr[:, 0:m2], Rr[:, 0:m2], Rbi[:, 0:1], None, ALU.mult)
                return e.tensor_scalar(gsi[:, 0:m2], Ri[:, 0:m2], Rbr[:, 0:1], None, ALU.mult)
            ph.on("vector", dbl)
            ph.bar()

            def dbl2(e, n=n, m2=m2):
                e.tensor_sub(Rr[:, n:n + m2], t1[:, 0:m2], t2[:, 0:m2])
                return e.tensor_add(Ri[:, n:n + m2], gsr[:, 0:m2], gsi[:, 0:m2])
            ph.on("vector", dbl2)
            ph.bar()
            if 2 * n < 512 + 1:
                def sqb(e):
                    e.tensor_mul(tA[:, :], Rbr[:, :], Rbr[:, :])
                    e.tensor_mul(tB[:, :], Rbi[:, :], Rbi[:, :])
                    return e.tensor_mul(tC[:, :], Rbr[:, :], Rbi[:, :])
                ph.on("vector", sqb)
                ph.bar()

                def sqb2(e):
                    e.tensor_sub(Rbr[:, :], tA[:, :], tB[:, :])
                    return e.tensor_scalar(Rbi[:, :], tC[:, :], 2.0, None, ALU.mult)
                ph.on("vector", sqb2)
                ph.bar()
            n *= 2
        def setr5(e):
            e.tensor_copy(R5r[:, :], Rbr[:, :])
            return e.tensor_copy(R5i[:, :], Rbi[:, :])
        ph.on("vector", setr5)
        ph.bar()

        chs = [(0, 256)] + [(256 + i * 512, 512) for i in range(TLOC // 512)]
        for c0, W in chs:
            TT = W // 128
            halo = (c0 == 0)

            def ldx(e, c0=c0, W=W, TT=TT, halo=halo):
                ph.tag(e.dma_start(out=x_tm_t.ap().rearrange("p (a d) -> p a d", a=4)[:, 0:TT, :],
                                   in_=x_tme[c0:c0 + W, :].rearrange("(a p) d -> p a d", p=128)))
                if not halo:
                    ph.tag(e.dma_start(out=x_fm.ap().rearrange("p (a t) -> p a t", a=8)[:, :, 0:W],
                                       in_=x_fme.rearrange("(a p) t -> p a t", p=128)[:, :, c0:c0 + W]))
            ph.on("sync", ldx, ndma=1 if halo else 2)
            ph.bar()
            ln_tm_batch(ph, lambda i: x_tm_t[:, i * D:(i + 1) * D],
                        lambda i: n_bf[:, i * D:(i + 1) * D],
                        ln_sq, ln_m, ln_v, ln_r, TT, D)
            for dt in range(8):
                def trp(e, dt=dt, TT=TT):
                    r = None
                    for tt in range(TT):
                        r = e.transpose(pT[:, tt * 128:(tt + 1) * 128],
                                        n_bf[:, tt * D + dt * 128:dt * 128 + tt * D + 128],
                                        ident[:, :])
                    return r
                ph.on("tensor", trp)
                ph.bar()

                def evt(e, dt=dt, W=W):
                    return e.tensor_scalar(xn_fm[:, dt * 512:dt * 512 + W], pT[:, 0:W],
                                           n1g_c[:, dt:dt + 1], n1b_c[:, dt:dt + 1], ALU.mult, ALU.add)
                ph.on("vector", evt)
                ph.bar()

            def mmb(e, W=W):
                r = None
                for i in range(8):
                    r = e.matmul(pMM[:, 0:W], ipw_bf[:, i * 128:(i + 1) * 128],
                                 xn_fm[:, i * 512:i * 512 + W], start=(i == 0), stop=(i == 7))
                return r
            ph.on("tensor", mmb)
            ph.bar()
            if halo:
                def evb(e, W=W):
                    e.tensor_scalar(betar[:, 0:W], pMM[0:64, 0:W], hl_c[0:64, 0:1], None, ALU.mult)
                    return e.tensor_scalar(betai[:, 0:W], pMM[64:128, 0:W], hl_c[0:64, 0:1], None, ALU.mult)
            else:
                def evb(e, W=W):
                    e.tensor_copy(betar[:, 0:W], pMM[0:64, 0:W])
                    return e.tensor_copy(betai[:, 0:W], pMM[64:128, 0:W])
            ph.on("vector", evb)
            ph.bar()

            def mkrc(e, W=W):
                e.tensor_scalar(Rcr[:, 0:W], Rr[:, 0:W], Bsr[:, 0:1], None, ALU.mult)
                e.tensor_scalar(t1[:, 0:W], Ri[:, 0:W], Bsi[:, 0:1], None, ALU.mult)
                e.tensor_scalar(Rci[:, 0:W], Rr[:, 0:W], Bsi[:, 0:1], None, ALU.mult)
                return e.tensor_scalar(t2[:, 0:W], Ri[:, 0:W], Bsr[:, 0:1], None, ALU.mult)
            ph.on("vector", mkrc)
            ph.bar()

            def mkrc2(e, W=W):
                e.tensor_sub(Rcr[:, 0:W], Rcr[:, 0:W], t1[:, 0:W])
                return e.tensor_add(Rci[:, 0:W], Rci[:, 0:W], t2[:, 0:W])
            ph.on("vector", mkrc2)
            ph.bar()
            if W == 512:
                def adv(e):
                    e.tensor_mul(tA[:, :], Bsr[:, :], R5r[:, :])
                    e.tensor_mul(tB[:, :], Bsi[:, :], R5i[:, :])
                    e.tensor_mul(tC[:, :], Bsr[:, :], R5i[:, :])
                    return e.tensor_mul(csn[:, 2:3], Bsi[:, :], R5r[:, :])
                ph.on("vector", adv)
                ph.bar()

                def adv2(e):
                    e.tensor_sub(Bsr[:, :], tA[:, :], tB[:, :])
                    return e.tensor_add(Bsi[:, :], tC[:, :], csn[:, 2:3])
                ph.on("vector", adv2)
                ph.bar()
            else:
                def adv(e):
                    e.tensor_scalar(tA[:, :], Bsr[:, :], Rr[:, 256:257], None, ALU.mult)
                    e.tensor_scalar(tB[:, :], Bsi[:, :], Ri[:, 256:257], None, ALU.mult)
                    e.tensor_scalar(tC[:, :], Bsr[:, :], Ri[:, 256:257], None, ALU.mult)
                    return e.tensor_scalar(csn[:, 2:3], Bsi[:, :], Rr[:, 256:257], None, ALU.mult)
                ph.on("vector", adv)
                ph.bar()

                def adv2(e):
                    e.tensor_sub(Bsr[:, :], tA[:, :], tB[:, :])
                    return e.tensor_add(Bsi[:, :], tC[:, :], csn[:, 2:3])
                ph.on("vector", adv2)
                ph.bar()

            def mkbp(e, W=W):
                e.tensor_mul(bpr[:, 0:W], betar[:, 0:W], Rcr[:, 0:W])
                e.tensor_mul(t1[:, 0:W], betai[:, 0:W], Rci[:, 0:W])
                e.tensor_mul(bpi[:, 0:W], betar[:, 0:W], Rci[:, 0:W])
                return e.tensor_mul(t2[:, 0:W], betai[:, 0:W], Rcr[:, 0:W])
            ph.on("vector", mkbp)
            ph.bar()

            def mkbp2(e, W=W):
                e.tensor_sub(bpr[:, 0:W], bpr[:, 0:W], t1[:, 0:W])
                return e.tensor_add(bpi[:, 0:W], bpi[:, 0:W], t2[:, 0:W])
            ph.on("vector", mkbp2)
            ph.bar()

            def sc1(e, W=W):
                return e.tensor_tensor_scan(gsr[:, 0:W], mbc[:, 0:W], bpr[:, 0:W],
                                            carr[:, 0:1], ALU.mult, ALU.add)
            ph.on("vector", sc1)
            ph.bar()

            def sc2(e, W=W):
                return e.tensor_tensor_scan(gsi[:, 0:W], mbc[:, 0:W], bpi[:, 0:W],
                                            cari[:, 0:1], ALU.mult, ALU.add)
            ph.on("vector", sc2)
            ph.bar()

            def upcar(e, W=W):
                e.tensor_copy(carr[:, 0:1], gsr[:, W - 1:W])
                return e.tensor_copy(cari[:, 0:1], gsi[:, W - 1:W])
            ph.on("vector", upcar)
            ph.bar()

            if not halo:
                def mkc(e, W=W):
                    e.tensor_mul(t1[:, 0:W], gsr[:, 0:W], Rcr[:, 0:W])
                    e.tensor_mul(t2[:, 0:W], gsi[:, 0:W], Rci[:, 0:W])
                    e.tensor_mul(bpr[:, 0:W], gsi[:, 0:W], Rcr[:, 0:W])
                    return e.tensor_mul(bpi[:, 0:W], gsr[:, 0:W], Rci[:, 0:W])
                ph.on("vector", mkc)
                ph.bar()

                def mkc2(e, W=W):
                    e.tensor_add(cc[0:64, 0:W], t1[:, 0:W], t2[:, 0:W])
                    return e.tensor_sub(cc[64:128, 0:W], bpr[:, 0:W], bpi[:, 0:W])
                ph.on("vector", mkc2)
                ph.bar()

                def mmcoup(e, W=W):
                    return e.matmul(pMM[:, 0:W], bigc[:, :], cc[:, 0:W], start=True, stop=True)
                ph.on("tensor", mmcoup)
                ph.bar()
                ph.on("vector", lambda e, W=W: e.tensor_copy(eig_bf[:, 0:W], pMM[:, 0:W]))
                ph.bar()

                for ot in range(8):
                    def mmop(e, ot=ot, W=W):
                        return e.matmul(pMM[:, 0:W], opw_bf[:, ot * 128:(ot + 1) * 128],
                                        eig_bf[:, 0:W], start=True, stop=True)
                    ph.on("tensor", mmop)
                    ph.bar()
                    ph.on("vector", lambda e, ot=ot, W=W: e.tensor_copy(o_fm[:, ot * 512:ot * 512 + W], pMM[:, 0:W]))
                    ph.bar()

                for ot in range(8):
                    def mmg(e, ot=ot, W=W):
                        r = None
                        for i in range(8):
                            r = e.matmul(pMM[:, 0:W], gw_bf[:, i * D + ot * 128:i * D + ot * 128 + 128],
                                         xn_fm[:, i * 512:i * 512 + W], start=(i == 0), stop=(i == 7))
                        return r
                    ph.on("tensor", mmg)
                    ph.bar()

                    def evg(e, ot=ot, W=W):
                        return e.activation(x7_fm[:, ot * 512:ot * 512 + W], pMM[:, 0:W],
                                            AF.Sigmoid, bias=gb_c[:, ot:ot + 1])
                    ph.on("scalar", evg)
                    ph.bar()
                def mkx7(e, W=W):
                    r = None
                    for dt in range(8):
                        sl = slice(dt * 512, dt * 512 + W)
                        r = e.tensor_mul(x7_fm[:, sl], x7_fm[:, sl], o_fm[:, sl])
                    return r
                ph.on("vector", mkx7)
                ph.bar()

                def mkx72(e, W=W):
                    r = None
                    for dt in range(8):
                        sl = slice(dt * 512, dt * 512 + W)
                        r = e.tensor_add(x7_fm[:, sl], x7_fm[:, sl], x_fm[:, sl])
                    return r
                ph.on("vector", mkx72)
                ph.bar()

                def stx7(e, c0=c0, W=W):
                    ph.tag(e.dma_start(out=o_x7.rearrange("(a p) t -> p a t", p=128)[:, :, c0 - 256:c0 - 256 + W],
                                       in_=x7_fm.ap().rearrange("p (a t) -> p a t", a=8)[:, :, 0:W]))
                ph.on("sync", stx7, ndma=1)
                ph.bar()

        ph.emit(S)
    return nc


def l13a_inmaps(x6_fm_full, inp):
    coupT = cont(inp["L3_attn_coupling"]).transpose(0, 2, 1)
    coupT = np.ascontiguousarray(np.concatenate([coupT, coupT], axis=0))
    args = dict(
        ipwT=fmT(inp["L3_attn_in_proj_w"]),
        ldec=cont(inp["L3_attn_log_decay"]).reshape(64, 1),
        freq=cont(inp["L3_attn_frequency"]).reshape(64, 1),
        coupT=coupT,
        opwT=fmT(inp["L3_attn_out_proj_w"]),
        gwT=fmT(inp["L3_attn_gate_w"]),
        gb=col128(inp["L3_attn_gate_b"]),
        n1g=col128(inp["L3_attn_n1_g"]), n1b=col128(inp["L3_attn_n1_b"]),
    )
    ins = []
    for c in range(NCORES):
        b, q = c // 4, c % 4
        t0 = q * TLOC
        xb = x6_fm_full[b]
        ext = np.zeros((TE3, D), np.float32)
        lo = t0 - 256
        slo = max(lo, 0)
        ext[slo - lo:256] = xb[:, slo:t0].T
        ext[256:] = xb[:, t0:t0 + TLOC].T
        extf = np.zeros((D, TE3), np.float32)
        extf[:, slo - lo:256] = xb[:, slo:t0]
        extf[:, 256:] = xb[:, t0:t0 + TLOC]
        m = dict(args)
        m["x_tme"] = ext
        m["x_fme"] = extf
        m["hlv"] = np.full((128, 1), 0.0 if q == 0 else 1.0, np.float32)
        ins.append(m)
    return ins


def build_l13b():
    nc = bass.Bass("TRN2", target_bir_lowering=False, num_devices=NCORES)
    x7_tm = nc.dram_tensor("x7_tm", [TLOC, D], F32, kind="ExternalInput").ap()
    x6_tm = nc.dram_tensor("x6_tm", [TLOC, D], F32, kind="ExternalInput").ap()
    n2g = nc.dram_tensor("n2g", [128, 8], F32, kind="ExternalInput").ap()
    n2b = nc.dram_tensor("n2b", [128, 8], F32, kind="ExternalInput").ap()
    w1T = nc.dram_tensor("w1T", [D, 4 * D], F32, kind="ExternalInput").ap()
    b1c = nc.dram_tensor("b1c", [128, 32], F32, kind="ExternalInput").ap()
    w2n = nc.dram_tensor("w2n", [4 * D, D], F32, kind="ExternalInput").ap()
    b2r = nc.dram_tensor("b2r", [1, D], F32, kind="ExternalInput").ap()
    wr8 = nc.dram_tensor("wr8", [D, 8], F32, kind="ExternalInput").ap()
    br8 = nc.dram_tensor("br8", [1, 8], F32, kind="ExternalInput").ap()
    out = nc.dram_tensor("y_tm", [TLOC, D], F32, kind="ExternalOutput").ap()
    o_lg8 = nc.dram_tensor("lg8", [8, TLOC], F32, kind="ExternalOutput").ap()

    with contextlib.ExitStack() as es:
        S = es.enter_context
        w1_bf = S(nc.sbuf_tensor("w1b", [128, 8 * 4 * D], BF16))
        w2_bf = S(nc.sbuf_tensor("w2b", [128, 32 * D], BF16))
        wr_bf = S(nc.sbuf_tensor("wrb", [128, 8 * 8], BF16))
        br_row = S(nc.sbuf_tensor("brr", [1, 8], BF16))
        b2_row = S(nc.sbuf_tensor("b2rw", [1, D], BF16))
        b1_c = S(nc.sbuf_tensor("b1cc", [128, 32], F32))
        n2g_c = S(nc.sbuf_tensor("n2gc", [128, 8], F32))
        n2b_c = S(nc.sbuf_tensor("n2bc", [128, 8], F32))
        onesr = S(nc.sbuf_tensor("onesrB", [1, 512], BF16))
        x7t = S(nc.sbuf_tensor("x7t", [128, 4 * D], F32))
        res6 = S(nc.sbuf_tensor("res6", [128, 4 * D], F32))
        n2_bf = S(nc.sbuf_tensor("n2bf", [128, 4 * D], BF16))
        n2_fm = S(nc.sbuf_tensor("n2fm", [128, 8 * 512], BF16))
        mid_bf = S(nc.sbuf_tensor("midb", [128, 32 * 128], BF16))
        o_tm = S(nc.sbuf_tensor("otmB", [128, 4 * D], F32))
        lg8_sb = S(nc.sbuf_tensor("lg8sb", [8, 512], F32))
        ident_f = S(nc.sbuf_tensor("idfB", [128, 128], F32))
        ident = S(nc.sbuf_tensor("idB", [128, 128], BF16))
        ln_m = S(nc.sbuf_tensor("lnmB", [128, 4], F32))
        ln_v = S(nc.sbuf_tensor("lnvB", [128, 4], F32))
        ln_r = S(nc.sbuf_tensor("lnrB", [128, 4], F32))
        pMM = S(nc.psum_tensor("pMMB", [128, 512], F32))
        pY = S(nc.psum_tensor("pYB", [128, 1024], F32))
        pT = S(nc.psum_tensor("pTB", [128, 512], BF16))

        ph = Ph(nc, S)

        def ldw(e):
            ph.tag(e.dma_start(out=w1_bf.ap().rearrange("p (a o) -> p a o", a=8),
                               in_=w1T.rearrange("(a p) o -> p a o", p=128)))
            ph.tag(e.dma_start(out=w2_bf.ap().rearrange("p (a o) -> p a o", a=32),
                               in_=w2n.rearrange("(a p) o -> p a o", p=128)))
            ph.tag(e.dma_start(out=wr_bf.ap().rearrange("p (a o) -> p a o", a=8),
                               in_=wr8.rearrange("(a p) o -> p a o", p=128)))
            ph.tag(e.dma_start(out=br_row[:, :], in_=br8))
            ph.tag(e.dma_start(out=b2_row[:, :], in_=b2r))
        ph.on("gpsimd", ldw, ndma=5)

        def ldc(e):
            for dst, srcd in ((b1_c, b1c), (n2g_c, n2g), (n2b_c, n2b)):
                ph.tag(e.dma_start(out=dst[:, :], in_=srcd))
        ph.on("sync", ldc, ndma=3)

        def mkid(e):
            return e.iota(ident_f[:, :], pattern=[[-1, 128]], base=0,
                          channel_multiplier=1, allow_small_or_imprecise_dtypes=True)
        ph.on("gpsimd", mkid)
        ph.on("vector", lambda e: e.memset(onesr[:, :], 1.0))
        ph.bar()
        ph.on("vector", lambda e: e.tensor_scalar(ident_f[:, :], ident_f[:, :], 0.0, None, ALU.is_equal))
        ph.bar()
        ph.on("vector", lambda e: e.tensor_copy(ident[:, :], ident_f[:, :]))
        ph.bar()

        CH = 512
        for ci in range(TLOC // CH):
            c0 = ci * CH

            def ldx(e, c0=c0):
                ph.tag(e.dma_start(out=x7t.ap().rearrange("p (a d) -> p a d", a=4),
                                   in_=x7_tm[c0:c0 + CH, :].rearrange("(a p) d -> p a d", p=128)))
                ph.tag(e.dma_start(out=res6.ap().rearrange("p (a d) -> p a d", a=4),
                                   in_=x6_tm[c0:c0 + CH, :].rearrange("(a p) d -> p a d", p=128)))
            ph.on("sync", ldx, ndma=2)
            ph.bar()
            ln_tm_batch(ph, lambda i: x7t[:, i * D:(i + 1) * D],
                        lambda i: n2_bf[:, i * D:(i + 1) * D],
                        o_tm, ln_m, ln_v, ln_r, 4, D)
            for dt in range(8):
                def trp(e, dt=dt):
                    r = None
                    for tt in range(4):
                        r = e.transpose(pT[:, tt * 128:(tt + 1) * 128],
                                        n2_bf[:, tt * D + dt * 128:tt * D + dt * 128 + 128],
                                        ident[:, :])
                    return r
                ph.on("tensor", trp)
                ph.bar()

                def evt(e, dt=dt):
                    return e.tensor_scalar(n2_fm[:, dt * 512:(dt + 1) * 512], pT[:, :],
                                           n2g_c[:, dt:dt + 1], n2b_c[:, dt:dt + 1], ALU.mult, ALU.add)
                ph.on("vector", evt)
                ph.bar()

            for tt in range(4):
                for ft in range(32):
                    def mm1(e, tt=tt, ft=ft):
                        r = None
                        for i in range(8):
                            r = e.matmul(pMM[:, 0:128], w1_bf[:, i * 4 * D + ft * 128:i * 4 * D + ft * 128 + 128],
                                         n2_fm[:, i * 512 + tt * 128:i * 512 + tt * 128 + 128],
                                         start=(i == 0), stop=(i == 7))
                        return r
                    ph.on("tensor", mm1)
                    ph.bar()

                    def ev1(e, ft=ft):
                        return e.activation(mid_bf[:, ft * 128:(ft + 1) * 128], pMM[:, 0:128],
                                            AF.Silu, bias=b1_c[:, ft:ft + 1])
                    ph.on("scalar", ev1)
                    ph.bar()

                def mm2(e, tt=tt):
                    r = None
                    for half in range(2):
                        psl = pY[:, half * 512:(half + 1) * 512]
                        for i in range(32):
                            e.matmul(psl, mid_bf[:, i * 128:(i + 1) * 128],
                                     w2_bf[:, i * D + half * 512:i * D + half * 512 + 512],
                                     start=(i == 0), stop=False)
                        r = e.matmul(psl, onesr[:, 0:128], b2_row[:, half * 512:(half + 1) * 512],
                                     start=False, stop=True)
                    return r
                ph.on("tensor", mm2)
                ph.bar()

                def evy(e, tt=tt):
                    return e.tensor_add(o_tm[:, tt * D:(tt + 1) * D], x7t[:, tt * D:(tt + 1) * D], pY[:, :])
                ph.on("vector", evy)
                ph.bar()
            ph.on("vector", lambda e: e.tensor_add(o_tm[:, :], o_tm[:, :], res6[:, :]))
            ph.bar()
            ln_tm_batch(ph, lambda i: o_tm[:, i * D:(i + 1) * D], lambda i: o_tm[:, i * D:(i + 1) * D],
                        x7t, ln_m, ln_v, ln_r, 4, D)

            def store(e, c0=c0):
                ph.tag(e.dma_start(out=out[c0:c0 + CH, :].rearrange("(a p) d -> p a d", p=128),
                                   in_=o_tm.ap().rearrange("p (a d) -> p a d", a=4)))
            ph.on("sync", store, ndma=1)
            ph.on("vector", lambda e: e.tensor_copy(n2_bf[:, :], o_tm[:, :]))
            ph.bar()

            for dt in range(8):
                def trp8(e, dt=dt):
                    r = None
                    for tt in range(4):
                        r = e.transpose(pT[:, tt * 128:(tt + 1) * 128],
                                        n2_bf[:, tt * D + dt * 128:tt * D + dt * 128 + 128],
                                        ident[:, :])
                    return r
                ph.on("tensor", trp8)
                ph.bar()
                ph.on("vector", lambda e, dt=dt: e.tensor_copy(n2_fm[:, dt * 512:(dt + 1) * 512], pT[:, :]))
                ph.bar()

            def mmlg(e):
                r = None
                for i in range(8):
                    e.matmul(pMM[0:8, :], wr_bf[:, i * 8:(i + 1) * 8],
                             n2_fm[:, i * 512:(i + 1) * 512], start=(i == 0), stop=False)
                r = e.matmul(pMM[0:8, :], onesr[0:1, 0:8].rearrange("a b -> a b") if False else br_row[:, :],
                             onesr[0:1, 0:512], start=False, stop=True)
                return r
            ph.on("tensor", mmlg)
            ph.bar()
            ph.on("vector", lambda e: e.tensor_copy(lg8_sb[:, :], pMM[0:8, :]))
            ph.bar()

            def stlg(e, c0=c0):
                ph.tag(e.dma_start(out=o_lg8[:, c0:c0 + CH], in_=lg8_sb[:, :]))
            ph.on("sync", stlg, ndma=1)
            ph.bar()

        ph.emit(S)
    return nc


def l13b_inmaps(x7_fm_full, x6_fm_full, inp):
    args = dict(
        n2g=col128(inp["L3_attn_n2_g"]), n2b=col128(inp["L3_attn_n2_b"]),
        w1T=fmT(inp["L3_attn_mlp_w1"]),
        b1c=col128(inp["L3_attn_mlp_b1"]),
        w2n=fmT(inp["L3_attn_mlp_w2"]),
        b2r=cont(inp["L3_attn_mlp_b2"]).reshape(1, D),
        wr8=cont(inp["L3_ffn_Wr"]),
        br8=cont(inp["L3_ffn_br"]).reshape(1, 8),
    )
    ins = []
    for c in range(NCORES):
        b, q = c // 4, c % 4
        m = dict(args)
        m["x7_tm"] = cont(x7_fm_full[b][:, q * TLOC:(q + 1) * TLOC].T)
        m["x6_tm"] = cont(x6_fm_full[b][:, q * TLOC:(q + 1) * TLOC].T)
        ins.append(m)
    return ins


L3F_ROWS = 9 * 512 * NCORES


def build_l14():
    nc = bass.Bass("TRN2", target_bir_lowering=False, num_devices=NCORES)
    TTK = NB * T
    NCH = TTK // 512
    lg = nc.dram_tensor("lg8", [8, TTK], F32, kind="ExternalInput").ap()
    x_tm = nc.dram_tensor("x8_tm", [TTK, D], F32, kind="ExternalInput").ap()
    w1d = nc.dram_tensor("w1d", [8 * D, 4 * D], F32, kind="ExternalInput").ap()
    b1d = nc.dram_tensor("b1d", [8, 4 * D], F32, kind="ExternalInput").ap()
    w2d = nc.dram_tensor("w2d", [8 * 4 * D, D], F32, kind="ExternalInput").ap()
    b2d = nc.dram_tensor("b2d", [8, D], F32, kind="ExternalInput").ap()
    eoff3 = nc.dram_tensor("eoff3", [1, 1], F32, kind="ExternalInput").ap()
    ybuf = nc.dram_tensor("ybuf3_sl", [9 * 512, D], BF16, kind="ExternalOutput").ap()
    o_slotc = nc.dram_tensor("slotc3", [128, 256], U32, kind="ExternalOutput").ap()
    iv2 = nc.dram_tensor("iv2", [9 * 512 + 128, 4], U32).ap()
    gate4 = nc.dram_tensor("gate4", [TTK, 4], F32).ap()

    with contextlib.ExitStack() as es:
        S = es.enter_context
        lgT = S(nc.sbuf_tensor("lgTF", [128, 32], F32))
        mx8 = S(nc.sbuf_tensor("mx8F", [128, 8], F32))
        zeq = S(nc.sbuf_tensor("zeqF", [128, 8], F32))
        zin = S(nc.sbuf_tensor("zinF", [128, 8], F32))
        ones8r = S(nc.sbuf_tensor("ones8rF", [128, 8], F32))
        iota8r = S(nc.sbuf_tensor("iota8rF", [128, 8], F32))
        ecol = S(nc.sbuf_tensor("ecolF", [128, 4], F32))
        gcol = S(nc.sbuf_tensor("gcolF", [128, 4], F32))
        den = S(nc.sbuf_tensor("denF", [128, 4], F32))
        et8 = S(nc.sbuf_tensor("et8F", [128, 8], F32))
        car8 = S(nc.sbuf_tensor("car8F", [8, 1], F32))
        ones8c = S(nc.sbuf_tensor("ones8cF", [8, 512], F32))
        one8col = S(nc.sbuf_tensor("one8colF", [8, 1], F32))
        posc = S(nc.sbuf_tensor("poscF", [128, 256], F32))
        ecc = S(nc.sbuf_tensor("eccF", [128, 256], F32))
        gatec = S(nc.sbuf_tensor("gatecF", [128, 256], F32))
        slotf = S(nc.sbuf_tensor("slotfF", [128, 256], F32))
        slocf = S(nc.sbuf_tensor("slocfF", [128, 256], F32))
        slotu = S(nc.sbuf_tensor("slotuF", [128, 256], U32))
        slocu = S(nc.sbuf_tensor("slocuF", [128, 256], U32))
        mskc = S(nc.sbuf_tensor("mskcF", [128, 256], U8))
        trashc = S(nc.sbuf_tensor("trashcF", [128, 256], F32))
        iotaS = S(nc.sbuf_tensor("iotaSF", [8, 64], F32))
        nco = S(nc.sbuf_tensor("ncoF", [8, 4], F32))
        orow = S(nc.sbuf_tensor("orowF", [1, 8], F32))
        prow = S(nc.sbuf_tensor("prowF", [1, 8], F32))
        obc = S(nc.sbuf_tensor("obcF", [128, 8], F32))
        i72 = S(nc.sbuf_tensor("i72F", [8, 9], F32))
        ecloc = S(nc.sbuf_tensor("eclocF", [1, 9], F32))
        eoffc = S(nc.sbuf_tensor("eoffcF", [1, 1], F32))
        eoffb = S(nc.sbuf_tensor("eoffbF", [128, 1], F32))
        ids4 = S(nc.sbuf_tensor("ids4F", [128, 4 * 256], U32))
        iotac8 = S(nc.sbuf_tensor("iotac8F", [128, 8], F32))
        iotac32 = S(nc.sbuf_tensor("iotac32F", [128, 32], F32))
        ecb = S(nc.sbuf_tensor("ecbF", [128, 1], F32))
        idxw = S(nc.sbuf_tensor("idxwF", [128, 8], F32))
        idxw2 = S(nc.sbuf_tensor("idxw2F", [128, 32], F32))
        idxwu = S(nc.sbuf_tensor("idxwuF", [128, 8], U32))
        idxw2u = S(nc.sbuf_tensor("idxw2uF", [128, 32], U32))
        oh8c = S(nc.sbuf_tensor("oh8cF", [8, 1], F32))
        b2row = S(nc.sbuf_tensor("b2rowF", [1, D], BF16))
        b1cols = S(nc.sbuf_tensor("b1colsF", [128, 32], F32))
        ivcols = S(nc.sbuf_tensor("ivcolsF", [128, 16], U32))
        xg_tm = S(nc.sbuf_tensor("xg_tmF", [128, 4 * D], F32))
        lgch = xg_tm[0:8, 0:512]
        oh8 = xg_tm[0:8, 512:1024]
        rk8 = xg_tm[0:8, 1024:1536]
        posr = xg_tm[0:1, 1536:2048]
        x_fm = S(nc.sbuf_tensor("x_fmF", [128, 8 * 512], BF16))
        w1g = S(nc.sbuf_tensor("w1gF", [128, 8 * 4 * D], BF16))
        w2g = S(nc.sbuf_tensor("w2gF", [128, 32 * D], BF16))
        h_bf = S(nc.sbuf_tensor("h_bfF", [128, 32 * 512], BF16))
        y_tmo = S(nc.sbuf_tensor("y_tmoF", [128, D], BF16))
        ggt = S(nc.sbuf_tensor("ggtF", [128, 16], F32))
        onesr = S(nc.sbuf_tensor("onesrF", [1, 512], BF16))
        zerou = S(nc.sbuf_tensor("zerouF", [128, 37 * 4], U32))
        ident_f = S(nc.sbuf_tensor("idfF", [128, 128], F32))
        identb = S(nc.sbuf_tensor("idbF", [128, 128], BF16))
        pMM = S(nc.psum_tensor("pMMF", [128, 512], F32))
        pT = S(nc.psum_tensor("pTF", [128, 512], F32))
        pY = S(nc.psum_tensor("pYF", [128, 1024], F32))
        pTb = S(nc.psum_tensor("pTbF", [128, 512], BF16))

        ph = Ph(nc, S)

        def setup_io(e):
            e.iota(iota8r[:, :], pattern=[[1, 8]], base=0, channel_multiplier=0,
                   allow_small_or_imprecise_dtypes=True)
            e.iota(ids4[:, :], pattern=[[128, 256], [0, 4]], base=0, channel_multiplier=1)
            e.iota(trashc[:, :], pattern=[[0, 256]], base=9 * 512, channel_multiplier=1,
                   allow_small_or_imprecise_dtypes=True)
            e.iota(iotaS[:, :], pattern=[[512, 64]], base=512, channel_multiplier=0,
                   allow_small_or_imprecise_dtypes=True)
            e.iota(i72[:, :], pattern=[[512, 9]], base=0, channel_multiplier=0,
                   allow_small_or_imprecise_dtypes=True)
            e.iota(iotac8[:, :], pattern=[[128, 8]], base=0, channel_multiplier=1,
                   allow_small_or_imprecise_dtypes=True)
            e.iota(iotac32[:, :], pattern=[[128, 32]], base=0, channel_multiplier=1,
                   allow_small_or_imprecise_dtypes=True)
            return e.iota(ident_f[:, :], pattern=[[-1, 128]], base=0, channel_multiplier=1,
                          allow_small_or_imprecise_dtypes=True)
        ph.on("gpsimd", setup_io)

        def setup_ms(e):
            e.memset(ones8r[:, :], 1.0)
            e.memset(ones8c[:, :], 1.0)
            e.memset(one8col[:, :], 1.0)
            e.memset(car8[:, :], 0.0)
            e.memset(onesr[:, :], 1.0)
            return e.memset(zerou[:, :], 0)
        ph.on("vector", setup_ms)

        def ld0(e):
            ph.tag(e.dma_start(out=eoffc[:, :], in_=eoff3))
        ph.on("sync", ld0, ndma=1)
        ph.bar()
        ph.on("vector", lambda e: e.tensor_scalar(ident_f[:, :], ident_f[:, :], 0.0, None, ALU.is_equal))
        ph.bar()
        ph.on("vector", lambda e: e.tensor_copy(identb[:, :], ident_f[:, :]))
        ph.bar()

        def iv_init(e):
            ph.tag(e.dma_start(out=iv2.rearrange("(p a) c -> p (a c)", p=128), in_=zerou[:, :]))
        ph.on("sync", iv_init, ndma=1)
        ph.bar()

        for ci in range(NCH):
            c0 = ci * 512

            def ldlg(e, c0=c0):
                ph.tag(e.dma_start(out=lgch[:, :], in_=lg[:, c0:c0 + 512]))
            ph.on("sync", ldlg, ndma=1)
            ph.bar()

            def trlg(e):
                r = None
                for tt in range(4):
                    r = e.transpose(pT[:, tt * 8:(tt + 1) * 8],
                                    lgch[:, tt * 128:(tt + 1) * 128], ident_f[0:8, 0:8])
                return r
            ph.on("tensor", trlg)
            ph.bar()
            ph.on("vector", lambda e: e.tensor_copy(lgT[:, :], pT[:, 0:32]))
            ph.bar()
            for tt in range(4):
                sl8 = slice(tt * 8, (tt + 1) * 8)
                ph.on("vector", lambda e, sl8=sl8: e.max(mx8[:, :], lgT[:, sl8]))
                ph.bar()
                ph.on("vector", lambda e, sl8=sl8: e.tensor_scalar(zeq[:, :], lgT[:, sl8],
                                                                   mx8[:, 0:1], None, ALU.is_equal))
                ph.bar()
                ph.on("vector", lambda e: e.tensor_tensor_scan(zin[:, :], ones8r[:, :], zeq[:, :],
                                                               0.0, ALU.mult, ALU.add))
                ph.bar()
                ph.on("vector", lambda e: e.tensor_scalar(zin[:, :], zin[:, :], 1.0, None, ALU.is_equal))
                ph.bar()
                ph.on("vector", lambda e: e.tensor_mul(zeq[:, :], zeq[:, :], zin[:, :]))
                ph.bar()
                def troh(e, tt=tt):
                    return e.transpose(pT[0:8, 256 + tt * 128:256 + (tt + 1) * 128] if False
                                       else pT[0:8, tt * 128:(tt + 1) * 128],
                                       zeq[:, :], ident_f[:, :])
                ph.on("tensor", troh)
                ph.bar()
                ph.on("vector", lambda e, tt=tt: e.tensor_mul(et8[:, :], zeq[:, :], iota8r[:, :]))
                ph.bar()
                ph.on("vector", lambda e, tt=tt: e.reduce_sum(ecol[:, tt:tt + 1], et8[:, :], axis=AX.X))
                ph.on("vector", lambda e, tt=tt: e.tensor_copy(den[:, tt:tt + 1], mx8[:, 0:1]))
                ph.bar()

                def ex(e, tt=tt, sl8=sl8):
                    return e.tensor_scalar(et8[:, :], lgT[:, sl8], den[:, tt:tt + 1], None, ALU.subtract)
                ph.on("vector", ex)
                ph.bar()
                ph.on("scalar", lambda e: e.activation(et8[:, :], et8[:, :], AF.Exp))
                ph.bar()
                ph.on("vector", lambda e, tt=tt: e.reduce_sum(den[:, tt:tt + 1], et8[:, :], axis=AX.X))
                ph.bar()
            ph.on("vector", lambda e: e.reciprocal(gcol[:, :], den[:, :]))
            ph.bar()
            ph.on("vector", lambda e: e.tensor_copy(oh8[:, :], pT[0:8, 0:512]))
            ph.bar()
            ph.on("vector", lambda e: e.tensor_tensor_scan(rk8[:, :], ones8c[:, :], oh8[:, :],
                                                           car8[:, 0:1], ALU.mult, ALU.add))
            ph.bar()
            ph.on("vector", lambda e: e.tensor_copy(car8[:, 0:1], rk8[:, 511:512]))
            ph.on("vector", lambda e: e.tensor_mul(oh8[:, :], oh8[:, :], rk8[:, :]))
            ph.bar()
            ph.on("tensor", lambda e: e.matmul(pMM[0:1, :], one8col[:, :], oh8[:, :], start=True, stop=True))
            ph.bar()
            ph.on("vector", lambda e: e.tensor_scalar(posr[:, :], pMM[0:1, :], 1.0, None, ALU.subtract))
            ph.bar()

            def trcols(e):
                r = None
                for tt in range(4):
                    r = e.transpose(pT[:, tt:tt + 1], posr[0:1, tt * 128:(tt + 1) * 128], ident_f[0:1, 0:1])
                return r
            ph.on("tensor", trcols)
            ph.bar()

            def evcols(e, ci=ci):
                e.tensor_copy(posc[:, ci * 4:(ci + 1) * 4], pT[:, 0:4])
                e.tensor_copy(ecc[:, ci * 4:(ci + 1) * 4], ecol[:, :])
                return e.tensor_copy(gatec[:, ci * 4:(ci + 1) * 4], gcol[:, :])
            ph.on("vector", evcols)
            ph.bar()

        ph.on("vector", lambda e: e.tensor_copy(nco[:, 0:1], car8[:, 0:1]))
        ph.bar()
        ph.on("vector", lambda e: e.tensor_scalar(iotaS[:, :], iotaS[:, :], nco[:, 0:1], None, ALU.is_le))
        ph.bar()
        ph.on("vector", lambda e: e.reduce_sum(nco[:, 1:2], iotaS[:, :], axis=AX.X))
        ph.bar()

        def mkpad(e):
            e.scalar_tensor_tensor(nco[:, 2:3], nco[:, 1:2], -512.0, nco[:, 0:1], ALU.mult, ALU.add)
            return None
        def mkpad1(e):
            return e.scalar_tensor_tensor(nco[:, 2:3], nco[:, 1:2], -512.0, nco[:, 0:1], ALU.mult, ALU.add)
        ph.on("vector", mkpad1)
        ph.bar()
        ph.on("vector", lambda e: e.tensor_scalar(nco[:, 2:3], nco[:, 2:3], 0.0, None, ALU.is_gt))
        ph.bar()
        ph.on("vector", lambda e: e.tensor_add(nco[:, 3:4], nco[:, 1:2], nco[:, 2:3]))
        ph.bar()
        ph.on("vector", lambda e: e.tensor_scalar(nco[:, 3:4], nco[:, 3:4], 512.0, None, ALU.mult))
        ph.bar()
        ph.on("tensor", lambda e: e.transpose(pT[0:1, 0:8], nco[:, 3:4], ident_f[0:8, 0:8]))
        ph.bar()
        ph.on("vector", lambda e: e.tensor_copy(prow[:, :], pT[0:1, 0:8]))
        ph.bar()
        ph.on("vector", lambda e: e.tensor_tensor_scan(orow[:, :], ones8c[0:1, 0:8], prow[:, :],
                                                       0.0, ALU.mult, ALU.add))
        ph.bar()
        ph.on("vector", lambda e: e.tensor_sub(orow[:, :], orow[:, :], prow[:, :]))
        ph.bar()
        ph.on("tensor", lambda e: e.matmul(pT[:, 0:8], ones8c[0:1, 0:128], orow[:, :], start=True, stop=True))
        ph.bar()
        ph.on("vector", lambda e: e.tensor_copy(obc[:, :], pT[:, 0:8]))
        ph.bar()
        ph.on("vector", lambda e: e.tensor_copy(slotf[:, :], posc[:, :]))
        ph.bar()
        for ex in range(8):
            def addo(e, ex=ex):
                e.tensor_scalar(slocf[:, :], ecc[:, :], float(ex), None, ALU.is_equal)
                return None
            def addo1(e, ex=ex):
                return e.tensor_scalar(slocf[:, :], ecc[:, :], float(ex), None, ALU.is_equal)
            ph.on("vector", addo1)
            ph.bar()
            ph.on("vector", lambda e, ex=ex: e.scalar_tensor_tensor(slotf[:, :], slocf[:, :],
                                                                    obc[0:128, ex:ex + 1], slotf[:, :],
                                                                    ALU.mult, ALU.add))
            ph.bar()
        ph.on("vector", lambda e: e.tensor_copy(slotu[:, :], slotf[:, :]))
        ph.bar()

        def stslot(e):
            ph.tag(e.dma_start(out=o_slotc, in_=slotu[:, :]))
        ph.on("sync", stslot, ndma=1)
        ph.bar()
        ph.on("tensor", lambda e: e.matmul(pT[:, 100:101], ones8c[0:1, 0:128], eoffc[0:1, 0:1],
                                           start=True, stop=True))
        ph.bar()
        ph.on("vector", lambda e: e.tensor_copy(eoffb[:, :], pT[:, 100:101]))
        ph.bar()
        ph.on("vector", lambda e: e.tensor_scalar(slocf[:, :], slotf[:, :], eoffb[:, 0:1], None, ALU.subtract))
        ph.bar()
        ph.on("vector", lambda e: e.tensor_scalar(mskc[:, :], slocf[:, :], 0.0, None, ALU.is_ge))
        ph.bar()
        ph.on("vector", lambda e: e.select(slotf[:, :], mskc[:, :], slocf[:, :], trashc[:, :]))
        ph.bar()
        ph.on("vector", lambda e: e.tensor_scalar(mskc[:, :], slotf[:, :], float(9 * 512), None, ALU.is_lt))
        ph.bar()
        ph.on("vector", lambda e: e.select(slocf[:, :], mskc[:, :], slotf[:, :], trashc[:, :]))
        ph.bar()
        ph.on("vector", lambda e: e.tensor_copy(slocu[:, :], slocf[:, :]))
        ph.bar()

        for g0 in range(0, 256, 64):
            def sct_b(e, g0=g0):
                for g in range(g0, g0 + 64):
                    ph.tag(e.indirect_dma_start(
                        out=iv2, out_offset=bass.IndirectOffsetOnAxis(ap=slocu[:, g:g + 1], axis=0),
                        in_=ids4[:, g * 4:(g + 1) * 4], in_offset=None))
            ph.on("gpsimd", sct_b, ndma=64)
            ph.bar()

        def repg(e):
            r = None
            for i in range(4):
                r = e.tensor_copy(xg_tm.ap().rearrange("p (a c) -> p a c", c=4)[:, 0:256, i:i + 1],
                                  gatec[:, :].rearrange("p (a c) -> p a c", c=1))
            return r
        ph.on("vector", repg)
        ph.bar()

        def stg4(e):
            ph.tag(e.dma_start(out=gate4.rearrange("(a p) c -> p a c", p=128),
                               in_=xg_tm[:, 0:1024].rearrange("p (a c) -> p a c", c=4)))
        ph.on("sync", stg4, ndma=1)
        ph.bar()

        ph.on("vector", lambda e: e.tensor_scalar(i72[:, :], i72[:, :], eoffb[0:8, 0:1], None, ALU.add))
        ph.bar()
        ph.on("tensor", lambda e: e.transpose(pT[0:8, 8:9], orow[0:1, 0:8], ident_f[0:1, 0:1]))
        ph.bar()
        ph.on("vector", lambda e: e.tensor_copy(nco[0:8, 0:1], pT[0:8, 8:9]))
        ph.bar()
        ph.on("vector", lambda e: e.tensor_scalar(i72[:, :], i72[:, :], nco[0:8, 0:1], None, ALU.is_ge))
        ph.bar()
        ph.on("tensor", lambda e: e.matmul(pMM[0:1, 0:9], one8col[:, :], i72[:, :], start=True, stop=True))
        ph.bar()
        ph.on("vector", lambda e: e.tensor_scalar(ecloc[:, :], pMM[0:1, 0:9], 1.0, None, ALU.subtract))
        ph.bar()

        for lc in range(9):
            r0 = lc * 512

            def ecb_mm(e, lc=lc):
                return e.matmul(pT[:, 100:101], ones8c[0:1, 0:128], ecloc[0:1, lc:lc + 1],
                                start=True, stop=True)
            ph.on("tensor", ecb_mm)
            ph.bar()
            ph.on("vector", lambda e: e.tensor_copy(ecb[:, :], pT[:, 100:101]))
            ph.bar()
            ph.on("vector", lambda e: e.tensor_scalar(oh8c[:, :], iota8r[0:8, 0:1], ecb[0:8, 0:1],
                                                      None, ALU.is_equal))
            ph.bar()

            def mkidx0(e):
                e.tensor_scalar(ggt[:, 8:9], ecb[:, 0:1], 1024.0, None, ALU.mult)
                return e.tensor_scalar(ggt[:, 9:10], ecb[:, 0:1], 4096.0, None, ALU.mult)
            ph.on("vector", mkidx0)
            ph.bar()

            def mkidx1(e):
                e.tensor_scalar(idxw[:, :], iotac8[:, :], ggt[:, 8:9], None, ALU.add)
                return e.tensor_scalar(idxw2[:, :], iotac32[:, :], ggt[:, 9:10], None, ALU.add)
            ph.on("vector", mkidx1)
            ph.bar()

            def mkidx3(e):
                e.tensor_copy(idxwu[:, :], idxw[:, :])
                return e.tensor_copy(idxw2u[:, :], idxw2[:, :])
            ph.on("vector", mkidx3)
            ph.bar()

            b1_sb = xg_tm[0:8, 0:4 * D]
            b1row = xg_tm[0:1, 0:4 * D]

            def ldb1(e):
                ph.tag(e.dma_start(out=b1_sb, in_=b1d))
            ph.on("sync", ldb1, ndma=1)
            ph.bar()
            for pc in range(8):
                def mmb1p(e, pc=pc):
                    return e.matmul(pMM[0:1, :], oh8c[:, :], b1_sb[:, pc * 512:(pc + 1) * 512],
                                    start=True, stop=True)
                ph.on("tensor", mmb1p)
                ph.bar()
                ph.on("vector", lambda e, pc=pc: e.tensor_copy(b1row[:, pc * 512:(pc + 1) * 512], pMM[0:1, :]))
                ph.bar()

            def trb1(e):
                r = None
                for j in range(32):
                    r = e.transpose(pT[:, j:j + 1], b1row[:, j * 128:(j + 1) * 128], ident_f[0:1, 0:1])
                return r
            ph.on("tensor", trb1)
            ph.bar()
            ph.on("vector", lambda e: e.tensor_copy(b1cols[:, :], pT[:, 0:32]))
            ph.bar()
            b2_sb = xg_tm[0:8, 0:D]

            def ldb2(e):
                ph.tag(e.dma_start(out=b2_sb, in_=b2d))
            ph.on("sync", ldb2, ndma=1)
            ph.bar()
            for pc in range(2):
                def mmb2p(e, pc=pc):
                    return e.matmul(pMM[0:1, :], oh8c[:, :], b2_sb[:, pc * 512:(pc + 1) * 512],
                                    start=True, stop=True)
                ph.on("tensor", mmb2p)
                ph.bar()
                ph.on("vector", lambda e, pc=pc: e.tensor_copy(b2row[:, pc * 512:(pc + 1) * 512], pMM[0:1, :]))
                ph.bar()

            def ldiv(e, r0=r0):
                ph.tag(e.dma_start(out=ivcols[:, :],
                                   in_=iv2.rearrange("(r a) c -> r (a c)", a=4)[r0 // 4:r0 // 4 + 128, :]))
            ph.on("sync", ldiv, ndma=1)
            ph.bar()

            def gat(e):
                for a in range(4):
                    ph.tag(e.indirect_dma_start(
                        out=xg_tm[:, a * D:(a + 1) * D], out_offset=None,
                        in_=x_tm, in_offset=bass.IndirectOffsetOnAxis(ap=ivcols[:, a * 4:a * 4 + 1], axis=0)))
                for a in range(4):
                    ph.tag(e.indirect_dma_start(
                        out=ggt[:, a * 4:(a + 1) * 4], out_offset=None,
                        in_=gate4, in_offset=bass.IndirectOffsetOnAxis(ap=ivcols[:, a * 4:a * 4 + 1], axis=0)))
            ph.on("gpsimd", gat, ndma=8)
            ph.bar()
            for dt in range(8):
                def trp(e, dt=dt):
                    r = None
                    for a in range(4):
                        r = e.transpose(pT[:, a * 128:(a + 1) * 128],
                                        xg_tm[:, a * D + dt * 128:a * D + dt * 128 + 128], ident_f[:, :])
                    return r
                ph.on("tensor", trp)
                ph.bar()
                ph.on("vector", lambda e, dt=dt: e.tensor_copy(x_fm[:, dt * 512:(dt + 1) * 512], pT[:, :]))
                ph.bar()

            wst = xg_tm
            for a in range(8):
                def gw1(e, a=a):
                    ph.tag(e.indirect_dma_start(
                        out=wst[:, :], out_offset=None,
                        in_=w1d, in_offset=bass.IndirectOffsetOnAxis(ap=idxwu[:, a:a + 1], axis=0)))
                ph.on("gpsimd", gw1, ndma=1)
                ph.bar()

                def cw1(e, a=a):
                    ph.tag(e.dma_start(out=w1g[:, a * 4 * D:(a + 1) * 4 * D], in_=wst[:, :]))
                ph.on("gpsimd", cw1, ndma=1)
                ph.bar()

            for ft in range(32):
                def mm1(e, ft=ft):
                    r = None
                    for a in range(8):
                        r = e.matmul(pMM[:, :], w1g[:, a * 4 * D + ft * 128:a * 4 * D + ft * 128 + 128],
                                     x_fm[:, a * 512:(a + 1) * 512], start=(a == 0), stop=(a == 7))
                    return r
                ph.on("tensor", mm1)
                ph.bar()
                ph.on("scalar", lambda e, ft=ft: e.activation(h_bf[:, ft * 512:(ft + 1) * 512], pMM[:, :],
                                                              AF.Relu, bias=b1cols[:, ft:ft + 1]))
                ph.bar()

            for a in range(32):
                def gw2(e, a=a):
                    ph.tag(e.indirect_dma_start(
                        out=wst[:, 0:D], out_offset=None,
                        in_=w2d, in_offset=bass.IndirectOffsetOnAxis(ap=idxw2u[:, a:a + 1], axis=0)))
                ph.on("gpsimd", gw2, ndma=1)
                ph.bar()

                def cw2(e, a=a):
                    ph.tag(e.dma_start(out=w2g[:, a * D:(a + 1) * D], in_=wst[:, 0:D]))
                ph.on("gpsimd", cw2, ndma=1)
                ph.bar()

            for tt in range(4):
                def mm2(e, tt=tt):
                    r = None
                    for half in range(2):
                        psl = pY[:, half * 512:(half + 1) * 512]
                        for i in range(32):
                            e.matmul(psl, h_bf[:, i * 512 + tt * 128:i * 512 + tt * 128 + 128],
                                     w2g[:, i * D + half * 512:i * D + half * 512 + 512],
                                     start=(i == 0), stop=False)
                        r = e.matmul(psl, onesr[0:1, 0:128], b2row[:, half * 512:(half + 1) * 512],
                                     start=False, stop=True)
                    return r
                ph.on("tensor", mm2)
                ph.bar()

                def evy(e, tt=tt):
                    return e.tensor_scalar(y_tmo[:, :], pY[:, :],
                                           ggt[:, tt * 4:tt * 4 + 1], None, ALU.mult)
                ph.on("vector", evy)
                ph.bar()

                def sty(e, r0=r0, tt=tt):
                    ph.tag(e.dma_start(
                        out=ybuf[r0:r0 + 512, :].rearrange("(p a) d -> p (a d)", a=4)[:, tt * D:(tt + 1) * D],
                        in_=y_tmo[:, :]))
                ph.on("sync", sty, ndma=1)
                ph.bar()

        ph.emit(S)
    return nc


def l14_inmaps(lg8_g, x8_tm_g, inp):
    args = dict(
        lg8=cont(lg8_g),
        x8_tm=cont(x8_tm_g),
        w1d=cont(inp["L3_ffn_W1"]).reshape(8 * D, 4 * D),
        b1d=cont(inp["L3_ffn_b1"]),
        w2d=cont(inp["L3_ffn_W2"]).reshape(8 * 4 * D, D),
        b2d=cont(inp["L3_ffn_b2"]),
    )
    ins = []
    for c in range(NCORES):
        m = dict(args)
        m["eoff3"] = np.full((1, 1), 9 * c * 512, np.float32)
        ins.append(m)
    return ins


def build_l15():
    nc = bass.Bass("TRN2", target_bir_lowering=False, num_devices=NCORES)
    CH = 512
    x_tm = nc.dram_tensor("x_tm", [TLOC, D], F32, kind="ExternalInput").ap()
    ybuf = nc.dram_tensor("ybuf", [L3F_ROWS, D], BF16, kind="ExternalInput").ap()
    slot = nc.dram_tensor("slot", [128, TLOC // 128], U32, kind="ExternalInput").ap()
    out = nc.dram_tensor("y_tm", [TLOC, D], F32, kind="ExternalOutput").ap()

    with contextlib.ExitStack() as es:
        S = es.enter_context
        slc = S(nc.sbuf_tensor("slcL", [128, 32], U32))
        yg = S(nc.sbuf_tensor("ygL", [128, 4 * D], BF16))
        o_tm = S(nc.sbuf_tensor("o_tmL", [128, 4 * D], F32))
        res_tm = S(nc.sbuf_tensor("res_tmL", [128, 4 * D], F32))
        ln_sq = S(nc.sbuf_tensor("ln_sqL", [128, 4 * D], F32))
        ln_m = S(nc.sbuf_tensor("ln_mL", [128, 4], F32))
        ln_v = S(nc.sbuf_tensor("ln_vL", [128, 4], F32))
        ln_r = S(nc.sbuf_tensor("ln_rL", [128, 4], F32))

        ph = Ph(nc, S)

        def ld0(e):
            ph.tag(e.dma_start(out=slc[:, :], in_=slot))
        ph.on("sync", ld0, ndma=1)
        ph.bar()

        for ci in range(TLOC // CH):
            c0 = ci * CH

            def ldres(e, c0=c0):
                ph.tag(e.dma_start(out=res_tm.ap().rearrange("p (a d) -> p a d", a=4),
                                   in_=x_tm[c0:c0 + CH, :].rearrange("(a p) d -> p a d", p=128)))
            ph.on("sync", ldres, ndma=1)
            ph.bar()

            def gat(e, ci=ci):
                for a in range(4):
                    ph.tag(e.indirect_dma_start(
                        out=yg[:, a * D:(a + 1) * D], out_offset=None,
                        in_=ybuf, in_offset=bass.IndirectOffsetOnAxis(ap=slc[:, ci * 4 + a:ci * 4 + a + 1], axis=0)))
            ph.on("gpsimd", gat, ndma=4)
            ph.bar()
            ph.on("vector", lambda e: e.tensor_add(o_tm[:, :], res_tm[:, :], yg[:, :]))
            ph.bar()
            ln_tm_batch(ph, lambda i: o_tm[:, i * D:(i + 1) * D], lambda i: o_tm[:, i * D:(i + 1) * D],
                        ln_sq, ln_m, ln_v, ln_r, 4, D)

            def store(e, c0=c0):
                ph.tag(e.dma_start(out=out[c0:c0 + CH, :].rearrange("(a p) d -> p a d", p=128),
                                   in_=o_tm.ap().rearrange("p (a d) -> p a d", a=4)))
            ph.on("sync", store, ndma=1)
            ph.bar()

        ph.emit(S)
    return nc


def l15_inmaps(x8_fm_full, ybuf_g, slotc_g):
    ins = []
    for c in range(NCORES):
        b, q = c // 4, c % 4
        g0 = (b * T + q * TLOC) // 128
        ins.append({
            "x_tm": cont(x8_fm_full[b][:, q * TLOC:(q + 1) * TLOC].T),
            "ybuf": np.ascontiguousarray(ybuf_g),
            "slot": np.ascontiguousarray(slotc_g[:, g0:g0 + TLOC // 128]),
        })
    return ins


# version 04
# speedup vs baseline: 1.0115x; 1.0115x over previous
import contextlib
import math
import numpy as np
import ml_dtypes
import concourse.bass as bass
import concourse.mybir as mybir
from concourse.bass_utils import run_bass_kernel_spmd

F32 = mybir.dt.float32
BF16 = mybir.dt.bfloat16
U32 = mybir.dt.uint32
U8 = mybir.dt.uint8
AF = mybir.ActivationFunctionType
ALU = mybir.AluOpType
AX = mybir.AxisListType

NCORES = 8
NB = 2
T = 16384
D = 1024
TLOC = T // 4
ALPHA = 0.99

bfdt = ml_dtypes.bfloat16
ENGINES = ("sync", "gpsimd", "scalar", "vector", "tensor")


class Ph:

    def __init__(self, nc, S):
        self.nc = nc
        self.barsem = S(nc.semaphore("bar"))
        self.seq = []

    def tag(self, inst):
        self._tagcount += 1
        inst.then_inc(self.barsem, 16)
        return None

    def on(self, engine, fn, ndma=0):
        self.seq.append((engine, fn, ndma))

    def dma(self, engine, fn, ndma=1):
        self.seq.append((engine, fn, ndma))

    def bar(self):
        if self.seq and self.seq[-1] is not None:
            self.seq.append(None)

    def emit(self, S):
        self.bar()
        totals = []
        run = 0
        ndma_ph = 0
        for it in self.seq:
            if it is None:
                run += 16 * ndma_ph + 5
                totals.append(run)
                ndma_ph = 0
            else:
                ndma_ph += it[2]

        seq = self.seq
        block = S(self.nc.Block())
        ph = self

        def mk(eng_name):
            def body(eng):
                bi = 0
                lastinst = None
                declared = 0
                ph._tagcount = 0
                for it in seq:
                    if it is None:
                        assert ph._tagcount == declared, (
                            f"{eng_name} phase {bi}: tagged {ph._tagcount} != declared {declared}")
                        if lastinst is None:
                            eng.nop().then_inc(ph.barsem, 1)
                        else:
                            lastinst.then_inc(ph.barsem, 1)
                        eng.wait_ge(ph.barsem, totals[bi])
                        bi += 1
                        lastinst = None
                        declared = 0
                        ph._tagcount = 0
                    elif it[0] == eng_name:
                        declared += it[2]
                        r = it[1](eng)
                        if r is not None:
                            lastinst = r
            return body

        block.sync(mk("sync"))
        block.gpsimd(mk("gpsimd"))
        block.scalar(mk("scalar"))
        block.vector(mk("vector"))
        block.tensor(mk("tensor"))


def cdiv(a, b):
    return (a + b - 1) // b


def fmT(x):
    return np.ascontiguousarray(np.asarray(x).T, dtype=np.float32)


def cont(x, dt=np.float32):
    return np.ascontiguousarray(np.asarray(x), dtype=dt)


def col128(v):
    v = np.asarray(v, dtype=np.float32)
    return np.ascontiguousarray(v.reshape(-1, 128).T)


def setup_identity(ph, ident_f32, ident_bf):
    def f_io(e):
        return e.iota(ident_f32[:, :], pattern=[[-1, 128]], base=0,
                      channel_multiplier=1, allow_small_or_imprecise_dtypes=True)
    ph.on("gpsimd", f_io)
    ph.bar()

    def f_eq(e):
        e.tensor_scalar(ident_f32[:, :], ident_f32[:, :], 0.0, None, ALU.is_equal)
        return e.tensor_copy(ident_bf[:, :], ident_f32[:, :])
    ph.on("vector", f_eq)
    ph.bar()


def ln_tm(ph, x_tm, out, sq, mcol, vcol, rcol, dm, eps=1e-5):
    ph.on("scalar", lambda e: e.activation(sq, x_tm, AF.Square))
    ph.bar()

    def red(e):
        e.reduce_sum(mcol, x_tm, axis=AX.X)
        return e.reduce_sum(vcol, sq, axis=AX.X)
    ph.on("vector", red)
    ph.bar()

    def stats(e):
        e.tensor_scalar(mcol, mcol, 1.0 / dm, None, ALU.mult)
        return e.tensor_scalar(vcol, vcol, 1.0 / dm, None, ALU.mult)
    ph.on("vector", stats)
    ph.bar()
    ph.on("vector", lambda e: e.tensor_mul(sq[:, 0:1], mcol, mcol))
    ph.bar()
    ph.on("vector", lambda e: e.scalar_tensor_tensor(vcol, sq[:, 0:1], -1.0, vcol, ALU.mult, ALU.add))
    ph.bar()
    ph.on("vector", lambda e: e.tensor_scalar(vcol, vcol, float(eps), None, ALU.add))
    ph.bar()
    ph.on("scalar", lambda e: e.activation(rcol, vcol, AF.Ln))
    ph.bar()
    ph.on("scalar", lambda e: e.activation(rcol, rcol, AF.Exp, scale=-0.5))
    ph.bar()
    ph.on("vector", lambda e: e.tensor_scalar(out, x_tm, mcol, rcol, ALU.subtract, ALU.mult))
    ph.bar()


L0_CH = 512
L0_HALO = 4096


def build_l01():
    nc = bass.Bass("TRN2", target_bir_lowering=False, num_devices=NCORES)
    CH = L0_CH
    NHC = L0_HALO // CH
    NOC = TLOC // CH
    LNA = float(np.log(ALPHA))
    AL = float(ALPHA ** CH)

    xh = nc.dram_tensor("xh_fm", [D, L0_HALO + TLOC], F32, kind="ExternalInput").ap()
    x_tm = nc.dram_tensor("x_tm", [TLOC, D], F32, kind="ExternalInput").ap()
    wqkvT = nc.dram_tensor("wqkvT", [D, 3 * D], F32, kind="ExternalInput").ap()
    bqkv = nc.dram_tensor("bqkv", [128, 24], F32, kind="ExternalInput").ap()
    hvalid = nc.dram_tensor("hvalid", [128, 1], F32, kind="ExternalInput").ap()
    out = nc.dram_tensor("y_tm", [TLOC, D], F32, kind="ExternalOutput").ap()

    with contextlib.ExitStack() as es:
        S = es.enter_context
        w_bf = S(nc.sbuf_tensor("w_bf", [128, 8 * 3 * D], BF16))
        bq = S(nc.sbuf_tensor("bq", [128, 24], F32))
        bq_q = S(nc.sbuf_tensor("bq_q", [128, 8], F32))
        x_bf = S(nc.sbuf_tensor("x_bf", [128, 8 * CH], BF16))
        kfm = S(nc.sbuf_tensor("kfm", [128, 8 * CH], BF16))
        vfm = S(nc.sbuf_tensor("vfm", [128, 8 * CH], BF16))
        qfm = S(nc.sbuf_tensor("qfm", [128, 8 * CH], BF16))
        kT = S(nc.sbuf_tensor("kT", [128, 4 * D], BF16))
        vT = S(nc.sbuf_tensor("vT", [128, 4 * D], BF16))
        sT = S(nc.sbuf_tensor("sT", [128, 4 * CH], BF16))
        S_sb = S(nc.sbuf_tensor("S_sb", [128, 8 * D], F32))
        S_bf = S(nc.sbuf_tensor("S_bf", [128, 8 * D], BF16))
        o_tm = S(nc.sbuf_tensor("o_tm", [128, 4 * D], F32))
        res_tm = S(nc.sbuf_tensor("res_tm", [128, 4 * D], F32))
        mask = S(nc.sbuf_tensor("maskT", [128, 4 * CH], F32))
        iot = S(nc.sbuf_tensor("iot", [128, 4 * CH], F32))
        dec_i = S(nc.sbuf_tensor("dec_i", [128, 4], F32))
        kdec = S(nc.sbuf_tensor("kdec", [128, 4], F32))
        kdec_h = S(nc.sbuf_tensor("kdec_h", [128, 4], F32))
        hv = S(nc.sbuf_tensor("hv", [128, 1], F32))
        ident_f = S(nc.sbuf_tensor("ident_f", [128, 128], F32))
        ident = S(nc.sbuf_tensor("ident", [128, 128], BF16))
        ln_sq = S(nc.sbuf_tensor("ln_sq", [128, D], F32))
        ln_m = S(nc.sbuf_tensor("ln_m", [128, 1], F32))
        ln_v = S(nc.sbuf_tensor("ln_v", [128, 1], F32))
        ln_r = S(nc.sbuf_tensor("ln_r", [128, 1], F32))

        pMM = S(nc.psum_tensor("pMM", [128, 512], F32))
        pO = S(nc.psum_tensor("pO", [128, 1024], F32))
        pI = S(nc.psum_tensor("pI", [128, 1024], F32))
        pS = S(nc.psum_tensor("pS", [128, 512], F32))
        pT = S(nc.psum_tensor("pT", [128, 512], BF16))

        ph = Ph(nc, S)

        def ld_w(e):
            ph.tag(e.dma_start(out=bq[:, :], in_=bqkv))
            ph.tag(e.dma_start(out=hv[:, :], in_=hvalid))
        ph.on("sync", ld_w, ndma=2)

        def ld_wbf(e):
            ph.tag(e.dma_start(out=w_bf.ap().rearrange("p (a o) -> p a o", a=8),
                               in_=wqkvT.rearrange("(a p) o -> p a o", p=128)))
        ph.on("gpsimd", ld_wbf, ndma=1)

        def mk_iota(e):
            r = None
            for jt in range(4):
                r = e.iota(iot[:, jt * CH:(jt + 1) * CH], pattern=[[1, CH]],
                           base=-(jt * 128), channel_multiplier=-1,
                           allow_small_or_imprecise_dtypes=True)
            for it in range(4):
                r = e.iota(dec_i[:, it:it + 1], pattern=[[0, 1]], base=it * 128 + 1,
                           channel_multiplier=1, allow_small_or_imprecise_dtypes=True)
            for jt in range(4):
                r = e.iota(kdec[:, jt:jt + 1], pattern=[[0, 1]], base=CH - 1 - jt * 128,
                           channel_multiplier=-1, allow_small_or_imprecise_dtypes=True)
            r = e.iota(ident_f[:, :], pattern=[[-1, 128]], base=0,
                       channel_multiplier=1, allow_small_or_imprecise_dtypes=True)
            return r
        ph.on("gpsimd", mk_iota)
        ph.bar()

        def mk_masks(e):
            e.activation(mask[:, :], iot[:, :], AF.Exp, scale=LNA)
            e.activation(dec_i[:, :], dec_i[:, :], AF.Exp, scale=LNA)
            return e.activation(kdec[:, :], kdec[:, :], AF.Exp, scale=LNA)
        ph.on("scalar", mk_masks)
        ph.bar()

        def mk_mask2(e):
            e.tensor_scalar(iot[:, :], iot[:, :], 0.0, None, ALU.is_ge)
            e.tensor_scalar(ident_f[:, :], ident_f[:, :], 0.0, None, ALU.is_equal)
            e.tensor_scalar(bq_q[:, :], bq[:, 0:8], 1.0 / 32.0, None, ALU.mult)
            e.tensor_scalar(kdec_h[:, :], kdec[:, :], hv[:, 0:1], None, ALU.mult)
            return e.memset(S_sb[:, :], 0.0)
        ph.on("vector", mk_mask2)
        ph.bar()

        def mk_mask3(e):
            e.tensor_mul(mask[:, :], mask[:, :], iot[:, :])
            return e.tensor_copy(ident[:, :], ident_f[:, :])
        ph.on("vector", mk_mask3)
        ph.bar()

        for ci in range(NHC + NOC):
            own = ci >= NHC
            c0 = ci * CH
            oc0 = (ci - NHC) * CH
            kcol = kdec if own else kdec_h

            def ldx(e, c0=c0):
                ph.tag(e.dma_start(out=x_bf.ap().rearrange("p (a t) -> p a t", a=8),
                                   in_=xh.rearrange("(a p) t -> p a t", p=128)[:, :, c0:c0 + CH]))
            ph.on("gpsimd", ldx, ndma=1)
            if own:
                def ldres(e, oc0=oc0):
                    ph.tag(e.dma_start(out=res_tm.ap().rearrange("p (a d) -> p a d", a=4),
                                       in_=x_tm[oc0:oc0 + CH, :].rearrange("(a p) d -> p a d", p=128)))
                ph.on("sync", ldres, ndma=1)
            ph.bar()

            plist = [(8, kfm), (16, vfm)] + ([(0, qfm)] if own else [])
            for base, dst in plist:
                for ot in range(8):
                    po = base + ot

                    def mm1(e, po=po):
                        r = None
                        for i in range(8):
                            r = e.matmul(pMM[:, :], w_bf[:, i * 3 * D + po * 128:i * 3 * D + po * 128 + 128],
                                         x_bf[:, i * CH:(i + 1) * CH], start=(i == 0), stop=(i == 7))
                        return r
                    ph.on("tensor", mm1)
                    ph.bar()
                    if base == 0:
                        def ev1(e, dst=dst, ot=ot, po=po):
                            return e.tensor_scalar(dst[:, ot * CH:(ot + 1) * CH], pMM[:, :],
                                                   bq[:, po:po + 1], 1.0 / 32.0, ALU.add, ALU.mult)
                    else:
                        def ev1(e, dst=dst, ot=ot, po=po):
                            return e.tensor_scalar(dst[:, ot * CH:(ot + 1) * CH], pMM[:, :],
                                                   bq[:, po:po + 1], None, ALU.add)
                    ph.on("vector", ev1)
                    ph.bar()

            for src, dst, scaled in ((kfm, kT, True), (vfm, vT, False)):
                for tt in range(4):
                    for half in range(2):
                        def trp(e, src=src, tt=tt, half=half):
                            r = None
                            for dd in range(4):
                                d_t = half * 4 + dd
                                r = e.transpose(pT[:, dd * 128:(dd + 1) * 128],
                                                src[:, d_t * CH + tt * 128:d_t * CH + tt * 128 + 128],
                                                ident[:, :])
                            return r
                        ph.on("tensor", trp)
                        ph.bar()
                        if scaled:
                            def evt(e, dst=dst, tt=tt, half=half, kcol=kcol):
                                return e.tensor_scalar(dst[:, tt * D + half * 512:tt * D + half * 512 + 512],
                                                       pT[:, :], kcol[:, tt:tt + 1], None, ALU.mult)
                        else:
                            def evt(e, dst=dst, tt=tt, half=half):
                                return e.tensor_copy(dst[:, tt * D + half * 512:tt * D + half * 512 + 512],
                                                     pT[:, :])
                        ph.on("vector", evt)
                        ph.bar()

            if own:
                for jt in range(4):
                    def mm_sc(e, jt=jt):
                        r = None
                        for i in range(8):
                            r = e.matmul(pMM[:, :], kfm[:, i * CH + jt * 128:i * CH + jt * 128 + 128],
                                         qfm[:, i * CH:(i + 1) * CH], start=(i == 0), stop=(i == 7))
                        return r
                    ph.on("tensor", mm_sc)
                    ph.bar()

                    def ev_sc(e, jt=jt):
                        return e.tensor_mul(sT[:, jt * CH:(jt + 1) * CH], pMM[:, :],
                                            mask[:, jt * CH:(jt + 1) * CH])
                    ph.on("vector", ev_sc)
                    ph.bar()

                for it in range(4):
                    def mm_oi(e, it=it):
                        r = None
                        for half in range(2):
                            for jt in range(4):
                                r = e.matmul(pO[:, half * 512:(half + 1) * 512],
                                             sT[:, jt * CH + it * 128:jt * CH + it * 128 + 128],
                                             vT[:, jt * D + half * 512:jt * D + half * 512 + 512],
                                             start=(jt == 0), stop=(jt == 3))
                        for half in range(2):
                            for dkt in range(8):
                                r = e.matmul(pI[:, half * 512:(half + 1) * 512],
                                             qfm[:, dkt * CH + it * 128:dkt * CH + it * 128 + 128],
                                             S_bf[:, dkt * D + half * 512:dkt * D + half * 512 + 512],
                                             start=(dkt == 0), stop=(dkt == 7))
                        return r
                    ph.on("tensor", mm_oi)
                    ph.bar()

                    def ev_oi(e, it=it):
                        e.tensor_copy(o_tm[:, it * D:it * D + 1024], pO[:, :])
                        return e.scalar_tensor_tensor(o_tm[:, it * D:it * D + 1024], pI[:, :],
                                                      dec_i[:, it:it + 1], o_tm[:, it * D:it * D + 1024],
                                                      ALU.mult, ALU.add)
                    ph.on("vector", ev_oi)
                    ph.bar()

            for dkt in range(8):
                for half in range(2):
                    def mm_s(e, dkt=dkt, half=half):
                        r = None
                        for tt in range(4):
                            r = e.matmul(pS[:, :], kT[:, tt * D + dkt * 128:tt * D + dkt * 128 + 128],
                                         vT[:, tt * D + half * 512:tt * D + half * 512 + 512],
                                         start=(tt == 0), stop=(tt == 3))
                        return r
                    ph.on("tensor", mm_s)
                    ph.bar()

                    def ev_s(e, dkt=dkt, half=half):
                        sl = S_sb[:, dkt * D + half * 512:dkt * D + half * 512 + 512]
                        return e.scalar_tensor_tensor(sl, sl, AL, pS[:, :], ALU.mult, ALU.add)
                    ph.on("vector", ev_s)
                    ph.bar()
            def cast_s(e):
                return e.tensor_copy(S_bf[:, :], S_sb[:, :])
            ph.on("vector", cast_s)
            ph.bar()

            if own:
                def addres(e):
                    return e.tensor_add(o_tm[:, :], o_tm[:, :], res_tm[:, :])
                ph.on("vector", addres)
                ph.bar()
                for it in range(4):
                    ln_tm(ph, o_tm[:, it * D:(it + 1) * D], o_tm[:, it * D:(it + 1) * D],
                          ln_sq[:, :], ln_m[:, :], ln_v[:, :], ln_r[:, :], D)

                def store(e, oc0=oc0):
                    ph.tag(e.dma_start(out=out[oc0:oc0 + CH, :].rearrange("(a p) d -> p a d", p=128),
                                       in_=o_tm.ap().rearrange("p (a d) -> p a d", a=4)))
                ph.on("sync", store, ndma=1)
                ph.bar()

        ph.emit(S)
    return nc


def l01_inmaps(x0, wqkv, bqkv):
    wT = fmT(wqkv)
    bq = col128(bqkv)
    ins = []
    for c in range(NCORES):
        b = c // 4
        q = c % 4
        t0 = q * TLOC
        xb = x0[b]
        own_fm = fmT(xb[t0:t0 + TLOC])
        if q == 0:
            halo = np.zeros((D, L0_HALO), np.float32)
            hvalid = np.zeros((128, 1), np.float32)
        else:
            halo = fmT(xb[t0 - L0_HALO:t0])
            hvalid = np.ones((128, 1), np.float32)
        ins.append({
            "xh_fm": np.concatenate([halo, own_fm], axis=1),
            "x_tm": cont(xb[t0:t0 + TLOC]),
            "wqkvT": wT,
            "bqkv": bq,
            "hvalid": hvalid,
        })
    return ins


def kernel(**inputs):
    inp = inputs
    cores = list(range(NCORES))

    def launch(nc, ins):
        last = None
        for _ in range(3):
            try:
                return run_bass_kernel_spmd(nc, ins, core_ids=cores)
            except Exception as ex:
                last = ex
        raise last

    def assemble_tm(res, key="y_tm"):
        out = np.empty((NB, T, D), np.float32)
        for c in cores:
            b, q = c // 4, c % 4
            out[b, q * TLOC:(q + 1) * TLOC] = np.asarray(res.results[c][key])
        return out

    def fm_full(x_bt):
        return [np.ascontiguousarray(x_bt[b].T) for b in range(NB)]

    def glob_fm(res, key, rows, trim=3):
        g = np.empty((rows, NB * T), np.float32)
        for c in cores:
            b, q = c // 4, c % 4
            cs = b * T + q * TLOC
            g[:, cs:cs + TLOC] = np.asarray(res.results[c][key])[:, trim:trim + TLOC]
        return g

    def glob_rows(res, key, rows):
        g = np.empty((rows, NB * T), np.float32)
        for c in cores:
            b, q = c // 4, c % 4
            cs = b * T + q * TLOC
            g[:, cs:cs + TLOC] = np.asarray(res.results[c][key])
        return g

    def revb(g):
        r = np.empty_like(g)
        for b in range(NB):
            r[:, b * T:(b + 1) * T] = g[:, b * T:(b + 1) * T][:, ::-1]
        return np.ascontiguousarray(r)

    x0 = np.ascontiguousarray(np.asarray(inp["L0_attn_x"], np.float32))
    res = launch(build_l01(),
                               l01_inmaps(x0, inp["L0_attn_W_qkv"], inp["L0_attn_b_qkv"]))
    x1 = assemble_tm(res)

    res = launch(build_l02(),
                               l02_inmaps(fm_full(x1), inp["L0_ffn_W_r"], inp["L0_ffn_W_k"],
                                          inp["L0_ffn_W_v"], inp["L0_ffn_mu_r"], inp["L0_ffn_mu_k"]))
    x2 = assemble_tm(res)
    x2fm = fm_full(x2)

    res = launch(build_l03(), l03_inmaps(x2fm, inp))
    za_g = glob_fm(res, "z_act", D)
    uf_g = glob_fm(res, "u_f", D)
    ub_g = glob_fm(res, "u_b", D)
    df_g = glob_fm(res, "delta_f", D)
    db_g = glob_fm(res, "delta_b", D)
    bcf_g = glob_fm(res, "bc_f", 32)
    bcb_g = glob_fm(res, "bc_b", 32)

    nsc = build_scan(1)
    res = launch(nsc,
                               scan_inmaps(df_g, uf_g, bcf_g[:16], bcf_g[16:], inp["L1_attn_A_log"]))
    yf_g = np.concatenate([np.asarray(res.results[c]["y_sl"]) for c in cores], axis=0)
    res = launch(nsc,
                               scan_inmaps(revb(db_g), revb(ub_g), revb(bcb_g[:16]),
                                           revb(bcb_g[16:]), inp["L1_attn_A_log_b"]))
    yb_g = revb(np.concatenate([np.asarray(res.results[c]["y_sl"]) for c in cores], axis=0))

    res = launch(build_l06(),
                               l06_inmaps(x2fm, yf_g, yb_g, uf_g, ub_g, za_g, inp))
    x3 = assemble_tm(res)
    lg = glob_rows(res, "lg_sl", 32)

    x3tm = np.ascontiguousarray(x3.reshape(-1, D))
    res = launch(build_l07(), l07_inmaps(lg, x3tm, inp))
    ybuf_g = np.concatenate([np.asarray(res.results[c]["ybuf_sl"]) for c in cores], axis=0)
    slotc = np.asarray(res.results[0]["slotc"])
    gatec = np.asarray(res.results[0]["gatec"])

    res = launch(build_l08(),
                               l08_inmaps(fm_full(x3), ybuf_g, slotc, gatec))
    x4 = assemble_tm(res)
    x4fm = fm_full(x4)

    res = launch(build_l09(), l09_inmaps(x4fm, inp))
    r2_g = glob_fm(res, "res_act", 2048)
    u2_g = glob_fm(res, "u", 2048)
    d2_g = glob_fm(res, "delta", 2048)
    bc2_g = glob_fm(res, "bc", 32)

    res = launch(build_scan(2),
                               scan_inmaps(d2_g, u2_g, bc2_g[:16], bc2_g[16:], inp["L2_attn_A_log"]))
    y2_g = np.concatenate([np.asarray(res.results[c]["y_sl"]) for c in cores], axis=0)

    res = launch(build_l11(),
                               l11_inmaps(x4fm, y2_g, u2_g, r2_g, inp))
    x5 = assemble_tm(res)

    res = launch(build_l12(), l12_inmaps(fm_full(x5), inp))
    x6 = assemble_tm(res)
    x6fm = fm_full(x6)

    res = launch(build_l13a(), l13a_inmaps(x6fm, inp))
    x7fm = [np.concatenate([np.asarray(res.results[b * 4 + q]["x7_fm"]) for q in range(4)], axis=1)
            for b in range(NB)]
    res = launch(build_l13b(), l13b_inmaps(x7fm, x6fm, inp))
    x8 = assemble_tm(res)
    lg8 = glob_rows(res, "lg8", 8)

    x8tm = np.ascontiguousarray(x8.reshape(-1, D))
    res = launch(build_l14(), l14_inmaps(lg8, x8tm, inp))
    ybuf3 = np.concatenate([np.asarray(res.results[c]["ybuf3_sl"]) for c in cores], axis=0)
    slotc3 = np.asarray(res.results[0]["slotc3"])

    res = launch(build_l15(), l15_inmaps(fm_full(x8), ybuf3, slotc3))
    return assemble_tm(res)


def build_l02():
    nc = bass.Bass("TRN2", target_bir_lowering=False, num_devices=NCORES)
    CH = 512
    NOC = TLOC // CH
    xh = nc.dram_tensor("xh_fm", [D, 1 + TLOC], F32, kind="ExternalInput").ap()
    x_tm = nc.dram_tensor("x_tm", [TLOC, D], F32, kind="ExternalInput").ap()
    wrT = nc.dram_tensor("wrT", [D, D], F32, kind="ExternalInput").ap()
    wkT = nc.dram_tensor("wkT", [D, D], F32, kind="ExternalInput").ap()
    wvT = nc.dram_tensor("wvT", [D, D], F32, kind="ExternalInput").ap()
    mu_r = nc.dram_tensor("mu_r", [128, 8], F32, kind="ExternalInput").ap()
    mu_k = nc.dram_tensor("mu_k", [128, 8], F32, kind="ExternalInput").ap()
    out = nc.dram_tensor("y_tm", [TLOC, D], F32, kind="ExternalOutput").ap()

    with contextlib.ExitStack() as es:
        S = es.enter_context
        wr_bf = S(nc.sbuf_tensor("wr_bf", [128, 8 * D], BF16))
        wk_bf = S(nc.sbuf_tensor("wk_bf", [128, 8 * D], BF16))
        wv_bf = S(nc.sbuf_tensor("wv_bf", [128, 8 * D], BF16))
        mur = S(nc.sbuf_tensor("mur", [128, 8], F32))
        muk = S(nc.sbuf_tensor("muk", [128, 8], F32))
        murn = S(nc.sbuf_tensor("murn", [128, 8], F32))
        mukn = S(nc.sbuf_tensor("mukn", [128, 8], F32))
        x_f = S(nc.sbuf_tensor("x_f", [128, 8 * CH], F32))
        xp_f = S(nc.sbuf_tensor("xp_f", [128, 8 * CH], F32))
        xr_bf = S(nc.sbuf_tensor("xr_bf", [128, 8 * CH], BF16))
        xk_bf = S(nc.sbuf_tensor("xk_bf", [128, 8 * CH], BF16))
        tmp = S(nc.sbuf_tensor("tmp", [128, 8 * CH], F32))
        kact = S(nc.sbuf_tensor("kact", [128, 8 * CH], BF16))
        r_tm = S(nc.sbuf_tensor("r_tm", [128, 4 * D], F32))
        o_tm = S(nc.sbuf_tensor("o_tm", [128, 4 * D], F32))
        res_tm = S(nc.sbuf_tensor("res_tm", [128, 4 * D], F32))
        ln_sq = S(nc.sbuf_tensor("ln_sq", [128, D], F32))
        ln_m = S(nc.sbuf_tensor("ln_m", [128, 1], F32))
        ln_v = S(nc.sbuf_tensor("ln_v", [128, 1], F32))
        ln_r = S(nc.sbuf_tensor("ln_r", [128, 1], F32))
        pR = S(nc.psum_tensor("pR", [128, 1024], F32))
        pK = S(nc.psum_tensor("pK", [128, 512], F32))
        pV = S(nc.psum_tensor("pV", [128, 1024], F32))

        ph = Ph(nc, S)

        def ld_w(e):
            ph.tag(e.dma_start(out=wr_bf.ap().rearrange("p (a o) -> p a o", a=8),
                               in_=wrT.rearrange("(a p) o -> p a o", p=128)))
            ph.tag(e.dma_start(out=wk_bf.ap().rearrange("p (a o) -> p a o", a=8),
                               in_=wkT.rearrange("(a p) o -> p a o", p=128)))
            ph.tag(e.dma_start(out=wv_bf.ap().rearrange("p (a o) -> p a o", a=8),
                               in_=wvT.rearrange("(a p) o -> p a o", p=128)))
        ph.on("gpsimd", ld_w, ndma=3)

        def ld_mu(e):
            ph.tag(e.dma_start(out=mur[:, :], in_=mu_r))
            ph.tag(e.dma_start(out=muk[:, :], in_=mu_k))
        ph.on("sync", ld_mu, ndma=2)
        ph.bar()

        def neg_mu(e):
            e.tensor_scalar(murn[:, :], mur[:, :], -1.0, 1.0, ALU.mult, ALU.add)
            return e.tensor_scalar(mukn[:, :], muk[:, :], -1.0, 1.0, ALU.mult, ALU.add)
        ph.on("vector", neg_mu)
        ph.bar()

        for ci in range(NOC):
            c0 = ci * CH

            def ldx(e, c0=c0):
                ph.tag(e.dma_start(out=x_f.ap().rearrange("p (a t) -> p a t", a=8),
                                   in_=xh.rearrange("(a p) t -> p a t", p=128)[:, :, c0 + 1:c0 + 1 + CH]))
                ph.tag(e.dma_start(out=xp_f.ap().rearrange("p (a t) -> p a t", a=8),
                                   in_=xh.rearrange("(a p) t -> p a t", p=128)[:, :, c0:c0 + CH]))
                ph.tag(e.dma_start(out=res_tm.ap().rearrange("p (a d) -> p a d", a=4),
                                   in_=x_tm[c0:c0 + CH, :].rearrange("(a p) d -> p a d", p=128)))
            ph.on("sync", ldx, ndma=3)
            ph.bar()

            def mk_xr1(e):
                r = None
                for dt in range(8):
                    sl = slice(dt * CH, (dt + 1) * CH)
                    r = e.tensor_scalar(tmp[:, sl], x_f[:, sl], mur[:, dt:dt + 1], None, ALU.mult)
                return r
            ph.on("vector", mk_xr1)
            ph.bar()

            def mk_xr2(e):
                r = None
                for dt in range(8):
                    sl = slice(dt * CH, (dt + 1) * CH)
                    r = e.scalar_tensor_tensor(xr_bf[:, sl], xp_f[:, sl], murn[:, dt:dt + 1],
                                               tmp[:, sl], ALU.mult, ALU.add)
                return r
            ph.on("vector", mk_xr2)
            ph.bar()

            def mk_xk1(e):
                r = None
                for dt in range(8):
                    sl = slice(dt * CH, (dt + 1) * CH)
                    r = e.tensor_scalar(tmp[:, sl], x_f[:, sl], muk[:, dt:dt + 1], None, ALU.mult)
                return r
            ph.on("vector", mk_xk1)
            ph.bar()

            def mk_xk2(e):
                r = None
                for dt in range(8):
                    sl = slice(dt * CH, (dt + 1) * CH)
                    r = e.scalar_tensor_tensor(xk_bf[:, sl], xp_f[:, sl], mukn[:, dt:dt + 1],
                                               tmp[:, sl], ALU.mult, ALU.add)
                return r
            ph.on("vector", mk_xk2)
            ph.bar()

            for tt in range(4):
                def mm_r(e, tt=tt):
                    r = None
                    for half in range(2):
                        for i in range(8):
                            r = e.matmul(pR[:, half * 512:(half + 1) * 512],
                                         xr_bf[:, i * CH + tt * 128:i * CH + tt * 128 + 128],
                                         wr_bf[:, i * D + half * 512:i * D + half * 512 + 512],
                                         start=(i == 0), stop=(i == 7))
                    return r
                ph.on("tensor", mm_r)
                ph.bar()
                ph.on("scalar", lambda e, tt=tt: e.activation(r_tm[:, tt * D:(tt + 1) * D], pR[:, :], AF.Sigmoid))
                ph.bar()

            for ot in range(8):
                def mm_k(e, ot=ot):
                    r = None
                    for i in range(8):
                        r = e.matmul(pK[:, :], wk_bf[:, i * D + ot * 128:i * D + ot * 128 + 128],
                                     xk_bf[:, i * CH:(i + 1) * CH], start=(i == 0), stop=(i == 7))
                    return r
                ph.on("tensor", mm_k)
                ph.bar()
                ph.on("scalar", lambda e, ot=ot: e.activation(kact[:, ot * CH:(ot + 1) * CH], pK[:, :], AF.Relu))
                ph.bar()
            def sq_k(e):
                return e.tensor_mul(kact[:, :], kact[:, :], kact[:, :])
            ph.on("vector", sq_k)
            ph.bar()

            for tt in range(4):
                def mm_v(e, tt=tt):
                    r = None
                    for half in range(2):
                        for i in range(8):
                            r = e.matmul(pV[:, half * 512:(half + 1) * 512],
                                         kact[:, i * CH + tt * 128:i * CH + tt * 128 + 128],
                                         wv_bf[:, i * D + half * 512:i * D + half * 512 + 512],
                                         start=(i == 0), stop=(i == 7))
                    return r
                ph.on("tensor", mm_v)
                ph.bar()

                def ev_v(e, tt=tt):
                    return e.tensor_mul(o_tm[:, tt * D:(tt + 1) * D], r_tm[:, tt * D:(tt + 1) * D], pV[:, :])
                ph.on("vector", ev_v)
                ph.bar()

            ph.on("vector", lambda e: e.tensor_add(o_tm[:, :], o_tm[:, :], res_tm[:, :]))
            ph.bar()
            for it in range(4):
                ln_tm(ph, o_tm[:, it * D:(it + 1) * D], o_tm[:, it * D:(it + 1) * D],
                      ln_sq[:, :], ln_m[:, :], ln_v[:, :], ln_r[:, :], D)

            def store(e, c0=c0):
                ph.tag(e.dma_start(out=out[c0:c0 + CH, :].rearrange("(a p) d -> p a d", p=128),
                                   in_=o_tm.ap().rearrange("p (a d) -> p a d", a=4)))
            ph.on("sync", store, ndma=1)
            ph.bar()

        ph.emit(S)
    return nc


def l02_inmaps(x1_fm_full, w_r, w_k, w_v, mu_r, mu_k):
    ins = []
    for c in range(NCORES):
        b, q = c // 4, c % 4
        t0 = q * TLOC
        xb = x1_fm_full[b]
        if q == 0:
            prevcol = np.zeros((D, 1), np.float32)
        else:
            prevcol = xb[:, t0 - 1:t0]
        ins.append({
            "xh_fm": np.ascontiguousarray(np.concatenate([prevcol, xb[:, t0:t0 + TLOC]], axis=1)),
            "x_tm": cont(xb[:, t0:t0 + TLOC].T),
            "wrT": fmT(w_r), "wkT": fmT(w_k), "wvT": fmT(w_v),
            "mu_r": col128(mu_r), "mu_k": col128(mu_k),
        })
    return ins


def ln_tm_batch(ph, xv, outv, sq, mc, vc, rc, nt, dm, eps=1e-5):
    def f_sq(e):
        r = None
        for i in range(nt):
            r = e.activation(sq[:, i * dm:(i + 1) * dm], xv(i), AF.Square)
        return r
    ph.on("scalar", f_sq)
    ph.bar()

    def f_red(e):
        r = None
        for i in range(nt):
            e.reduce_sum(mc[:, i:i + 1], xv(i), axis=AX.X)
            r = e.reduce_sum(vc[:, i:i + 1], sq[:, i * dm:(i + 1) * dm], axis=AX.X)
        return r
    ph.on("vector", f_red)
    ph.bar()

    def f_sc(e):
        e.tensor_scalar(mc[:, :nt], mc[:, :nt], 1.0 / dm, None, ALU.mult)
        return e.tensor_scalar(vc[:, :nt], vc[:, :nt], 1.0 / dm, None, ALU.mult)
    ph.on("vector", f_sc)
    ph.bar()
    ph.on("vector", lambda e: e.tensor_mul(rc[:, :nt], mc[:, :nt], mc[:, :nt]))
    ph.bar()
    ph.on("vector", lambda e: e.scalar_tensor_tensor(vc[:, :nt], rc[:, :nt], -1.0, vc[:, :nt], ALU.mult, ALU.add))
    ph.bar()
    ph.on("vector", lambda e: e.tensor_scalar(vc[:, :nt], vc[:, :nt], float(eps), None, ALU.add))
    ph.bar()
    ph.on("scalar", lambda e: e.activation(rc[:, :nt], vc[:, :nt], AF.Ln))
    ph.bar()
    ph.on("scalar", lambda e: e.activation(rc[:, :nt], rc[:, :nt], AF.Exp, scale=-0.5))
    ph.bar()

    def f_norm(e):
        r = None
        for i in range(nt):
            r = e.tensor_scalar(outv(i), xv(i), mc[:, i:i + 1], rc[:, i:i + 1], ALU.subtract, ALU.mult)
        return r
    ph.on("vector", f_norm)
    ph.bar()


TEXT = TLOC + 128


def build_l03():
    nc = bass.Bass("TRN2", target_bir_lowering=False, num_devices=NCORES)
    x_tme = nc.dram_tensor("x_tme", [TEXT, D], F32, kind="ExternalInput").ap()
    ipwT = nc.dram_tensor("ipwT", [D, 2 * D], F32, kind="ExternalInput").ap()
    ln_g = nc.dram_tensor("ln_g", [128, 8], F32, kind="ExternalInput").ap()
    ln_b = nc.dram_tensor("ln_b", [128, 8], F32, kind="ExternalInput").ap()
    cwf = nc.dram_tensor("cwf", [128, 32], F32, kind="ExternalInput").ap()
    cbf = nc.dram_tensor("cbf", [128, 8], F32, kind="ExternalInput").ap()
    cwb = nc.dram_tensor("cwb", [128, 32], F32, kind="ExternalInput").ap()
    cbb = nc.dram_tensor("cbb", [128, 8], F32, kind="ExternalInput").ap()
    xpwfT = nc.dram_tensor("xpwfT", [D, 96], F32, kind="ExternalInput").ap()
    xpwbT = nc.dram_tensor("xpwbT", [D, 96], F32, kind="ExternalInput").ap()
    dtwfT = nc.dram_tensor("dtwfT", [64, D], F32, kind="ExternalInput").ap()
    dtwbT = nc.dram_tensor("dtwbT", [64, D], F32, kind="ExternalInput").ap()
    dtbf = nc.dram_tensor("dtbf", [128, 8], F32, kind="ExternalInput").ap()
    dtbb = nc.dram_tensor("dtbb", [128, 8], F32, kind="ExternalInput").ap()
    hlv = nc.dram_tensor("hlv", [128, 1], F32, kind="ExternalInput").ap()
    hrv = nc.dram_tensor("hrv", [128, 1], F32, kind="ExternalInput").ap()

    o_z = nc.dram_tensor("z_act", [D, TEXT], F32, kind="ExternalOutput").ap()
    o_uf = nc.dram_tensor("u_f", [D, TEXT], F32, kind="ExternalOutput").ap()
    o_ub = nc.dram_tensor("u_b", [D, TEXT], F32, kind="ExternalOutput").ap()
    o_df = nc.dram_tensor("delta_f", [D, TEXT], F32, kind="ExternalOutput").ap()
    o_db = nc.dram_tensor("delta_b", [D, TEXT], F32, kind="ExternalOutput").ap()
    o_bcf = nc.dram_tensor("bc_f", [32, TEXT], F32, kind="ExternalOutput").ap()
    o_bcb = nc.dram_tensor("bc_b", [32, TEXT], F32, kind="ExternalOutput").ap()

    NCH = TEXT // 512 + (1 if TEXT % 512 else 0)

    with contextlib.ExitStack() as es:
        S = es.enter_context
        ipw_bf = S(nc.sbuf_tensor("ipw_bf", [128, 8 * 2 * D], BF16))
        XW = TEXT + 6
        xzx = S(nc.sbuf_tensor("xzx", [128, 8 * XW], BF16))
        g_c = S(nc.sbuf_tensor("g_c", [128, 8], F32))
        b_c = S(nc.sbuf_tensor("b_c", [128, 8], F32))
        cw_f = S(nc.sbuf_tensor("cw_f", [128, 32], F32))
        cb_f = S(nc.sbuf_tensor("cb_f", [128, 8], F32))
        cw_b = S(nc.sbuf_tensor("cw_b", [128, 32], F32))
        cb_b = S(nc.sbuf_tensor("cb_b", [128, 8], F32))
        xpf_bf = S(nc.sbuf_tensor("xpf_bf", [128, 8 * 96], BF16))
        xpb_bf = S(nc.sbuf_tensor("xpb_bf", [128, 8 * 96], BF16))
        dtwf_bf = S(nc.sbuf_tensor("dtwf_bf", [64, D], BF16))
        dtwb_bf = S(nc.sbuf_tensor("dtwb_bf", [64, D], BF16))
        dtb_f = S(nc.sbuf_tensor("dtb_f", [128, 8], F32))
        dtb_b = S(nc.sbuf_tensor("dtb_b", [128, 8], F32))
        x_tm_t = S(nc.sbuf_tensor("x_tm_t", [128, 4 * D], F32))
        n_bf = S(nc.sbuf_tensor("n_bf", [128, 4 * D], BF16))
        hsn = S(nc.sbuf_tensor("hsn", [128, 8 * 512], BF16))
        u_ch = S(nc.sbuf_tensor("u_ch", [128, 8 * 512], F32))
        u_bf = S(nc.sbuf_tensor("u_bf", [128, 8 * 512], BF16))
        xdt = S(nc.sbuf_tensor("xdt", [64, 512], BF16))
        ident_f = S(nc.sbuf_tensor("ident_f", [128, 128], F32))
        ident = S(nc.sbuf_tensor("ident", [128, 128], BF16))
        ln_sq = S(nc.sbuf_tensor("ln_sq", [128, 4 * D], F32))
        dl1 = x_tm_t
        dl2 = ln_sq
        ln_m = S(nc.sbuf_tensor("ln_m", [128, 4], F32))
        ln_v = S(nc.sbuf_tensor("ln_v", [128, 4], F32))
        ln_r = S(nc.sbuf_tensor("ln_r", [128, 4], F32))
        pMM = S(nc.psum_tensor("pMM", [128, 512], F32))
        pT = S(nc.psum_tensor("pT", [128, 512], BF16))

        ph = Ph(nc, S)

        def ld1(e):
            ph.tag(e.dma_start(out=ipw_bf.ap().rearrange("p (a o) -> p a o", a=8),
                               in_=ipwT.rearrange("(a p) o -> p a o", p=128)))
            ph.tag(e.dma_start(out=xpf_bf.ap().rearrange("p (a o) -> p a o", a=8),
                               in_=xpwfT.rearrange("(a p) o -> p a o", p=128)))
            ph.tag(e.dma_start(out=xpb_bf.ap().rearrange("p (a o) -> p a o", a=8),
                               in_=xpwbT.rearrange("(a p) o -> p a o", p=128)))
            ph.tag(e.dma_start(out=dtwf_bf[:, :], in_=dtwfT))
            ph.tag(e.dma_start(out=dtwb_bf[:, :], in_=dtwbT))
        ph.on("gpsimd", ld1, ndma=5)

        hl_c = S(nc.sbuf_tensor("hl_c", [128, 1], F32))
        hr_c = S(nc.sbuf_tensor("hr_c", [128, 1], F32))

        def ld2(e):
            for dst, srcd in ((g_c, ln_g), (b_c, ln_b), (cw_f, cwf), (cb_f, cbf),
                              (cw_b, cwb), (cb_b, cbb), (dtb_f, dtbf), (dtb_b, dtbb),
                              (hl_c, hlv), (hr_c, hrv)):
                ph.tag(e.dma_start(out=dst[:, :], in_=srcd))
        ph.on("sync", ld2, ndma=10)

        def mkid2(e):
            return e.iota(ident_f[:, :], pattern=[[-1, 128]], base=0,
                          channel_multiplier=1, allow_small_or_imprecise_dtypes=True)
        ph.on("gpsimd", mkid2)
        ph.on("vector", lambda e: e.memset(xzx[:, :], 0.0))
        ph.bar()

        def mkid3(e):
            return e.tensor_scalar(ident_f[:, :], ident_f[:, :], 0.0, None, ALU.is_equal)
        ph.on("vector", mkid3)
        ph.bar()
        ph.on("vector", lambda e: e.tensor_copy(ident[:, :], ident_f[:, :]))
        ph.bar()

        chs = [(i * 512, 512) for i in range(TEXT // 512)]
        if TEXT % 512:
            chs.append((TEXT // 512 * 512, TEXT % 512))
        for c0, W in chs:
            TT = W // 128

            def ldc(e, c0=c0, W=W, TT=TT):
                ph.tag(e.dma_start(out=x_tm_t.ap().rearrange("p (a d) -> p a d", a=4)[:, 0:TT, :],
                                   in_=x_tme[c0:c0 + W, :].rearrange("(a p) d -> p a d", p=128)))
            ph.on("sync", ldc, ndma=1)
            ph.bar()
            ln_tm_batch(ph, lambda i: x_tm_t[:, i * D:(i + 1) * D],
                        lambda i: n_bf[:, i * D:(i + 1) * D],
                        ln_sq, ln_m, ln_v, ln_r, TT, D)

            for dt in range(8):
                def trp(e, dt=dt, TT=TT):
                    r = None
                    for tt in range(TT):
                        r = e.transpose(pT[:, tt * 128:(tt + 1) * 128],
                                        n_bf[:, tt * D + dt * 128:tt * D + dt * 128 + 128],
                                        ident[:, :])
                    return r
                ph.on("tensor", trp)
                ph.bar()

                def evt(e, dt=dt, W=W):
                    return e.tensor_scalar(hsn[:, dt * 512:dt * 512 + W], pT[:, 0:W],
                                           g_c[:, dt:dt + 1], b_c[:, dt:dt + 1], ALU.mult, ALU.add)
                ph.on("vector", evt)
                ph.bar()

            for ot in range(16):
                def mmip(e, ot=ot, W=W):
                    r = None
                    for i in range(8):
                        r = e.matmul(pMM[:, 0:W], ipw_bf[:, i * 2 * D + ot * 128:i * 2 * D + ot * 128 + 128],
                                     hsn[:, i * 512:i * 512 + W], start=(i == 0), stop=(i == 7))
                    return r
                ph.on("tensor", mmip)
                ph.bar()
                if ot < 8:
                    def evip(e, ot=ot, c0=c0, W=W):
                        return e.tensor_copy(xzx[:, ot * XW + c0 + 3:ot * XW + c0 + 3 + W], pMM[:, 0:W])
                    ph.on("vector", evip)
                    ph.bar()
                else:
                    def evz(e, ot=ot, W=W):
                        return e.activation(u_ch[:, (ot - 8) * 512:(ot - 8) * 512 + W], pMM[:, 0:W], AF.Silu)
                    ph.on("scalar", evz)
                    ph.bar()
            def stz(e, c0=c0, W=W):
                ph.tag(e.dma_start(out=o_z.rearrange("(a p) t -> p a t", p=128)[:, :, c0:c0 + W],
                                   in_=u_ch.ap().rearrange("p (a t) -> p a t", a=8)[:, :, 0:W]))
            ph.on("sync", stz, ndma=1)
            ph.bar()

        def zhalo(e):
            xv = xzx.ap().rearrange("p (a t) -> p a t", a=8)
            e.tensor_scalar(xv[:, :, 3:6], xv[:, :, 3:6], hl_c[:, 0:1], None, ALU.mult)
            return e.tensor_scalar(xv[:, :, TLOC + 6:TLOC + 9], xv[:, :, TLOC + 6:TLOC + 9],
                                   hr_c[:, 0:1], None, ALU.mult)
        ph.on("vector", zhalo)
        ph.bar()

        for br in range(2):
            cw, cb = (cw_f, cb_f) if br == 0 else (cw_b, cb_b)
            xp = xpf_bf if br == 0 else xpb_bf
            dtw = dtwf_bf if br == 0 else dtwb_bf
            dtb = dtb_f if br == 0 else dtb_b
            o_u = o_uf if br == 0 else o_ub
            o_d = o_df if br == 0 else o_db
            o_bc = o_bcf if br == 0 else o_bcb
            for c0, W in chs:
                TT = W // 128
                def cvi(e, c0=c0, W=W, br=br, cw=cw):
                    r = None
                    for dt in range(8):
                        for k in range(4):
                            off = (c0 + k) if br == 0 else (c0 + 6 - k)
                            src = xzx[:, dt * XW + off:dt * XW + off + W]
                            wcol = cw[:, dt * 4 + k:dt * 4 + k + 1]
                            if k == 0:
                                r = e.tensor_scalar(u_ch[:, dt * 512:dt * 512 + W], src, wcol, None, ALU.mult)
                            else:
                                r = e.scalar_tensor_tensor(u_ch[:, dt * 512:dt * 512 + W], src, wcol,
                                                           u_ch[:, dt * 512:dt * 512 + W], ALU.mult, ALU.add)
                    return r
                ph.on("vector", cvi)
                ph.bar()

                def cvb(e, cb=cb, W=W):
                    r = None
                    for dt in range(8):
                        r = e.tensor_scalar(u_ch[:, dt * 512:dt * 512 + W],
                                            u_ch[:, dt * 512:dt * 512 + W], cb[:, dt:dt + 1], None, ALU.add)
                    return r
                ph.on("vector", cvb)
                ph.bar()

                def usil(e, W=W):
                    r = None
                    for dt in range(8):
                        r = e.activation(u_ch[:, dt * 512:dt * 512 + W], u_ch[:, dt * 512:dt * 512 + W], AF.Silu)
                    return r
                ph.on("scalar", usil)
                ph.bar()
                ph.on("vector", lambda e, W=W: e.tensor_copy(u_bf[:, :], u_ch[:, :]))
                ph.bar()

                def stu(e, c0=c0, W=W, o_u=o_u):
                    ph.tag(e.dma_start(out=o_u.rearrange("(a p) t -> p a t", p=128)[:, :, c0:c0 + W],
                                       in_=u_ch.ap().rearrange("p (a t) -> p a t", a=8)[:, :, 0:W]))
                ph.on("sync", stu, ndma=1)

                def mmxd(e, xp=xp, W=W):
                    r = None
                    for i in range(8):
                        r = e.matmul(pMM[0:96, 0:W], xp[:, i * 96:(i + 1) * 96],
                                     u_bf[:, i * 512:i * 512 + W], start=(i == 0), stop=(i == 7))
                    return r
                ph.on("tensor", mmxd)
                ph.bar()

                def evxd(e, W=W):
                    return e.tensor_copy(xdt[:, 0:W], pMM[0:64, 0:W])
                ph.on("vector", evxd)

                ph.bar()

                def evbc(e, W=W):
                    return e.tensor_copy(ln_sq[0:32, 0:W], pMM[64:96, 0:W])
                ph.on("vector", evbc)
                ph.bar()

                def stbc2(e, c0=c0, W=W, o_bc=o_bc):
                    ph.tag(e.dma_start(out=o_bc[:, c0:c0 + W], in_=ln_sq[0:32, 0:W]))
                ph.on("sync", stbc2, ndma=1)

                for ot in range(8):
                    def mmdt(e, ot=ot, dtw=dtw, W=W):
                        return e.matmul(pMM[:, 0:W], dtw[:, ot * 128:(ot + 1) * 128],
                                        xdt[:, 0:W], start=True, stop=True)
                    ph.on("tensor", mmdt)
                    ph.bar()

                    def evdt(e, ot=ot, dtb=dtb, W=W):
                        return e.tensor_scalar(dl1[:, ot * 512:ot * 512 + W], pMM[:, 0:W],
                                               dtb[:, ot:ot + 1], None, ALU.add)
                    ph.on("vector", evdt)
                    ph.bar()
                def sp1(e, W=W):
                    r = None
                    for dt in range(8):
                        r = e.activation(dl2[:, dt * 512:dt * 512 + W], dl1[:, dt * 512:dt * 512 + W], AF.Abs)
                    return r
                ph.on("scalar", sp1)
                ph.bar()

                def sp2(e, W=W):
                    r = None
                    for dt in range(8):
                        r = e.activation(dl2[:, dt * 512:dt * 512 + W], dl2[:, dt * 512:dt * 512 + W], AF.Exp, scale=-1.0)
                    return r
                ph.on("scalar", sp2)
                ph.bar()
                ph.on("vector", lambda e, W=W: e.tensor_scalar(dl2[:, :], dl2[:, :], 1.0, None, ALU.add))
                ph.bar()

                def sp3(e, W=W):
                    r = None
                    for dt in range(8):
                        r = e.activation(dl2[:, dt * 512:dt * 512 + W], dl2[:, dt * 512:dt * 512 + W], AF.Ln)
                    return r
                ph.on("scalar", sp3)
                ph.bar()

                def sp4(e, W=W):
                    r = None
                    for dt in range(8):
                        r = e.activation(dl1[:, dt * 512:dt * 512 + W], dl1[:, dt * 512:dt * 512 + W], AF.Relu)
                    return r
                ph.on("scalar", sp4)
                ph.bar()
                ph.on("vector", lambda e, W=W: e.tensor_add(dl1[:, :], dl1[:, :], dl2[:, :]))
                ph.bar()

                def std(e, c0=c0, W=W, o_d=o_d):
                    ph.tag(e.dma_start(out=o_d.rearrange("(a p) t -> p a t", p=128)[:, :, c0:c0 + W],
                                       in_=dl1.ap().rearrange("p (a t) -> p a t", a=8)[:, :, 0:W]))
                ph.on("sync", std, ndma=1)
                ph.bar()

        ph.emit(S)
    return nc


def l03_inmaps(x2_fm_full, inp):
    ins = []
    ipwT = fmT(inp["L1_attn_in_proj_w"])
    args = dict(
        ipwT=ipwT,
        ln_g=col128(inp["L1_attn_ln_g"]), ln_b=col128(inp["L1_attn_ln_b"]),
        cwf=cont(inp["L1_attn_conv_w"]).reshape(8, 128, 4).transpose(1, 0, 2).reshape(128, 32).copy(),
        cbf=col128(inp["L1_attn_conv_b"]),
        cwb=cont(inp["L1_attn_conv_w_b"]).reshape(8, 128, 4).transpose(1, 0, 2).reshape(128, 32).copy(),
        cbb=col128(inp["L1_attn_conv_b_b"]),
        xpwfT=fmT(inp["L1_attn_x_proj_w"]), xpwbT=fmT(inp["L1_attn_x_proj_w_b"]),
        dtwfT=fmT(inp["L1_attn_dt_proj_w"]), dtwbT=fmT(inp["L1_attn_dt_proj_w_b"]),
        dtbf=col128(inp["L1_attn_dt_proj_b"]), dtbb=col128(inp["L1_attn_dt_proj_b_b"]),
    )
    for c in range(NCORES):
        b, q = c // 4, c % 4
        t0 = q * TLOC
        xb = x2_fm_full[b]
        ext = np.zeros((TEXT, D), np.float32)
        lo = t0 - 3
        hi = t0 + TLOC + 3
        slo = max(lo, 0)
        shi = min(hi, T)
        ext[slo - lo:slo - lo + (shi - slo)] = xb[:, slo:shi].T
        m = dict(args)
        m["x_tme"] = ext
        m["hlv"] = np.full((128, 1), 0.0 if q == 0 else 1.0, np.float32)
        m["hrv"] = np.full((128, 1), 0.0 if q == 3 else 1.0, np.float32)
        ins.append(m)
    return ins


def build_scan(NT, TB=T, NBT=NB, TC=2048):
    nc = bass.Bass("TRN2", target_bir_lowering=False, num_devices=NCORES)
    ROWS = 128 * NT
    TTOT = NBT * TB
    d_sl = nc.dram_tensor("delta_sl", [ROWS, TTOT], F32, kind="ExternalInput").ap()
    u_sl = nc.dram_tensor("u_sl", [ROWS, TTOT], F32, kind="ExternalInput").ap()
    bc = nc.dram_tensor("bc", [32, TTOT], F32, kind="ExternalInput").ap()
    alog = nc.dram_tensor("alog", [128, NT * 16], F32, kind="ExternalInput").ap()
    y_sl = nc.dram_tensor("y_sl", [ROWS, TTOT], F32, kind="ExternalOutput").ap()

    NPC = TC // 512

    with contextlib.ExitStack() as es:
        S = es.enter_context
        acol = S(nc.sbuf_tensor("acol", [128, NT * 16], F32))
        sel = S(nc.sbuf_tensor("sel", [32, 32 * 128], F32))
        dch = S(nc.sbuf_tensor("dch", [128, TC], F32))
        uch = S(nc.sbuf_tensor("uch", [128, TC], F32))
        duch = S(nc.sbuf_tensor("duch", [128, TC], F32))
        bcch = S(nc.sbuf_tensor("bcch", [32, TC], F32))
        dA = S(nc.sbuf_tensor("dA", [128, TC], F32))
        dbu = S(nc.sbuf_tensor("dbu", [128, TC], F32))
        hch = S(nc.sbuf_tensor("hch", [128, TC], F32))
        zch = S(nc.sbuf_tensor("zch", [128, TC], F32))
        yacc = S(nc.sbuf_tensor("yacc", [128, TC], F32))
        carry = S(nc.sbuf_tensor("carry", [128, 16 * NT], F32))
        pB = S(nc.psum_tensor("pB", [128, TC], F32))
        pC = S(nc.psum_tensor("pC", [128, TC], F32))

        ph = Ph(nc, S)

        def ld0(e):
            ph.tag(e.dma_start(out=acol[:, :], in_=alog))
        ph.on("sync", ld0, ndma=1)
        ph.on("gpsimd", lambda e: e.iota(sel[:, :], pattern=[[-1, 32], [0, 128]], base=0,
                                         channel_multiplier=1, allow_small_or_imprecise_dtypes=True))
        ph.bar()
        ph.on("vector", lambda e: e.tensor_scalar(sel[:, :], sel[:, :], 0.0, None, ALU.is_equal))
        ph.bar()
        ph.on("scalar", lambda e: e.activation(acol[:, :], acol[:, :], AF.Exp))
        ph.bar()
        ph.on("vector", lambda e: e.tensor_scalar(acol[:, :], acol[:, :], -1.0, None, ALU.mult))
        ph.bar()

        for dt in range(NT):
            r0 = dt * 128
            for b in range(NBT):
                ph.on("vector", lambda e, dt=dt: e.memset(carry[:, dt * 16:(dt + 1) * 16], 0.0))
                ph.bar()
                for ci in range(TB // TC):
                    c0 = b * TB + ci * TC

                    def ldc(e, r0=r0, c0=c0):
                        ph.tag(e.dma_start(out=dch[:, :], in_=d_sl[r0:r0 + 128, c0:c0 + TC]))
                        ph.tag(e.dma_start(out=uch[:, :], in_=u_sl[r0:r0 + 128, c0:c0 + TC]))
                        ph.tag(e.dma_start(out=bcch[:, :], in_=bc[:, c0:c0 + TC]))
                    ph.on("sync", ldc, ndma=3)
                    ph.bar()
                    ph.on("vector", lambda e: e.tensor_mul(duch[:, :], dch[:, :], uch[:, :]))
                    ph.bar()

                    for n in range(16):
                        def mkda(e, dt=dt, n=n):
                            return e.activation(dA[:, :], dch[:, :], AF.Exp,
                                                scale=acol[:, dt * 16 + n:dt * 16 + n + 1])
                        ph.on("scalar", mkda)

                        def bbc(e, n=n):
                            r = None
                            for pc in range(NPC):
                                r = e.matmul(pB[:, pc * 512:(pc + 1) * 512],
                                             sel[:, n * 128:(n + 1) * 128],
                                             bcch[:, pc * 512:(pc + 1) * 512],
                                             start=True, stop=True)
                                r = e.matmul(pC[:, pc * 512:(pc + 1) * 512],
                                             sel[:, (16 + n) * 128:(17 + n) * 128],
                                             bcch[:, pc * 512:(pc + 1) * 512],
                                             start=True, stop=True)
                            return r
                        ph.on("tensor", bbc)
                        ph.bar()

                        def vecall(e, dt=dt, n=n):
                            for pc in range(NPC):
                                sl = slice(pc * 512, (pc + 1) * 512)
                                e.tensor_mul(dbu[:, sl], duch[:, sl], pB[:, sl])
                            e.tensor_tensor_scan(hch[:, :], dA[:, :], dbu[:, :],
                                                 carry[:, dt * 16 + n:dt * 16 + n + 1],
                                                 ALU.mult, ALU.add)
                            e.tensor_copy(carry[:, dt * 16 + n:dt * 16 + n + 1], hch[:, TC - 1:TC])
                            r = None
                            if n == 0:
                                for pc in range(NPC):
                                    sl = slice(pc * 512, (pc + 1) * 512)
                                    r = e.tensor_mul(yacc[:, sl], hch[:, sl], pC[:, sl])
                            else:
                                for pc in range(NPC):
                                    sl = slice(pc * 512, (pc + 1) * 512)
                                    e.tensor_mul(zch[:, sl], hch[:, sl], pC[:, sl])
                                r = e.tensor_add(yacc[:, :], yacc[:, :], zch[:, :])
                            return r
                        ph.on("vector", vecall)
                        ph.bar()

                    def sty(e, r0=r0, c0=c0):
                        ph.tag(e.dma_start(out=y_sl[r0:r0 + 128, c0:c0 + TC], in_=yacc[:, :]))
                    ph.on("sync", sty, ndma=1)
                    ph.bar()

        ph.emit(S)
    return nc


def scan_inmaps(delta_g, u_g, b_g, c_g, a_log):
    Dn = delta_g.shape[0]
    NT = Dn // (128 * NCORES)
    bcm = np.concatenate([b_g, c_g], axis=0).astype(np.float32)
    bcm = np.ascontiguousarray(bcm)
    ins = []
    for c in range(NCORES):
        r0 = c * 128 * NT
        sl = slice(r0, r0 + 128 * NT)
        al = a_log[sl]
        alc = np.ascontiguousarray(al.reshape(NT, 128, 16).transpose(1, 0, 2).reshape(128, NT * 16))
        ins.append({
            "delta_sl": np.ascontiguousarray(delta_g[sl]),
            "u_sl": np.ascontiguousarray(u_g[sl]),
            "bc": bcm,
            "alog": alc.astype(np.float32),
        })
    return ins


def build_l06():
    nc = bass.Bass("TRN2", target_bir_lowering=False, num_devices=NCORES)
    CH = 512
    x_tm = nc.dram_tensor("x_tm", [TLOC, D], F32, kind="ExternalInput").ap()
    yf = nc.dram_tensor("yf", [D, TLOC], F32, kind="ExternalInput").ap()
    yb = nc.dram_tensor("yb", [D, TLOC], F32, kind="ExternalInput").ap()
    uf = nc.dram_tensor("uf", [D, TLOC], F32, kind="ExternalInput").ap()
    ub = nc.dram_tensor("ub", [D, TLOC], F32, kind="ExternalInput").ap()
    za = nc.dram_tensor("za", [D, TLOC], F32, kind="ExternalInput").ap()
    dpf = nc.dram_tensor("dpf", [128, 8], F32, kind="ExternalInput").ap()
    dpb = nc.dram_tensor("dpb", [128, 8], F32, kind="ExternalInput").ap()
    outwT = nc.dram_tensor("outwT", [D, D], F32, kind="ExternalInput").ap()
    fcwT = nc.dram_tensor("fcwT", [D, D], F32, kind="ExternalInput").ap()
    fcb = nc.dram_tensor("fcb", [1, D], F32, kind="ExternalInput").ap()
    wrt = nc.dram_tensor("wrt", [D, 32], F32, kind="ExternalInput").ap()
    out = nc.dram_tensor("y_tm", [TLOC, D], F32, kind="ExternalOutput").ap()
    o_lg = nc.dram_tensor("lg_sl", [32, TLOC], F32, kind="ExternalOutput").ap()

    with contextlib.ExitStack() as es:
        S = es.enter_context
        ow_bf = S(nc.sbuf_tensor("ow_bf", [128, 8 * D], BF16))
        fw_bf = S(nc.sbuf_tensor("fw_bf", [128, 8 * D], BF16))
        fb_row = S(nc.sbuf_tensor("fb_row", [1, D], BF16))
        onesr = S(nc.sbuf_tensor("onesr", [1, 512], BF16))
        dp_f = S(nc.sbuf_tensor("dp_f", [128, 8], F32))
        dp_b = S(nc.sbuf_tensor("dp_b", [128, 8], F32))
        t_yf = S(nc.sbuf_tensor("t_yf", [128, 8 * CH], F32))
        t_yb = S(nc.sbuf_tensor("t_yb", [128, 8 * CH], F32))
        t_uf = S(nc.sbuf_tensor("t_uf", [128, 8 * CH], F32))
        t_ub = S(nc.sbuf_tensor("t_ub", [128, 8 * CH], F32))
        t_za = S(nc.sbuf_tensor("t_za", [128, 8 * CH], F32))
        g_bf = S(nc.sbuf_tensor("g_bf", [128, 8 * CH], BF16))
        mid_bf = S(nc.sbuf_tensor("mid_bf", [128, 8 * CH], BF16))
        o_tm = S(nc.sbuf_tensor("o_tm", [128, 4 * D], F32))
        res_tm = S(nc.sbuf_tensor("res_tm", [128, 4 * D], F32))
        ln_sq = S(nc.sbuf_tensor("ln_sq", [128, 4 * D], F32))
        ln_m = S(nc.sbuf_tensor("ln_m", [128, 4], F32))
        ln_v = S(nc.sbuf_tensor("ln_v", [128, 4], F32))
        ln_r = S(nc.sbuf_tensor("ln_r", [128, 4], F32))
        wr_bf = S(nc.sbuf_tensor("wr_bf", [128, 8 * 32], BF16))
        xn_bf = S(nc.sbuf_tensor("xn_bf", [128, 4 * D], BF16))
        xf_bf = S(nc.sbuf_tensor("xf_bf", [128, 8 * CH], BF16))
        lg_sb = S(nc.sbuf_tensor("lg_sb", [32, CH], F32))
        ident_f = S(nc.sbuf_tensor("ident_f6", [128, 128], F32))
        ident = S(nc.sbuf_tensor("ident6", [128, 128], BF16))
        pMM = S(nc.psum_tensor("pMM", [128, 512], F32))
        pO = S(nc.psum_tensor("pO", [128, 1024], F32))
        pT = S(nc.psum_tensor("pT", [128, 512], BF16))

        ph = Ph(nc, S)

        def ldw(e):
            ph.tag(e.dma_start(out=ow_bf.ap().rearrange("p (a o) -> p a o", a=8),
                               in_=outwT.rearrange("(a p) o -> p a o", p=128)))
            ph.tag(e.dma_start(out=fw_bf.ap().rearrange("p (a o) -> p a o", a=8),
                               in_=fcwT.rearrange("(a p) o -> p a o", p=128)))
            ph.tag(e.dma_start(out=fb_row[:, :], in_=fcb))
            ph.tag(e.dma_start(out=wr_bf.ap().rearrange("p (a o) -> p a o", a=8),
                               in_=wrt.rearrange("(a p) o -> p a o", p=128)))
        ph.on("gpsimd", ldw, ndma=4)
        ph.on("gpsimd", lambda e: e.iota(ident_f[:, :], pattern=[[-1, 128]], base=0,
                                         channel_multiplier=1, allow_small_or_imprecise_dtypes=True))
        ph.bar()
        ph.on("vector", lambda e: e.tensor_scalar(ident_f[:, :], ident_f[:, :], 0.0, None, ALU.is_equal))
        ph.bar()
        ph.on("vector", lambda e: e.tensor_copy(ident[:, :], ident_f[:, :]))
        ph.bar()

        def ldc0(e):
            ph.tag(e.dma_start(out=dp_f[:, :], in_=dpf))
            ph.tag(e.dma_start(out=dp_b[:, :], in_=dpb))
        ph.on("sync", ldc0, ndma=2)
        ph.on("vector", lambda e: e.memset(onesr[:, :], 1.0))
        ph.bar()

        for ci in range(TLOC // CH):
            c0 = ci * CH

            def ldc(e, c0=c0):
                for dst, srcd in ((t_yf, yf), (t_yb, yb), (t_uf, uf), (t_ub, ub), (t_za, za)):
                    ph.tag(e.dma_start(out=dst.ap().rearrange("p (a t) -> p a t", a=8),
                                       in_=srcd.rearrange("(a p) t -> p a t", p=128)[:, :, c0:c0 + CH]))
                ph.tag(e.dma_start(out=res_tm.ap().rearrange("p (a d) -> p a d", a=4),
                                   in_=x_tm[c0:c0 + CH, :].rearrange("(a p) d -> p a d", p=128)))
            ph.on("sync", ldc, ndma=6)
            ph.bar()

            ph.on("vector", lambda e: e.tensor_add(t_yf[:, :], t_yf[:, :], t_yb[:, :]))
            ph.bar()

            def dterm1(e):
                r = None
                for dt in range(8):
                    sl = slice(dt * CH, (dt + 1) * CH)
                    r = e.scalar_tensor_tensor(t_yf[:, sl], t_uf[:, sl], dp_f[:, dt:dt + 1],
                                               t_yf[:, sl], ALU.mult, ALU.add)
                return r
            ph.on("vector", dterm1)
            ph.bar()

            def dterm2(e):
                r = None
                for dt in range(8):
                    sl = slice(dt * CH, (dt + 1) * CH)
                    r = e.scalar_tensor_tensor(t_yf[:, sl], t_ub[:, sl], dp_b[:, dt:dt + 1],
                                               t_yf[:, sl], ALU.mult, ALU.add)
                return r
            ph.on("vector", dterm2)
            ph.bar()
            ph.on("vector", lambda e: e.tensor_mul(g_bf[:, :], t_yf[:, :], t_za[:, :]))
            ph.bar()

            for ot in range(8):
                def mm1(e, ot=ot):
                    r = None
                    for i in range(8):
                        r = e.matmul(pMM[:, :], ow_bf[:, i * D + ot * 128:i * D + ot * 128 + 128],
                                     g_bf[:, i * CH:(i + 1) * CH], start=(i == 0), stop=(i == 7))
                    return r
                ph.on("tensor", mm1)
                ph.bar()
                ph.on("vector", lambda e, ot=ot: e.tensor_copy(mid_bf[:, ot * CH:(ot + 1) * CH], pMM[:, :]))
                ph.bar()

            for tt in range(4):
                def mm2(e, tt=tt):
                    r = None
                    for half in range(2):
                        psl = pO[:, half * 512:(half + 1) * 512]
                        for i in range(8):
                            e.matmul(psl, mid_bf[:, i * CH + tt * 128:i * CH + tt * 128 + 128],
                                     fw_bf[:, i * D + half * 512:i * D + half * 512 + 512],
                                     start=(i == 0), stop=False)
                        r = e.matmul(psl, onesr[:, tt * 128:(tt + 1) * 128],
                                     fb_row[:, half * 512:(half + 1) * 512], start=False, stop=True)
                    return r
                ph.on("tensor", mm2)
                ph.bar()

                def evo(e, tt=tt):
                    return e.scalar_tensor_tensor(o_tm[:, tt * D:(tt + 1) * D],
                                                  res_tm[:, tt * D:(tt + 1) * D], 2.0,
                                                  pO[:, :], ALU.mult, ALU.add)
                ph.on("vector", evo)
                ph.bar()

            ln_tm_batch(ph, lambda i: o_tm[:, i * D:(i + 1) * D], lambda i: o_tm[:, i * D:(i + 1) * D],
                        ln_sq, ln_m, ln_v, ln_r, 4, D)

            def store(e, c0=c0):
                ph.tag(e.dma_start(out=out[c0:c0 + CH, :].rearrange("(a p) d -> p a d", p=128),
                                   in_=o_tm.ap().rearrange("p (a d) -> p a d", a=4)))
            ph.on("sync", store, ndma=1)
            ph.on("vector", lambda e: e.tensor_copy(xn_bf[:, :], o_tm[:, :]))
            ph.bar()

            for dt in range(8):
                def trp(e, dt=dt):
                    r = None
                    for tt in range(4):
                        r = e.transpose(pT[:, tt * 128:(tt + 1) * 128],
                                        xn_bf[:, tt * D + dt * 128:tt * D + dt * 128 + 128],
                                        ident[:, :])
                    return r
                ph.on("tensor", trp)
                ph.bar()
                ph.on("vector", lambda e, dt=dt: e.tensor_copy(xf_bf[:, dt * CH:(dt + 1) * CH], pT[:, :]))
                ph.bar()

            def mmlg(e):
                r = None
                for i in range(8):
                    r = e.matmul(pMM[0:32, :], wr_bf[:, i * 32:(i + 1) * 32],
                                 xf_bf[:, i * CH:(i + 1) * CH], start=(i == 0), stop=(i == 7))
                return r
            ph.on("tensor", mmlg)
            ph.bar()
            ph.on("vector", lambda e: e.tensor_copy(lg_sb[:, :], pMM[0:32, :]))
            ph.bar()

            def stlg(e, c0=c0):
                ph.tag(e.dma_start(out=o_lg[:, c0:c0 + CH], in_=lg_sb[:, :]))
            ph.on("sync", stlg, ndma=1)
            ph.bar()

        ph.emit(S)
    return nc


def l06_inmaps(x2_fm_full, yf_g, yb_g, uf_g, ub_g, za_g, inp):
    args = dict(
        dpf=col128(inp["L1_attn_Dp"]), dpb=col128(inp["L1_attn_Dp_b"]),
        outwT=fmT(inp["L1_attn_out_proj_w"]), fcwT=fmT(inp["L1_attn_fc_w"]),
        fcb=cont(inp["L1_attn_fc_b"]).reshape(1, D),
        wrt=cont(inp["L1_ffn_w_router"]),
    )
    ins = []
    for c in range(NCORES):
        b, q = c // 4, c % 4
        cs = b * T + q * TLOC
        m = dict(args)
        m["x_tm"] = cont(x2_fm_full[b][:, q * TLOC:(q + 1) * TLOC].T)
        for nmk, g in (("yf", yf_g), ("yb", yb_g), ("uf", uf_g), ("ub", ub_g), ("za", za_g)):
            m[nmk] = np.ascontiguousarray(g[:, cs:cs + TLOC])
        ins.append(m)
    return ins


L1F_C = 1280
L1F_CP = 1536
L1F_BIG = float(1 << 20)


def build_l07():
    nc = bass.Bass("TRN2", target_bir_lowering=False, num_devices=NCORES)
    TTK = NB * T
    NCH = TTK // 512
    lg = nc.dram_tensor("lg", [32, TTK], F32, kind="ExternalInput").ap()
    x_tm = nc.dram_tensor("x3_tm", [TTK, D], F32, kind="ExternalInput").ap()
    wg_sl = nc.dram_tensor("wg_sl", [4 * D, D], F32, kind="ExternalInput").ap()
    wu_sl = nc.dram_tensor("wu_sl", [4 * D, D], F32, kind="ExternalInput").ap()
    wd_sl = nc.dram_tensor("wd_sl", [4 * D, D], F32, kind="ExternalInput").ap()
    eoff = nc.dram_tensor("eoff", [1, 1], F32, kind="ExternalInput").ap()
    ybuf = nc.dram_tensor("ybuf_sl", [4 * L1F_CP, D], BF16, kind="ExternalOutput").ap()
    o_slotc = nc.dram_tensor("slotc", [128, 256], U32, kind="ExternalOutput").ap()
    o_gatec = nc.dram_tensor("gatec", [128, 256], F32, kind="ExternalOutput").ap()
    iv = nc.dram_tensor("iv", [4 * L1F_CP + 128, 4], U32).ap()

    with contextlib.ExitStack() as es:
        S = es.enter_context
        lgch = S(nc.sbuf_tensor("lgch", [32, 512], F32))
        lgT = S(nc.sbuf_tensor("lgT", [128, 4 * 32], F32))
        mx8 = S(nc.sbuf_tensor("mx8", [128, 8], F32))
        zeq = S(nc.sbuf_tensor("zeq", [128, 32], F32))
        zin = S(nc.sbuf_tensor("zin", [128, 32], F32))
        ones32r = S(nc.sbuf_tensor("ones32r", [128, 32], F32))
        ecol = S(nc.sbuf_tensor("ecol", [128, 4], F32))
        gcol = S(nc.sbuf_tensor("gcol", [128, 4], F32))
        den = S(nc.sbuf_tensor("den", [128, 4], F32))
        etmp = S(nc.sbuf_tensor("etmp", [128, 4 * 32], F32))
        erow = S(nc.sbuf_tensor("erow", [1, 512], F32))
        oh_tm = S(nc.sbuf_tensor("oh_tm", [128, 4 * 32], F32))
        iota32r = S(nc.sbuf_tensor("iota32r", [128, 32], F32))
        ones32 = S(nc.sbuf_tensor("ones32", [32, 512], F32))
        onescol = S(nc.sbuf_tensor("onescol", [32, 1], F32))
        oh = S(nc.sbuf_tensor("oh", [32, 512], F32))
        rk = S(nc.sbuf_tensor("rk", [32, 512], F32))
        car32 = S(nc.sbuf_tensor("car32", [32, 1], F32))
        posr = S(nc.sbuf_tensor("posr", [1, 512], F32))
        slotr = S(nc.sbuf_tensor("slotr", [1, 512], F32))
        slotr2 = S(nc.sbuf_tensor("slotr2", [1, 512], F32))
        slocr = S(nc.sbuf_tensor("slocr", [1, 512], F32))
        slocr2 = S(nc.sbuf_tensor("slocr2", [1, 512], F32))
        slocc2 = S(nc.sbuf_tensor("slocc2", [128, 4], F32))
        bigr = S(nc.sbuf_tensor("bigr", [1, 512], F32))
        mskr = S(nc.sbuf_tensor("mskr", [1, 512], U8))
        slotg_c = S(nc.sbuf_tensor("slotg_c", [128, 4], U32))
        slcols = S(nc.sbuf_tensor("slcols", [128, 256], U32))
        ids = S(nc.sbuf_tensor("ids", [128, 4 * 256], U32))
        ivcols = S(nc.sbuf_tensor("ivcols", [128, 16], U32))
        eoffc = S(nc.sbuf_tensor("eoffc", [1, 1], F32))
        zu = S(nc.sbuf_tensor("zu", [49, 4 * 128], U32))
        trash4 = S(nc.sbuf_tensor("trash4", [128, 4], F32))
        slocc = S(nc.sbuf_tensor("slocc", [128, 4], F32))
        mskc = S(nc.sbuf_tensor("mskc", [128, 4], U8))
        xg_tm = S(nc.sbuf_tensor("xg_tm", [128, 4 * D], F32))
        xg_bf = S(nc.sbuf_tensor("xg_bf", [128, 4 * D], BF16))
        x_fm = S(nc.sbuf_tensor("x_fm", [128, 8 * 512], BF16))
        wg_bf = S(nc.sbuf_tensor("wg_bf", [128, 8 * D], BF16))
        wu_bf = S(nc.sbuf_tensor("wu_bf", [128, 8 * D], BF16))
        wd_bf = S(nc.sbuf_tensor("wd_bf", [128, 8 * D], BF16))
        g_bf = S(nc.sbuf_tensor("g_bf", [128, 8 * 512], BF16))
        u_bf2 = S(nc.sbuf_tensor("u_bf2", [128, 8 * 512], BF16))
        y_bf = S(nc.sbuf_tensor("y_bf", [128, 8 * 512], BF16))
        y_tmo = S(nc.sbuf_tensor("y_tmo", [128, 4 * D], BF16))
        ident_f = S(nc.sbuf_tensor("ident_f7", [128, 128], F32))
        identb = S(nc.sbuf_tensor("ident7b", [128, 128], BF16))
        pMM = S(nc.psum_tensor("pMM", [128, 512], F32))
        pT = S(nc.psum_tensor("pT7", [128, 512], F32))
        pTb = S(nc.psum_tensor("pT7b", [128, 512], BF16))

        ph = Ph(nc, S)

        def setup_io(e):
            e.iota(iota32r[:, :], pattern=[[1, 32]], base=0, channel_multiplier=0,
                   allow_small_or_imprecise_dtypes=True)
            e.iota(ids[:, :], pattern=[[128, 256], [0, 4]], base=0, channel_multiplier=1)
            e.iota(trash4[:, :], pattern=[[0, 4]], base=4 * L1F_CP, channel_multiplier=1,
                   allow_small_or_imprecise_dtypes=True)
            return e.iota(ident_f[:, :], pattern=[[-1, 128]], base=0, channel_multiplier=1,
                          allow_small_or_imprecise_dtypes=True)
        ph.on("gpsimd", setup_io)

        def setup_ms(e):
            e.memset(ones32[:, :], 1.0)
            e.memset(ones32r[:, :], 1.0)
            e.memset(onescol[:, :], 1.0)
            e.memset(bigr[:, :], L1F_BIG)
            e.memset(zu[:, :], 0)
            return e.memset(car32[:, :], 0.0)
        ph.on("vector", setup_ms)

        def ld_eoff(e):
            ph.tag(e.dma_start(out=eoffc[:, :], in_=eoff))
        ph.on("sync", ld_eoff, ndma=1)
        ph.bar()
        ph.on("vector", lambda e: e.tensor_scalar(ident_f[:, :], ident_f[:, :], 0.0, None, ALU.is_equal))
        ph.bar()
        ph.on("vector", lambda e: e.tensor_copy(identb[:, :], ident_f[:, :]))
        ph.bar()

        def iv_init(e):
            ph.tag(e.dma_start(out=iv.rearrange("(a p) c -> a (p c)", p=128), in_=zu[:, :]))
        ph.on("sync", iv_init, ndma=1)
        ph.bar()

        for ci in range(NCH):
            c0 = ci * 512

            def ldlg(e, c0=c0):
                ph.tag(e.dma_start(out=lgch[:, :], in_=lg[:, c0:c0 + 512]))
            ph.on("sync", ldlg, ndma=1)
            ph.bar()

            def trlg(e):
                r = None
                for tt in range(4):
                    r = e.transpose(pT[:, tt * 32:(tt + 1) * 32],
                                    lgch[:, tt * 128:(tt + 1) * 128], ident_f[0:32, 0:32])
                return r
            ph.on("tensor", trlg)
            ph.bar()
            ph.on("vector", lambda e: e.tensor_copy(lgT[:, :], pT[:, 0:128]))
            ph.bar()
            for tt in range(4):
                sl32 = slice(tt * 32, (tt + 1) * 32)
                ph.on("vector", lambda e, sl32=sl32: e.max(mx8[:, :], lgT[:, sl32]))
                ph.bar()
                ph.on("vector", lambda e, sl32=sl32: e.tensor_scalar(zeq[:, :], lgT[:, sl32],
                                                                     mx8[:, 0:1], None, ALU.is_equal))
                ph.bar()
                ph.on("vector", lambda e: e.tensor_tensor_scan(zin[:, :], ones32r[:, :], zeq[:, :],
                                                               0.0, ALU.mult, ALU.add))
                ph.bar()
                ph.on("vector", lambda e: e.tensor_scalar(zin[:, :], zin[:, :], 1.0, None, ALU.is_equal))
                ph.bar()
                ph.on("vector", lambda e, tt=tt: e.tensor_mul(oh_tm[:, tt * 32:(tt + 1) * 32], zeq[:, :], zin[:, :]))
                ph.bar()
                ph.on("vector", lambda e, tt=tt: e.tensor_mul(zeq[:, :], oh_tm[:, tt * 32:(tt + 1) * 32], iota32r[:, :]))
                ph.bar()
                ph.on("vector", lambda e, tt=tt: e.reduce_sum(ecol[:, tt:tt + 1], zeq[:, :], axis=AX.X))
                ph.on("vector", lambda e, tt=tt: e.tensor_copy(den[:, tt:tt + 1], mx8[:, 0:1]))
                ph.bar()

                def ex(e, tt=tt, sl32=sl32):
                    return e.tensor_scalar(etmp[:, sl32], lgT[:, sl32],
                                           den[:, tt:tt + 1], None, ALU.subtract)
                ph.on("vector", ex)
                ph.bar()
                ph.on("scalar", lambda e, sl32=sl32: e.activation(etmp[:, sl32], etmp[:, sl32], AF.Exp))
                ph.bar()
                ph.on("vector", lambda e, tt=tt, sl32=sl32: e.reduce_sum(den[:, tt:tt + 1],
                                                                         etmp[:, sl32], axis=AX.X))
                ph.bar()
            ph.on("vector", lambda e: e.reciprocal(gcol[:, :], den[:, :]))
            ph.bar()

            def stg(e, ci=ci):
                ph.tag(e.dma_start(out=o_gatec[:, ci * 4:(ci + 1) * 4], in_=gcol[:, :]))
            ph.on("sync", stg, ndma=1)

            def troh(e):
                r = None
                for tt in range(4):
                    r = e.transpose(pT[0:32, tt * 128:(tt + 1) * 128],
                                    oh_tm[:, tt * 32:(tt + 1) * 32], ident_f[:, :])
                for tt in range(4):
                    r = e.transpose(pMM[0:1, tt * 128:(tt + 1) * 128],
                                    ecol[:, tt:tt + 1], ident_f[:, :])
                return r
            ph.on("tensor", troh)
            ph.bar()

            def evoh(e):
                e.tensor_copy(oh[:, :], pT[0:32, 0:512])
                return e.tensor_copy(erow[:, :], pMM[0:1, :])
            ph.on("vector", evoh)
            ph.bar()
            ph.on("vector", lambda e: e.tensor_tensor_scan(rk[:, :], ones32[:, :], oh[:, :],
                                                           car32[:, 0:1], ALU.mult, ALU.add))
            ph.bar()
            ph.on("vector", lambda e: e.tensor_copy(car32[:, 0:1], rk[:, 511:512]))
            ph.on("vector", lambda e: e.tensor_mul(oh[:, :], oh[:, :], rk[:, :]))
            ph.bar()
            ph.on("tensor", lambda e: e.matmul(pMM[0:1, :], onescol[:, :], oh[:, :], start=True, stop=True))
            ph.bar()
            ph.on("vector", lambda e: e.tensor_scalar(posr[:, :], pMM[0:1, :], 1.0, None, ALU.subtract))
            ph.bar()
            ph.on("vector", lambda e: e.scalar_tensor_tensor(slotr[:, :], erow[:, :], float(L1F_CP),
                                                             posr[:, :], ALU.mult, ALU.add))
            ph.on("vector", lambda e: e.tensor_scalar(mskr[:, :], posr[:, :], float(L1F_C), None, ALU.is_lt))
            ph.bar()
            ph.on("vector", lambda e: e.select(slotr2[:, :], mskr[:, :], slotr[:, :], bigr[:, :]))
            ph.bar()
            ph.on("vector", lambda e: e.tensor_scalar(slocr[:, :], slotr2[:, :], eoffc[0:1, 0:1], None, ALU.subtract))
            ph.bar()
            ph.on("vector", lambda e: e.tensor_scalar(mskr[:, :], slocr[:, :], 0.0, None, ALU.is_ge))
            ph.bar()
            ph.on("vector", lambda e: e.select(slocr2[:, :], mskr[:, :], slocr[:, :], bigr[:, :]))
            ph.bar()

            def trslot(e):
                r = None
                for tt in range(4):
                    r = e.transpose(pT[:, tt:tt + 1], slotr2[0:1, tt * 128:(tt + 1) * 128], ident_f[0:1, 0:1])
                for tt in range(4):
                    r = e.transpose(pT[:, 8 + tt:8 + tt + 1], slocr2[0:1, tt * 128:(tt + 1) * 128], ident_f[0:1, 0:1])
                return r
            ph.on("tensor", trslot)
            ph.bar()

            def evslot(e, ci=ci):
                e.tensor_copy(slotg_c[:, :], pT[:, 0:4])
                return e.tensor_copy(slocc[:, :], pT[:, 8:12])
            ph.on("vector", evslot)
            ph.bar()
            ph.on("vector", lambda e: e.tensor_scalar(mskc[:, :], slocc[:, :], float(4 * L1F_CP), None, ALU.is_lt))
            ph.bar()
            ph.on("vector", lambda e: e.select(slocc2[:, :], mskc[:, :], slocc[:, :], trash4[:, :]))
            ph.bar()
            ph.on("vector", lambda e, ci=ci: e.tensor_copy(slcols[:, ci * 4:(ci + 1) * 4], slocc2[:, :]))
            ph.bar()

            def stslot(e, ci=ci):
                ph.tag(e.dma_start(out=o_slotc[:, ci * 4:(ci + 1) * 4], in_=slotg_c[:, :]))
            ph.on("sync", stslot, ndma=1)
            ph.bar()

        def sct_all(e):
            for g in range(256):
                ph.tag(e.indirect_dma_start(
                    out=iv, out_offset=bass.IndirectOffsetOnAxis(ap=slcols[:, g:g + 1], axis=0),
                    in_=ids[:, g * 4:(g + 1) * 4], in_offset=None))
        ph.on("gpsimd", sct_all, ndma=256)
        ph.bar()

        for el in range(4):
            def ldwe(e, el=el):
                for dst, srcd in ((wg_bf, wg_sl), (wu_bf, wu_sl), (wd_bf, wd_sl)):
                    ph.tag(e.dma_start(out=dst.ap().rearrange("p (a o) -> p a o", a=8),
                                       in_=srcd[el * D:(el + 1) * D, :].rearrange("(a p) o -> p a o", p=128)))
            ph.on("gpsimd", ldwe, ndma=3)
            ph.bar()
            for grp in range(3):
                r0 = el * L1F_CP + grp * 512

                def ldiv(e, r0=r0):
                    ph.tag(e.dma_start(out=ivcols[:, :],
                                       in_=iv.rearrange("(r a) c -> r (a c)", a=4)[r0 // 4:r0 // 4 + 128, :]))
                ph.on("sync", ldiv, ndma=1)
                ph.on("vector", lambda e: e.memset(xg_tm[:, :], 0.0))
                ph.bar()

                def gat(e):
                    for a in range(4):
                        ph.tag(e.indirect_dma_start(
                            out=xg_tm[:, a * D:(a + 1) * D], out_offset=None,
                            in_=x_tm, in_offset=bass.IndirectOffsetOnAxis(ap=ivcols[:, a * 4:a * 4 + 1], axis=0)))
                ph.on("gpsimd", gat, ndma=4)
                ph.bar()
                ph.on("vector", lambda e: e.tensor_copy(xg_bf[:, :], xg_tm[:, :]))
                ph.bar()

                for dt in range(8):
                    def trp(e, dt=dt):
                        r = None
                        for a in range(4):
                            r = e.transpose(pTb[:, a * 128:(a + 1) * 128],
                                            xg_bf[:, a * D + dt * 128:a * D + dt * 128 + 128], identb[:, :])
                        return r
                    ph.on("tensor", trp)
                    ph.bar()
                    ph.on("vector", lambda e, dt=dt: e.tensor_copy(x_fm[:, dt * 512:(dt + 1) * 512], pTb[:, :]))
                    ph.bar()

                for dst, wsb, act in ((g_bf, wg_bf, AF.Silu), (u_bf2, wu_bf, None)):
                    for ft in range(8):
                        def mmg(e, wsb=wsb, ft=ft):
                            r = None
                            for i in range(8):
                                r = e.matmul(pMM[:, :], wsb[:, i * D + ft * 128:i * D + ft * 128 + 128],
                                             x_fm[:, i * 512:(i + 1) * 512], start=(i == 0), stop=(i == 7))
                            return r
                        ph.on("tensor", mmg)
                        ph.bar()
                        if act is None:
                            ph.on("vector", lambda e, dst=dst, ft=ft: e.tensor_copy(dst[:, ft * 512:(ft + 1) * 512], pMM[:, :]))
                        else:
                            ph.on("scalar", lambda e, dst=dst, ft=ft, act=act: e.activation(dst[:, ft * 512:(ft + 1) * 512], pMM[:, :], act))
                        ph.bar()
                ph.on("vector", lambda e: e.tensor_mul(g_bf[:, :], g_bf[:, :], u_bf2[:, :]))
                ph.bar()

                for ht in range(8):
                    def mmy(e, ht=ht):
                        r = None
                        for i in range(8):
                            r = e.matmul(pMM[:, :], wd_bf[:, i * D + ht * 128:i * D + ht * 128 + 128],
                                         g_bf[:, i * 512:(i + 1) * 512], start=(i == 0), stop=(i == 7))
                        return r
                    ph.on("tensor", mmy)
                    ph.bar()
                    ph.on("vector", lambda e, ht=ht: e.tensor_copy(y_bf[:, ht * 512:(ht + 1) * 512], pMM[:, :]))
                    ph.bar()

                for dt in range(8):
                    def trpy(e, dt=dt):
                        r = None
                        for a in range(4):
                            r = e.transpose(pTb[:, a * 128:(a + 1) * 128],
                                            y_bf[:, dt * 512 + a * 128:dt * 512 + a * 128 + 128], identb[:, :])
                        return r
                    ph.on("tensor", trpy)
                    ph.bar()

                    def evy(e, dt=dt):
                        return e.tensor_copy(
                            y_tmo.ap().rearrange("p (a d) -> p a d", a=4)[:, :, dt * 128:(dt + 1) * 128],
                            pTb.ap().rearrange("p (a t) -> p a t", a=4))
                    ph.on("vector", evy)
                    ph.bar()

                def sty(e, r0=r0):
                    ph.tag(e.dma_start(out=ybuf[r0:r0 + 512, :].rearrange("(p a) d -> p a d", a=4),
                                       in_=y_tmo.ap().rearrange("p (a d) -> p a d", a=4)))
                ph.on("sync", sty, ndma=1)
                ph.bar()

        ph.emit(S)
    return nc


def l07_inmaps(lg_g, x3_tm_g, inp):
    ins = []
    for c in range(NCORES):
        ins.append({
            "lg": cont(lg_g),
            "x3_tm": cont(x3_tm_g),
            "wg_sl": cont(inp["L1_ffn_w_gate"][4 * c:4 * c + 4]).reshape(4 * D, D),
            "wu_sl": cont(inp["L1_ffn_w_up"][4 * c:4 * c + 4]).reshape(4 * D, D),
            "wd_sl": cont(inp["L1_ffn_w_down"][4 * c:4 * c + 4]).reshape(4 * D, D),
            "eoff": np.full((1, 1), 4 * c * L1F_CP, np.float32),
        })
    return ins


def build_l08():
    nc = bass.Bass("TRN2", target_bir_lowering=False, num_devices=NCORES)
    CH = 512
    x_tm = nc.dram_tensor("x_tm", [TLOC, D], F32, kind="ExternalInput").ap()
    ybuf = nc.dram_tensor("ybuf", [32 * L1F_CP, D], BF16, kind="ExternalInput").ap()
    slot = nc.dram_tensor("slot", [128, TLOC // 128], U32, kind="ExternalInput").ap()
    gate = nc.dram_tensor("gate", [128, TLOC // 128], F32, kind="ExternalInput").ap()
    out = nc.dram_tensor("y_tm", [TLOC, D], F32, kind="ExternalOutput").ap()

    with contextlib.ExitStack() as es:
        S = es.enter_context
        slc = S(nc.sbuf_tensor("slc", [128, 32], U32))
        slcf = S(nc.sbuf_tensor("slcf", [128, 32], F32))
        mk8 = S(nc.sbuf_tensor("mk8", [128, 32], F32))
        slc2 = S(nc.sbuf_tensor("slc2", [128, 32], U32))
        gc = S(nc.sbuf_tensor("gc", [128, 32], F32))
        yg = S(nc.sbuf_tensor("yg", [128, 4 * D], BF16))
        o_tm = S(nc.sbuf_tensor("o_tm", [128, 4 * D], F32))
        res_tm = S(nc.sbuf_tensor("res_tm", [128, 4 * D], F32))
        ln_sq = S(nc.sbuf_tensor("ln_sq", [128, 4 * D], F32))
        ln_m = S(nc.sbuf_tensor("ln_m", [128, 4], F32))
        ln_v = S(nc.sbuf_tensor("ln_v", [128, 4], F32))
        ln_r = S(nc.sbuf_tensor("ln_r", [128, 4], F32))

        ph = Ph(nc, S)

        def ld0(e):
            ph.tag(e.dma_start(out=slc[:, :], in_=slot))
            ph.tag(e.dma_start(out=gc[:, :], in_=gate))
        ph.on("sync", ld0, ndma=2)
        ph.bar()
        ph.on("vector", lambda e: e.tensor_copy(slcf[:, :], slc[:, :]))
        ph.bar()
        ph.on("vector", lambda e: e.tensor_scalar(mk8[:, :], slcf[:, :], float(32 * L1F_CP), None, ALU.is_lt))
        ph.bar()

        def msk2(e):
            e.tensor_mul(gc[:, :], gc[:, :], mk8[:, :])
            return e.tensor_mul(slcf[:, :], slcf[:, :], mk8[:, :])
        ph.on("vector", msk2)
        ph.bar()
        ph.on("vector", lambda e: e.tensor_copy(slc2[:, :], slcf[:, :]))
        ph.bar()

        for ci in range(TLOC // CH):
            c0 = ci * CH

            def ldres(e, c0=c0):
                ph.tag(e.dma_start(out=res_tm.ap().rearrange("p (a d) -> p a d", a=4),
                                   in_=x_tm[c0:c0 + CH, :].rearrange("(a p) d -> p a d", p=128)))
            ph.on("sync", ldres, ndma=1)
            ph.on("vector", lambda e: e.memset(yg[:, :], 0.0))
            ph.bar()

            def gat(e, ci=ci):
                for a in range(4):
                    ph.tag(e.indirect_dma_start(
                        out=yg[:, a * D:(a + 1) * D], out_offset=None,
                        in_=ybuf, in_offset=bass.IndirectOffsetOnAxis(ap=slc2[:, ci * 4 + a:ci * 4 + a + 1], axis=0)))
            ph.on("gpsimd", gat, ndma=4)
            ph.bar()

            def comb(e, ci=ci):
                r = None
                for a in range(4):
                    r = e.tensor_scalar(o_tm[:, a * D:(a + 1) * D], yg[:, a * D:(a + 1) * D],
                                        gc[:, ci * 4 + a:ci * 4 + a + 1], None, ALU.mult)
                return r
            ph.on("vector", comb)
            ph.bar()
            ph.on("vector", lambda e: e.scalar_tensor_tensor(o_tm[:, :], res_tm[:, :], 2.0, o_tm[:, :],
                                                             ALU.mult, ALU.add))
            ph.bar()
            ln_tm_batch(ph, lambda i: o_tm[:, i * D:(i + 1) * D], lambda i: o_tm[:, i * D:(i + 1) * D],
                        ln_sq, ln_m, ln_v, ln_r, 4, D)

            def store(e, c0=c0):
                ph.tag(e.dma_start(out=out[c0:c0 + CH, :].rearrange("(a p) d -> p a d", p=128),
                                   in_=o_tm.ap().rearrange("p (a d) -> p a d", a=4)))
            ph.on("sync", store, ndma=1)
            ph.bar()

        ph.emit(S)
    return nc


def l08_inmaps(x3_fm_full, ybuf_g, slotc_g, gatec_g):
    ins = []
    for c in range(NCORES):
        b, q = c // 4, c % 4
        g0 = (b * T + q * TLOC) // 128
        ins.append({
            "x_tm": cont(x3_fm_full[b][:, q * TLOC:(q + 1) * TLOC].T),
            "ybuf": np.ascontiguousarray(ybuf_g),
            "slot": np.ascontiguousarray(slotc_g[:, g0:g0 + TLOC // 128]),
            "gate": np.ascontiguousarray(gatec_g[:, g0:g0 + TLOC // 128]),
        })
    return ins


def build_l09():
    nc = bass.Bass("TRN2", target_bir_lowering=False, num_devices=NCORES)
    DI = 2048
    x_tme = nc.dram_tensor("x_tme", [TEXT, D], F32, kind="ExternalInput").ap()
    inwT = nc.dram_tensor("inwT", [D, 2 * DI], F32, kind="ExternalInput").ap()
    cw = nc.dram_tensor("cw", [128, 16 * 4], F32, kind="ExternalInput").ap()
    cb = nc.dram_tensor("cb", [128, 16], F32, kind="ExternalInput").ap()
    xpwT = nc.dram_tensor("xpwT", [DI, 96], F32, kind="ExternalInput").ap()
    dtwT = nc.dram_tensor("dtwT", [64, DI], F32, kind="ExternalInput").ap()
    dtb = nc.dram_tensor("dtb", [128, 16], F32, kind="ExternalInput").ap()
    hlv = nc.dram_tensor("hlv", [128, 1], F32, kind="ExternalInput").ap()
    o_res = nc.dram_tensor("res_act", [DI, TEXT], F32, kind="ExternalOutput").ap()
    o_u = nc.dram_tensor("u", [DI, TEXT], F32, kind="ExternalOutput").ap()
    o_d = nc.dram_tensor("delta", [DI, TEXT], F32, kind="ExternalOutput").ap()
    o_bc = nc.dram_tensor("bc", [32, TEXT], F32, kind="ExternalOutput").ap()

    NT16 = 16
    with contextlib.ExitStack() as es:
        S = es.enter_context
        XW = TEXT + 6
        inw_bf = S(nc.sbuf_tensor("inw_bf", [128, 8 * 2 * DI], BF16))
        xx = S(nc.sbuf_tensor("xx9", [128, 16 * XW // 4], BF16)) if False else None
        xzx = S(nc.sbuf_tensor("xzx9", [128, 8 * XW], BF16))
        cw_c = S(nc.sbuf_tensor("cw_c9", [128, 64], F32))
        cb_c = S(nc.sbuf_tensor("cb_c9", [128, 16], F32))
        dtb_c = S(nc.sbuf_tensor("dtb_c9", [128, 16], F32))
        hl_c = S(nc.sbuf_tensor("hl_c9", [128, 1], F32))
        xpw_bf = S(nc.sbuf_tensor("xpw_bf9", [128, 16 * 96], BF16))
        dtw_bf = S(nc.sbuf_tensor("dtw_bf9", [64, DI], BF16))
        x_tm_t = S(nc.sbuf_tensor("x_tm_t9", [128, 4 * D], F32))
        n_bf = S(nc.sbuf_tensor("n_bf9", [128, 4 * D], BF16))
        u_ch = S(nc.sbuf_tensor("u_ch9", [128, 8 * 512], F32))
        u_bf = S(nc.sbuf_tensor("u_bf9", [128, 8 * 512], BF16))
        xdt = S(nc.sbuf_tensor("xdt9", [64, 512], BF16))
        ident_f = S(nc.sbuf_tensor("ident_f9", [128, 128], F32))
        ident = S(nc.sbuf_tensor("ident9", [128, 128], BF16))
        ln_sq = S(nc.sbuf_tensor("ln_sq9", [128, 4 * D], F32))
        dl1 = x_tm_t
        dl2 = ln_sq
        ln_m = S(nc.sbuf_tensor("ln_m9", [128, 4], F32))
        ln_v = S(nc.sbuf_tensor("ln_v9", [128, 4], F32))
        ln_r = S(nc.sbuf_tensor("ln_r9", [128, 4], F32))
        pMM = S(nc.psum_tensor("pMM9", [128, 512], F32))
        pT = S(nc.psum_tensor("pT9", [128, 512], BF16))

        ph = Ph(nc, S)

        def ld1(e):
            ph.tag(e.dma_start(out=inw_bf.ap().rearrange("p (a o) -> p a o", a=8),
                               in_=inwT.rearrange("(a p) o -> p a o", p=128)))
            ph.tag(e.dma_start(out=xpw_bf.ap().rearrange("p (a o) -> p a o", a=16),
                               in_=xpwT.rearrange("(a p) o -> p a o", p=128)))
            ph.tag(e.dma_start(out=dtw_bf[:, :], in_=dtwT))
        ph.on("gpsimd", ld1, ndma=3)

        def ld2(e):
            for dst, srcd in ((cw_c, cw), (cb_c, cb), (dtb_c, dtb), (hl_c, hlv)):
                ph.tag(e.dma_start(out=dst[:, :], in_=srcd))
        ph.on("sync", ld2, ndma=4)

        def mkid(e):
            return e.iota(ident_f[:, :], pattern=[[-1, 128]], base=0,
                          channel_multiplier=1, allow_small_or_imprecise_dtypes=True)
        ph.on("gpsimd", mkid)
        ph.bar()
        ph.on("vector", lambda e: e.tensor_scalar(ident_f[:, :], ident_f[:, :], 0.0, None, ALU.is_equal))
        ph.bar()
        ph.on("vector", lambda e: e.tensor_copy(ident[:, :], ident_f[:, :]))
        ph.bar()

        chs = [(i * 512, 512) for i in range(TEXT // 512)]
        if TEXT % 512:
            chs.append((TEXT // 512 * 512, TEXT % 512))

        for half in range(2):
            for c0, W in chs:
                TT = W // 128

                def ldc(e, c0=c0, W=W, TT=TT):
                    ph.tag(e.dma_start(out=x_tm_t.ap().rearrange("p (a d) -> p a d", a=4)[:, 0:TT, :],
                                       in_=x_tme[c0:c0 + W, :].rearrange("(a p) d -> p a d", p=128)))
                ph.on("sync", ldc, ndma=1)
                ph.bar()
                ph.on("vector", lambda e, TT=TT: e.tensor_copy(n_bf[:, 0:TT * D], x_tm_t[:, 0:TT * D]))
                ph.bar()
                for dt in range(8):
                    def trp(e, dt=dt, TT=TT):
                        r = None
                        for tt in range(TT):
                            r = e.transpose(pT[:, tt * 128:(tt + 1) * 128],
                                            n_bf[:, tt * D + dt * 128:tt * D + dt * 128 + 128],
                                            ident[:, :])
                        return r
                    ph.on("tensor", trp)
                    ph.bar()
                    ph.on("vector", lambda e, dt=dt, W=W: e.tensor_copy(u_ch[:, dt * 512:dt * 512 + W], pT[:, 0:W]))
                    ph.bar()
                ph.on("vector", lambda e: e.tensor_copy(u_bf[:, :], u_ch[:, :]))
                ph.bar()

                for ot in range(8):
                    go = half * 8 + ot

                    def mmip(e, go=go, W=W):
                        r = None
                        for i in range(8):
                            r = e.matmul(pMM[:, 0:W], inw_bf[:, i * 2 * DI + go * 128:i * 2 * DI + go * 128 + 128],
                                         u_bf[:, i * 512:i * 512 + W], start=(i == 0), stop=(i == 7))
                        return r
                    ph.on("tensor", mmip)
                    ph.bar()

                    def evip(e, ot=ot, c0=c0, W=W):
                        return e.tensor_copy(xzx[:, ot * XW + c0 + 3:ot * XW + c0 + 3 + W], pMM[:, 0:W])
                    ph.on("vector", evip)
                    ph.bar()
                for ot in range(8):
                    go = 16 + half * 8 + ot

                    def mmrs(e, go=go, W=W):
                        r = None
                        for i in range(8):
                            r = e.matmul(pMM[:, 0:W], inw_bf[:, i * 2 * DI + go * 128:i * 2 * DI + go * 128 + 128],
                                         u_bf[:, i * 512:i * 512 + W], start=(i == 0), stop=(i == 7))
                        return r
                    ph.on("tensor", mmrs)
                    ph.bar()

                    def evrs(e, ot=ot, W=W):
                        return e.activation(u_ch[:, ot * 512:ot * 512 + W], pMM[:, 0:W], AF.Silu)
                    ph.on("scalar", evrs)
                    ph.bar()

                def strs(e, c0=c0, W=W, half=half):
                    ph.tag(e.dma_start(
                        out=o_res.rearrange("(h a p) t -> h p a t", h=2, p=128)[half][:, :, c0:c0 + W],
                        in_=u_ch.ap().rearrange("p (a t) -> p a t", a=8)[:, :, 0:W]))
                ph.on("sync", strs, ndma=1)
                ph.bar()


            for c0, W in chs:
                def cvi(e, c0=c0, W=W, half=half):
                    r = None
                    for dt in range(8):
                        gdt = half * 8 + dt
                        for k in range(4):
                            off = c0 + k
                            srcv = xzx[:, dt * XW + off:dt * XW + off + W]
                            wcol = cw_c[:, gdt * 4 + k:gdt * 4 + k + 1]
                            if k == 0:
                                r = e.tensor_scalar(u_ch[:, dt * 512:dt * 512 + W], srcv, wcol, None, ALU.mult)
                            else:
                                r = e.scalar_tensor_tensor(u_ch[:, dt * 512:dt * 512 + W], srcv, wcol,
                                                           u_ch[:, dt * 512:dt * 512 + W], ALU.mult, ALU.add)
                    return r
                ph.on("vector", cvi)
                ph.bar()

                def cvb(e, W=W, half=half):
                    r = None
                    for dt in range(8):
                        gdt = half * 8 + dt
                        r = e.tensor_scalar(u_ch[:, dt * 512:dt * 512 + W],
                                            u_ch[:, dt * 512:dt * 512 + W], cb_c[:, gdt:gdt + 1], None, ALU.add)
                    return r
                ph.on("vector", cvb)
                ph.bar()

                def usil(e, W=W):
                    r = None
                    for dt in range(8):
                        r = e.activation(u_ch[:, dt * 512:dt * 512 + W], u_ch[:, dt * 512:dt * 512 + W], AF.Silu)
                    return r
                ph.on("scalar", usil)
                ph.bar()

                def stu(e, c0=c0, W=W, half=half):
                    ph.tag(e.dma_start(
                        out=o_u.rearrange("(h a p) t -> h p a t", h=2, p=128)[half][:, :, c0:c0 + W],
                        in_=u_ch.ap().rearrange("p (a t) -> p a t", a=8)[:, :, 0:W]))
                ph.on("sync", stu, ndma=1)
                ph.bar()

        for c0, W in chs:
            def ldu(e, c0=c0, W=W):
                ph.tag(e.dma_start(out=u_bf.ap().rearrange("p (a t) -> p a t", a=8)[:, :, 0:W],
                                   in_=o_u.rearrange("(a p) t -> p a t", p=128)[:, 0:8, c0:c0 + W]))
                ph.tag(e.dma_start(out=n_bf.ap().rearrange("p (a t) -> p a t", a=8)[:, :, 0:W],
                                   in_=o_u.rearrange("(a p) t -> p a t", p=128)[:, 8:16, c0:c0 + W]))
            ph.on("gpsimd", ldu, ndma=2)
            ph.bar()

            def mmxd(e, W=W):
                r = None
                for i in range(8):
                    r = e.matmul(pMM[0:96, 0:W], xpw_bf[:, i * 96:(i + 1) * 96],
                                 u_bf[:, i * 512:i * 512 + W], start=(i == 0), stop=False)
                for i in range(8):
                    r = e.matmul(pMM[0:96, 0:W], xpw_bf[:, (8 + i) * 96:(9 + i) * 96],
                                 n_bf[:, i * 512:i * 512 + W], start=False, stop=(i == 7))
                return r
            ph.on("tensor", mmxd)
            ph.bar()

            def evxd(e, W=W):
                return e.tensor_copy(xdt[:, 0:W], pMM[0:64, 0:W])
            ph.on("vector", evxd)
            ph.bar()

            def evbc(e, W=W):
                return e.tensor_copy(ln_sq[0:32, 0:W], pMM[64:96, 0:W])
            ph.on("vector", evbc)
            ph.bar()

            def stbc2(e, c0=c0, W=W):
                ph.tag(e.dma_start(out=o_bc[:, c0:c0 + W], in_=ln_sq[0:32, 0:W]))
            ph.on("sync", stbc2, ndma=1)
            ph.bar()

            for grp in range(2):
                for ot in range(8):
                    got = grp * 8 + ot

                    def mmdt(e, got=got, W=W):
                        return e.matmul(pMM[:, 0:W], dtw_bf[:, got * 128:(got + 1) * 128],
                                        xdt[:, 0:W], start=True, stop=True)
                    ph.on("tensor", mmdt)
                    ph.bar()

                    def evdt(e, ot=ot, got=got, W=W):
                        return e.tensor_scalar(dl1[:, ot * 512:ot * 512 + W], pMM[:, 0:W],
                                               dtb_c[:, got:got + 1], None, ALU.add)
                    ph.on("vector", evdt)
                    ph.bar()

                def sp1(e, W=W):
                    r = None
                    for dt in range(8):
                        r = e.activation(dl2[:, dt * 512:dt * 512 + W], dl1[:, dt * 512:dt * 512 + W], AF.Abs)
                    return r
                ph.on("scalar", sp1)
                ph.bar()

                def sp2(e, W=W):
                    r = None
                    for dt in range(8):
                        r = e.activation(dl2[:, dt * 512:dt * 512 + W], dl2[:, dt * 512:dt * 512 + W], AF.Exp, scale=-1.0)
                    return r
                ph.on("scalar", sp2)
                ph.bar()
                ph.on("vector", lambda e: e.tensor_scalar(dl2[:, :], dl2[:, :], 1.0, None, ALU.add))
                ph.bar()

                def sp3(e, W=W):
                    r = None
                    for dt in range(8):
                        r = e.activation(dl2[:, dt * 512:dt * 512 + W], dl2[:, dt * 512:dt * 512 + W], AF.Ln)
                    return r
                ph.on("scalar", sp3)
                ph.bar()

                def sp4(e, W=W):
                    r = None
                    for dt in range(8):
                        r = e.activation(dl1[:, dt * 512:dt * 512 + W], dl1[:, dt * 512:dt * 512 + W], AF.Relu)
                    return r
                ph.on("scalar", sp4)
                ph.bar()
                ph.on("vector", lambda e: e.tensor_add(dl1[:, :], dl1[:, :], dl2[:, :]))
                ph.bar()

                def std(e, c0=c0, W=W, grp=grp):
                    ph.tag(e.dma_start(
                        out=o_d.rearrange("(g a p) t -> g p a t", g=2, p=128)[grp][:, :, c0:c0 + W],
                        in_=dl1.ap().rearrange("p (a t) -> p a t", a=8)[:, :, 0:W]))
                ph.on("sync", std, ndma=1)
                ph.bar()

        ph.emit(S)
    return nc


def l09_inmaps(x4_fm_full, inp):
    args = dict(
        inwT=fmT(inp["L2_attn_in_w"]),
        cw=cont(inp["L2_attn_conv_w"]).reshape(2048, 4).reshape(16, 128, 4).transpose(1, 0, 2).reshape(128, 64).copy(),
        cb=col128(inp["L2_attn_conv_b"]),
        xpwT=fmT(inp["L2_attn_xproj_w"]),
        dtwT=fmT(inp["L2_attn_dt_w"]),
        dtb=col128(inp["L2_attn_dt_b"]),
    )
    ins = []
    for c in range(NCORES):
        b, q = c // 4, c % 4
        t0 = q * TLOC
        xb = x4_fm_full[b]
        ext = np.zeros((TEXT, D), np.float32)
        lo = t0 - 3
        hi = t0 + TLOC + 3
        slo = max(lo, 0)
        shi = min(hi, T)
        ext[slo - lo:slo - lo + (shi - slo)] = xb[:, slo:shi].T
        m = dict(args)
        m["x_tme"] = ext
        m["hlv"] = np.full((128, 1), 0.0 if q == 0 else 1.0, np.float32)
        ins.append(m)
    return ins


def build_l11():
    nc = bass.Bass("TRN2", target_bir_lowering=False, num_devices=NCORES)
    DI = 2048
    CH = 512
    x_tm = nc.dram_tensor("x_tm", [TLOC, D], F32, kind="ExternalInput").ap()
    yd = nc.dram_tensor("y", [DI, TLOC], F32, kind="ExternalInput").ap()
    ud = nc.dram_tensor("u", [DI, TLOC], F32, kind="ExternalInput").ap()
    rd = nc.dram_tensor("res_act", [DI, TLOC], F32, kind="ExternalInput").ap()
    dpc = nc.dram_tensor("dpc", [128, 16], F32, kind="ExternalInput").ap()
    outwT = nc.dram_tensor("outwT", [DI, D], F32, kind="ExternalInput").ap()
    out = nc.dram_tensor("y_tm", [TLOC, D], F32, kind="ExternalOutput").ap()

    with contextlib.ExitStack() as es:
        S = es.enter_context
        ow_bf = S(nc.sbuf_tensor("ow_bf11", [128, 16 * D], BF16))
        dp_c = S(nc.sbuf_tensor("dp_c11", [128, 16], F32))
        t_y = S(nc.sbuf_tensor("t_y11", [128, 16 * CH], F32))
        t_u = S(nc.sbuf_tensor("t_u11", [128, 16 * CH], F32))
        t_r = S(nc.sbuf_tensor("t_r11", [128, 16 * CH], F32))
        g_bf = S(nc.sbuf_tensor("g_bf11", [128, 16 * CH], BF16))
        o_tm = S(nc.sbuf_tensor("o_tm11", [128, 4 * D], F32))
        res_tm = S(nc.sbuf_tensor("res_tm11", [128, 4 * D], F32))
        ln_sq = S(nc.sbuf_tensor("ln_sq11", [128, 4 * D], F32))
        ln_m = S(nc.sbuf_tensor("ln_m11", [128, 4], F32))
        ln_v = S(nc.sbuf_tensor("ln_v11", [128, 4], F32))
        ln_r = S(nc.sbuf_tensor("ln_r11", [128, 4], F32))
        pO = S(nc.psum_tensor("pO11", [128, 1024], F32))

        ph = Ph(nc, S)

        def ldw(e):
            ph.tag(e.dma_start(out=ow_bf.ap().rearrange("p (a o) -> p a o", a=16),
                               in_=outwT.rearrange("(a p) o -> p a o", p=128)))
        ph.on("gpsimd", ldw, ndma=1)

        def ldc0(e):
            ph.tag(e.dma_start(out=dp_c[:, :], in_=dpc))
        ph.on("sync", ldc0, ndma=1)
        ph.bar()

        for ci in range(TLOC // CH):
            c0 = ci * CH

            def ldc(e, c0=c0):
                for dst, srcd in ((t_y, yd), (t_u, ud), (t_r, rd)):
                    ph.tag(e.dma_start(out=dst.ap().rearrange("p (a t) -> p a t", a=16),
                                       in_=srcd.rearrange("(a p) t -> p a t", p=128)[:, :, c0:c0 + CH]))
                ph.tag(e.dma_start(out=res_tm.ap().rearrange("p (a d) -> p a d", a=4),
                                   in_=x_tm[c0:c0 + CH, :].rearrange("(a p) d -> p a d", p=128)))
            ph.on("sync", ldc, ndma=4)
            ph.bar()

            def dterm(e):
                r = None
                for dt in range(16):
                    sl = slice(dt * CH, (dt + 1) * CH)
                    r = e.scalar_tensor_tensor(t_y[:, sl], t_u[:, sl], dp_c[:, dt:dt + 1],
                                               t_y[:, sl], ALU.mult, ALU.add)
                return r
            ph.on("vector", dterm)
            ph.bar()
            ph.on("vector", lambda e: e.tensor_mul(g_bf[:, :], t_y[:, :], t_r[:, :]))
            ph.bar()

            for tt in range(4):
                def mm2(e, tt=tt):
                    r = None
                    for half in range(2):
                        psl = pO[:, half * 512:(half + 1) * 512]
                        for i in range(16):
                            r = e.matmul(psl, g_bf[:, i * CH + tt * 128:i * CH + tt * 128 + 128],
                                         ow_bf[:, i * D + half * 512:i * D + half * 512 + 512],
                                         start=(i == 0), stop=(i == 15))
                    return r
                ph.on("tensor", mm2)
                ph.bar()

                def evo(e, tt=tt):
                    return e.tensor_add(o_tm[:, tt * D:(tt + 1) * D], res_tm[:, tt * D:(tt + 1) * D], pO[:, :])
                ph.on("vector", evo)
                ph.bar()

            ln_tm_batch(ph, lambda i: o_tm[:, i * D:(i + 1) * D], lambda i: o_tm[:, i * D:(i + 1) * D],
                        ln_sq, ln_m, ln_v, ln_r, 4, D)

            def store(e, c0=c0):
                ph.tag(e.dma_start(out=out[c0:c0 + CH, :].rearrange("(a p) d -> p a d", p=128),
                                   in_=o_tm.ap().rearrange("p (a d) -> p a d", a=4)))
            ph.on("sync", store, ndma=1)
            ph.bar()

        ph.emit(S)
    return nc


def l11_inmaps(x4_fm_full, y_g, u_g, r_g, inp):
    args = dict(dpc=col128(inp["L2_attn_D"]), outwT=fmT(inp["L2_attn_out_w"]))
    ins = []
    for c in range(NCORES):
        b, q = c // 4, c % 4
        cs = b * T + q * TLOC
        m = dict(args)
        m["x_tm"] = cont(x4_fm_full[b][:, q * TLOC:(q + 1) * TLOC].T)
        m["y"] = np.ascontiguousarray(y_g[:, cs:cs + TLOC])
        m["u"] = np.ascontiguousarray(u_g[:, cs:cs + TLOC])
        m["res_act"] = np.ascontiguousarray(r_g[:, cs:cs + TLOC])
        ins.append(m)
    return ins


def build_l12():
    nc = bass.Bass("TRN2", target_bir_lowering=False, num_devices=NCORES)
    CH = 512
    x_tm = nc.dram_tensor("x_tm", [TLOC, D], F32, kind="ExternalInput").ap()
    x_fm = nc.dram_tensor("x_fm", [D, TLOC], F32, kind="ExternalInput").ap()
    wg = nc.dram_tensor("wg", [D, D], F32, kind="ExternalInput").ap()
    bg_r = nc.dram_tensor("bg_r", [1, D], F32, kind="ExternalInput").ap()
    we = nc.dram_tensor("we", [8 * D, D], F32, kind="ExternalInput").ap()
    be_r = nc.dram_tensor("be_r", [1, 8 * D], F32, kind="ExternalInput").ap()
    out = nc.dram_tensor("y_tm", [TLOC, D], F32, kind="ExternalOutput").ap()
    o_part = nc.dram_tensor("o_part", [TLOC, D], F32).ap()

    with contextlib.ExitStack() as es:
        S = es.enter_context
        wg_bf = S(nc.sbuf_tensor("wg_bf12", [128, 8 * D], BF16))
        we_bf = S(nc.sbuf_tensor("we_bf12", [128, 32 * D], BF16))
        bg_row = S(nc.sbuf_tensor("bg_row12", [1, D], BF16))
        be_row = S(nc.sbuf_tensor("be_row12", [1, 8 * D], BF16))
        onesr = S(nc.sbuf_tensor("onesr12", [1, 512], BF16))
        x_bf = S(nc.sbuf_tensor("x_bf12", [128, 8 * CH], BF16))
        lg_tm = S(nc.sbuf_tensor("lg_tm12", [128, 4 * D], F32))
        sq = S(nc.sbuf_tensor("sq12", [128, 4 * D], F32))
        zeq = S(nc.sbuf_tensor("zeq12", [128, D], F32))
        zin = S(nc.sbuf_tensor("zin12", [128, D], F32))
        ones1k = S(nc.sbuf_tensor("ones1k12", [128, D], F32))
        i8row = S(nc.sbuf_tensor("i8row12", [128, D], F32))
        mxc = S(nc.sbuf_tensor("mxc12", [128, 4], F32))
        dnc = S(nc.sbuf_tensor("dnc12", [128, 4], F32))
        pmc = S(nc.sbuf_tensor("pmc12", [128, 4], F32))
        evc = S(nc.sbuf_tensor("evc12", [128, 4], F32))
        msk8 = S(nc.sbuf_tensor("msk812", [128, 4 * 8], F32))
        h_tm = S(nc.sbuf_tensor("h_tm12", [128, 4 * D], F32))
        o_tm = S(nc.sbuf_tensor("o_tm12", [128, 4 * D], F32))
        res_tm = S(nc.sbuf_tensor("res_tm12", [128, 4 * D], F32))
        ln_m = S(nc.sbuf_tensor("ln_m12", [128, 4], F32))
        ln_v = S(nc.sbuf_tensor("ln_v12", [128, 4], F32))
        ln_r = S(nc.sbuf_tensor("ln_r12", [128, 4], F32))
        pMM = S(nc.psum_tensor("pMM12", [128, 1024], F32))

        ph = Ph(nc, S)

        def ldw(e):
            ph.tag(e.dma_start(out=wg_bf.ap().rearrange("p (a o) -> p a o", a=8),
                               in_=wg.rearrange("(a p) o -> p a o", p=128)))
            ph.tag(e.dma_start(out=bg_row[:, :], in_=bg_r))
            ph.tag(e.dma_start(out=be_row[:, :], in_=be_r))
        ph.on("gpsimd", ldw, ndma=3)
        ph.on("gpsimd", lambda e: e.iota(i8row[:, :], pattern=[[0, 128], [1, 8]], base=0,
                                         channel_multiplier=0, allow_small_or_imprecise_dtypes=True))

        def ms(e):
            e.memset(onesr[:, :], 1.0)
            return e.memset(ones1k[:, :], 1.0)
        ph.on("vector", ms)
        ph.bar()

        for grp in range(2):
            def ldwe(e, grp=grp):
                ph.tag(e.dma_start(out=we_bf.ap().rearrange("p (a o) -> p a o", a=32),
                                   in_=we[grp * 4 * D:(grp + 1) * 4 * D, :].rearrange("(a p) o -> p a o", p=128)))
            ph.on("gpsimd", ldwe, ndma=1)
            ph.bar()

            for ci in range(TLOC // CH):
                c0 = ci * CH

                def ldc(e, c0=c0):
                    ph.tag(e.dma_start(out=x_bf.ap().rearrange("p (a t) -> p a t", a=8),
                                       in_=x_fm.rearrange("(a p) t -> p a t", p=128)[:, :, c0:c0 + CH]))
                ph.on("gpsimd", ldc, ndma=1)
                if grp == 1:
                    def ldres(e, c0=c0):
                        ph.tag(e.dma_start(out=res_tm.ap().rearrange("p (a d) -> p a d", a=4),
                                           in_=x_tm[c0:c0 + CH, :].rearrange("(a p) d -> p a d", p=128)))
                    ph.on("sync", ldres, ndma=1)

                    def ldpart(e, c0=c0):
                        ph.tag(e.dma_start(out=o_tm.ap().rearrange("p (a d) -> p a d", a=4),
                                           in_=o_part[c0:c0 + CH, :].rearrange("(a p) d -> p a d", p=128)))
                    ph.on("sync", ldpart, ndma=1)
                ph.bar()

                for tt in range(4):
                    def mmlg(e, tt=tt):
                        r = None
                        for half in range(2):
                            psl = pMM[:, half * 512:(half + 1) * 512]
                            for i in range(8):
                                e.matmul(psl, x_bf[:, i * CH + tt * 128:i * CH + tt * 128 + 128],
                                         wg_bf[:, i * D + half * 512:i * D + half * 512 + 512],
                                         start=(i == 0), stop=False)
                            r = e.matmul(psl, onesr[:, tt * 128:(tt + 1) * 128],
                                         bg_row[:, half * 512:(half + 1) * 512], start=False, stop=True)
                        return r
                    ph.on("tensor", mmlg)
                    ph.bar()
                    ph.on("vector", lambda e, tt=tt: e.tensor_copy(lg_tm[:, tt * D:(tt + 1) * D], pMM[:, :]))
                    ph.bar()

                def red1(e):
                    r = None
                    for tt in range(4):
                        r = e.reduce_max(mxc[:, tt:tt + 1], lg_tm[:, tt * D:(tt + 1) * D], axis=AX.X)
                    return r
                ph.on("vector", red1)
                ph.bar()
                if grp == 1:
                    def esub(e):
                        r = None
                        for tt in range(4):
                            r = e.tensor_scalar(sq[:, tt * D:(tt + 1) * D], lg_tm[:, tt * D:(tt + 1) * D],
                                                mxc[:, tt:tt + 1], None, ALU.subtract)
                        return r
                    ph.on("vector", esub)
                    ph.bar()
                    ph.on("scalar", lambda e: e.activation(sq[:, :], sq[:, :], AF.Exp))
                    ph.bar()

                    def red2(e):
                        r = None
                        for tt in range(4):
                            r = e.reduce_sum(dnc[:, tt:tt + 1], sq[:, tt * D:(tt + 1) * D], axis=AX.X)
                        return r
                    ph.on("vector", red2)
                    ph.bar()
                    ph.on("vector", lambda e: e.reciprocal(pmc[:, :], dnc[:, :]))
                    ph.bar()

                for tt in range(4):
                    sl = slice(tt * D, (tt + 1) * D)
                    ph.on("vector", lambda e, sl=sl, tt=tt: e.tensor_scalar(zeq[:, :], lg_tm[:, sl],
                                                                            mxc[:, tt:tt + 1], None, ALU.is_equal))
                    ph.bar()
                    ph.on("vector", lambda e: e.tensor_tensor_scan(zin[:, :], ones1k[:, :], zeq[:, :],
                                                                   0.0, ALU.mult, ALU.add))
                    ph.bar()
                    ph.on("vector", lambda e: e.tensor_scalar(zin[:, :], zin[:, :], 1.0, None, ALU.is_equal))
                    ph.bar()
                    ph.on("vector", lambda e: e.tensor_mul(zeq[:, :], zeq[:, :], zin[:, :]))
                    ph.bar()
                    ph.on("vector", lambda e: e.tensor_mul(zeq[:, :], zeq[:, :], i8row[:, :]))
                    ph.bar()
                    ph.on("vector", lambda e, tt=tt: e.reduce_sum(evc[:, tt:tt + 1], zeq[:, :], axis=AX.X))
                    ph.bar()

                def mkmask(e, grp=grp):
                    r = None
                    for tt in range(4):
                        for exl in range(4):
                            r = e.tensor_scalar(msk8[:, tt * 8 + exl:tt * 8 + exl + 1], evc[:, tt:tt + 1],
                                                float(grp * 4 + exl), None, ALU.is_equal)
                    return r
                ph.on("vector", mkmask)
                ph.bar()
                if grp == 0:
                    ph.on("vector", lambda e: e.memset(o_tm[:, :], 0.0))
                    ph.bar()

                for exl in range(4):
                    ex = grp * 4 + exl
                    for tt in range(4):
                        def mme(e, exl=exl, ex=ex, tt=tt):
                            r = None
                            for half in range(2):
                                psl = pMM[:, half * 512:(half + 1) * 512]
                                for i in range(8):
                                    e.matmul(psl, x_bf[:, i * CH + tt * 128:i * CH + tt * 128 + 128],
                                             we_bf[:, (exl * 8 + i) * D + half * 512:(exl * 8 + i) * D + half * 512 + 512],
                                             start=(i == 0), stop=False)
                                r = e.matmul(psl, onesr[:, tt * 128:(tt + 1) * 128],
                                             be_row[:, ex * D + half * 512:ex * D + half * 512 + 512],
                                             start=False, stop=True)
                            return r
                        ph.on("tensor", mme)
                        ph.bar()
                        ph.on("scalar", lambda e, tt=tt: e.activation(h_tm[:, tt * D:(tt + 1) * D], pMM[:, :], AF.Relu))
                        ph.bar()

                        def acc(e, exl=exl, tt=tt):
                            return e.scalar_tensor_tensor(o_tm[:, tt * D:(tt + 1) * D], h_tm[:, tt * D:(tt + 1) * D],
                                                          msk8[:, tt * 8 + exl:tt * 8 + exl + 1],
                                                          o_tm[:, tt * D:(tt + 1) * D], ALU.mult, ALU.add)
                        ph.on("vector", acc)
                        ph.bar()

                if grp == 0:
                    def stpart(e, c0=c0):
                        ph.tag(e.dma_start(out=o_part[c0:c0 + CH, :].rearrange("(a p) d -> p a d", p=128),
                                           in_=o_tm.ap().rearrange("p (a d) -> p a d", a=4)))
                    ph.on("sync", stpart, ndma=1)
                    ph.bar()
                else:
                    def fin(e):
                        r = None
                        for tt in range(4):
                            r = e.tensor_scalar(o_tm[:, tt * D:(tt + 1) * D], o_tm[:, tt * D:(tt + 1) * D],
                                                pmc[:, tt:tt + 1], None, ALU.mult)
                        return r
                    ph.on("vector", fin)
                    ph.bar()
                    ph.on("vector", lambda e: e.tensor_add(o_tm[:, :], o_tm[:, :], res_tm[:, :]))
                    ph.bar()
                    ln_tm_batch(ph, lambda i: o_tm[:, i * D:(i + 1) * D], lambda i: o_tm[:, i * D:(i + 1) * D],
                                sq, ln_m, ln_v, ln_r, 4, D)

                    def store(e, c0=c0):
                        ph.tag(e.dma_start(out=out[c0:c0 + CH, :].rearrange("(a p) d -> p a d", p=128),
                                           in_=o_tm.ap().rearrange("p (a d) -> p a d", a=4)))
                    ph.on("sync", store, ndma=1)
                    ph.bar()

        ph.emit(S)
    return nc


def l12_inmaps(x5_fm_full, inp):
    args = dict(
        wg=cont(inp["L2_ffn_Wg"]),
        bg_r=cont(inp["L2_ffn_bg"]).reshape(1, D),
        we=cont(inp["L2_ffn_We"]).reshape(8 * D, D),
        be_r=cont(inp["L2_ffn_be"]).reshape(1, 8 * D),
    )
    ins = []
    for c in range(NCORES):
        b, q = c // 4, c % 4
        m = dict(args)
        m["x_tm"] = cont(x5_fm_full[b][:, q * TLOC:(q + 1) * TLOC].T)
        m["x_fm"] = np.ascontiguousarray(x5_fm_full[b][:, q * TLOC:(q + 1) * TLOC])
        ins.append(m)
    return ins


TE3 = TLOC + 256


def build_l13a():
    nc = bass.Bass("TRN2", target_bir_lowering=False, num_devices=NCORES)
    x_tme = nc.dram_tensor("x_tme", [TE3, D], F32, kind="ExternalInput").ap()
    x_fme = nc.dram_tensor("x_fme", [D, TE3], F32, kind="ExternalInput").ap()
    ipwT = nc.dram_tensor("ipwT", [D, 128], F32, kind="ExternalInput").ap()
    ldec = nc.dram_tensor("ldec", [64, 1], F32, kind="ExternalInput").ap()
    freq = nc.dram_tensor("freq", [64, 1], F32, kind="ExternalInput").ap()
    coupT = nc.dram_tensor("coupT", [8, 16, 16], F32, kind="ExternalInput").ap()
    opwT = nc.dram_tensor("opwT", [128, D], F32, kind="ExternalInput").ap()
    gwT = nc.dram_tensor("gwT", [D, D], F32, kind="ExternalInput").ap()
    gb = nc.dram_tensor("gb", [128, 8], F32, kind="ExternalInput").ap()
    n1g = nc.dram_tensor("n1g", [128, 8], F32, kind="ExternalInput").ap()
    n1b = nc.dram_tensor("n1b", [128, 8], F32, kind="ExternalInput").ap()
    hlv = nc.dram_tensor("hlv", [128, 1], F32, kind="ExternalInput").ap()
    o_x7 = nc.dram_tensor("x7_fm", [D, TLOC], F32, kind="ExternalOutput").ap()

    with contextlib.ExitStack() as es:
        S = es.enter_context
        ipw_bf = S(nc.sbuf_tensor("ipw13", [128, 8 * 128], BF16))
        opw_bf = S(nc.sbuf_tensor("opw13", [128, D], BF16))
        gw_bf = S(nc.sbuf_tensor("gw13", [128, 8 * D], BF16))
        gb_c = S(nc.sbuf_tensor("gb13", [128, 8], F32))
        n1g_c = S(nc.sbuf_tensor("n1g13", [128, 8], F32))
        n1b_c = S(nc.sbuf_tensor("n1b13", [128, 8], F32))
        hl_c = S(nc.sbuf_tensor("hl13", [128, 1], F32))
        bigc = S(nc.sbuf_tensor("bigc13", [128, 128], BF16))
        bigcf = S(nc.sbuf_tensor("bigcf13", [128, 128], F32))
        mcol = S(nc.sbuf_tensor("mcol13", [64, 1], F32))
        thcol = S(nc.sbuf_tensor("thcol13", [64, 1], F32))
        csn = S(nc.sbuf_tensor("csn13", [64, 4], F32))
        Rbr = S(nc.sbuf_tensor("Rbr13", [64, 1], F32))
        Rbi = S(nc.sbuf_tensor("Rbi13", [64, 1], F32))
        R5r = S(nc.sbuf_tensor("R5r13", [64, 1], F32))
        R5i = S(nc.sbuf_tensor("R5i13", [64, 1], F32))
        Bsr = S(nc.sbuf_tensor("Bsr13", [64, 1], F32))
        Bsi = S(nc.sbuf_tensor("Bsi13", [64, 1], F32))
        tA = S(nc.sbuf_tensor("tA13", [64, 1], F32))
        tB = S(nc.sbuf_tensor("tB13", [64, 1], F32))
        tC = S(nc.sbuf_tensor("tC13", [64, 1], F32))
        mbc = S(nc.sbuf_tensor("mbc13", [64, 512], F32))
        Rr = S(nc.sbuf_tensor("Rr13", [64, 512], F32))
        Ri = S(nc.sbuf_tensor("Ri13", [64, 512], F32))
        Rcr = S(nc.sbuf_tensor("Rcr13", [64, 512], F32))
        Rci = S(nc.sbuf_tensor("Rci13", [64, 512], F32))
        t1 = S(nc.sbuf_tensor("t113", [64, 512], F32))
        t2 = S(nc.sbuf_tensor("t213", [64, 512], F32))
        bpr = S(nc.sbuf_tensor("bpr13", [64, 512], F32))
        bpi = S(nc.sbuf_tensor("bpi13", [64, 512], F32))
        gsr = S(nc.sbuf_tensor("gsr13", [64, 512], F32))
        gsi = S(nc.sbuf_tensor("gsi13", [64, 512], F32))
        carr = S(nc.sbuf_tensor("carr13", [64, 1], F32))
        cari = S(nc.sbuf_tensor("cari13", [64, 1], F32))
        cc = S(nc.sbuf_tensor("cc13", [128, 512], BF16))
        x_tm_t = S(nc.sbuf_tensor("xtm13", [128, 4 * D], F32))
        n_bf = S(nc.sbuf_tensor("nbf13", [128, 4 * D], BF16))
        xn_fm = S(nc.sbuf_tensor("xnfm13", [128, 8 * 512], BF16))
        x_fm = S(nc.sbuf_tensor("xfm13", [128, 8 * 512], F32))
        betar = S(nc.sbuf_tensor("betar13", [64, 512], F32))
        betai = S(nc.sbuf_tensor("betai13", [64, 512], F32))
        eig_bf = S(nc.sbuf_tensor("eig13", [128, 512], BF16))
        o_fm = S(nc.sbuf_tensor("ofm13", [128, 8 * 512], F32))
        x7_fm = S(nc.sbuf_tensor("x7fm13", [128, 8 * 512], F32))
        ident_f = S(nc.sbuf_tensor("idf13", [128, 128], F32))
        ident = S(nc.sbuf_tensor("id13", [128, 128], BF16))
        ln_sq = S(nc.sbuf_tensor("lnsq13", [128, 4 * D], F32))
        ln_m = S(nc.sbuf_tensor("lnm13", [128, 4], F32))
        ln_v = S(nc.sbuf_tensor("lnv13", [128, 4], F32))
        ln_r = S(nc.sbuf_tensor("lnr13", [128, 4], F32))
        pMM = S(nc.psum_tensor("pMM13", [128, 512], F32))
        pT = S(nc.psum_tensor("pT13", [128, 512], BF16))

        ph = Ph(nc, S)

        def ldw(e):
            ph.tag(e.dma_start(out=ipw_bf.ap().rearrange("p (a o) -> p a o", a=8),
                               in_=ipwT.rearrange("(a p) o -> p a o", p=128)))
            ph.tag(e.dma_start(out=opw_bf[:, :], in_=opwT))
            ph.tag(e.dma_start(out=gw_bf.ap().rearrange("p (a o) -> p a o", a=8),
                               in_=gwT.rearrange("(a p) o -> p a o", p=128)))
        ph.on("gpsimd", ldw, ndma=3)

        def ldc(e):
            for dst, srcd in ((gb_c, gb), (n1g_c, n1g), (n1b_c, n1b), (hl_c, hlv),
                              (mcol, ldec), (thcol, freq)):
                ph.tag(e.dma_start(out=dst[:, :], in_=srcd))
        ph.on("sync", ldc, ndma=6)

        def mkid(e):
            return e.iota(ident_f[:, :], pattern=[[-1, 128]], base=0,
                          channel_multiplier=1, allow_small_or_imprecise_dtypes=True)
        ph.on("gpsimd", mkid)
        ph.bar()
        ph.on("vector", lambda e: e.tensor_scalar(ident_f[:, :], ident_f[:, :], 0.0, None, ALU.is_equal))
        ph.on("vector", lambda e: e.memset(bigcf[:, :], 0.0))
        ph.bar()
        ph.on("vector", lambda e: e.tensor_copy(ident[:, :], ident_f[:, :]))
        ph.bar()

        def ldcoup(e):
            for g in range(8):
                ph.tag(e.dma_start(out=bigcf[g * 16:(g + 1) * 16, g * 16:(g + 1) * 16], in_=coupT[g]))
        ph.on("gpsimd", ldcoup, ndma=8)
        ph.bar()
        ph.on("vector", lambda e: e.tensor_copy(bigc[:, :], bigcf[:, :]))
        ph.on("scalar", lambda e: e.activation(mcol[:, :], mcol[:, :], AF.Sigmoid))
        ph.bar()
        ph.on("vector", lambda e: e.tensor_scalar(csn[:, 0:1], thcol[:, :], float(np.pi), None, ALU.is_gt))
        ph.bar()
        ph.on("vector", lambda e: e.scalar_tensor_tensor(thcol[:, :], csn[:, 0:1], float(-2 * np.pi),
                                                         thcol[:, :], ALU.mult, ALU.add))
        ph.bar()
        ph.on("scalar", lambda e: e.activation(csn[:, 0:1], thcol[:, :], AF.Sin, scale=0.5))
        ph.bar()
        ph.on("vector", lambda e: e.tensor_mul(csn[:, 1:2], csn[:, 0:1], csn[:, 0:1]))
        ph.bar()
        ph.on("vector", lambda e: e.tensor_scalar(csn[:, 2:3], csn[:, 1:2], -1.0, 1.0, ALU.mult, ALU.add))
        ph.bar()
        ph.on("scalar", lambda e: e.activation(csn[:, 2:3], csn[:, 2:3], AF.Sqrt))
        ph.bar()

        def mk_trig(e):
            e.tensor_mul(csn[:, 3:4], csn[:, 0:1], csn[:, 2:3])
            return e.tensor_scalar(csn[:, 1:2], csn[:, 1:2], -2.0, 1.0, ALU.mult, ALU.add)
        ph.on("vector", mk_trig)
        ph.bar()
        ph.on("vector", lambda e: e.tensor_scalar(csn[:, 3:4], csn[:, 3:4], -2.0, None, ALU.mult))
        ph.bar()

        def rinit(e):
            e.memset(Rr[:, 0:1], 1.0)
            e.memset(Ri[:, 0:1], 0.0)
            e.tensor_copy(Rbr[:, :], csn[:, 1:2])
            e.tensor_copy(Rbi[:, :], csn[:, 3:4])
            e.memset(carr[:, :], 0.0)
            e.memset(cari[:, :], 0.0)
            e.memset(mbc[:, :], 1.0)
            e.memset(Bsr[:, :], 1.0)
            return e.memset(Bsi[:, :], 0.0)
        ph.on("vector", rinit)
        ph.bar()
        ph.on("vector", lambda e: e.tensor_scalar(mbc[:, :], mbc[:, :], mcol[:, 0:1], None, ALU.mult))
        ph.bar()
        n = 1
        while n < 512:
            m2 = min(n, 512 - n)

            def dbl(e, n=n, m2=m2):
                e.tensor_scalar(t1[:, 0:m2], Rr[:, 0:m2], Rbr[:, 0:1], None, ALU.mult)
                e.tensor_scalar(t2[:, 0:m2], Ri[:, 0:m2], Rbi[:, 0:1], None, ALU.mult)
                e.tensor_scalar(gsr[:, 0:m2], Rr[:, 0:m2], Rbi[:, 0:1], None, ALU.mult)
                return e.tensor_scalar(gsi[:, 0:m2], Ri[:, 0:m2], Rbr[:, 0:1], None, ALU.mult)
            ph.on("vector", dbl)
            ph.bar()

            def dbl2(e, n=n, m2=m2):
                e.tensor_sub(Rr[:, n:n + m2], t1[:, 0:m2], t2[:, 0:m2])
                return e.tensor_add(Ri[:, n:n + m2], gsr[:, 0:m2], gsi[:, 0:m2])
            ph.on("vector", dbl2)
            ph.bar()
            if 2 * n < 512 + 1:
                def sqb(e):
                    e.tensor_mul(tA[:, :], Rbr[:, :], Rbr[:, :])
                    e.tensor_mul(tB[:, :], Rbi[:, :], Rbi[:, :])
                    return e.tensor_mul(tC[:, :], Rbr[:, :], Rbi[:, :])
                ph.on("vector", sqb)
                ph.bar()

                def sqb2(e):
                    e.tensor_sub(Rbr[:, :], tA[:, :], tB[:, :])
                    return e.tensor_scalar(Rbi[:, :], tC[:, :], 2.0, None, ALU.mult)
                ph.on("vector", sqb2)
                ph.bar()
            n *= 2
        def setr5(e):
            e.tensor_copy(R5r[:, :], Rbr[:, :])
            return e.tensor_copy(R5i[:, :], Rbi[:, :])
        ph.on("vector", setr5)
        ph.bar()

        chs = [(0, 256)] + [(256 + i * 512, 512) for i in range(TLOC // 512)]
        for c0, W in chs:
            TT = W // 128
            halo = (c0 == 0)

            def ldx(e, c0=c0, W=W, TT=TT, halo=halo):
                ph.tag(e.dma_start(out=x_tm_t.ap().rearrange("p (a d) -> p a d", a=4)[:, 0:TT, :],
                                   in_=x_tme[c0:c0 + W, :].rearrange("(a p) d -> p a d", p=128)))
                if not halo:
                    ph.tag(e.dma_start(out=x_fm.ap().rearrange("p (a t) -> p a t", a=8)[:, :, 0:W],
                                       in_=x_fme.rearrange("(a p) t -> p a t", p=128)[:, :, c0:c0 + W]))
            ph.on("sync", ldx, ndma=1 if halo else 2)
            ph.bar()
            ln_tm_batch(ph, lambda i: x_tm_t[:, i * D:(i + 1) * D],
                        lambda i: n_bf[:, i * D:(i + 1) * D],
                        ln_sq, ln_m, ln_v, ln_r, TT, D)
            for dt in range(8):
                def trp(e, dt=dt, TT=TT):
                    r = None
                    for tt in range(TT):
                        r = e.transpose(pT[:, tt * 128:(tt + 1) * 128],
                                        n_bf[:, tt * D + dt * 128:dt * 128 + tt * D + 128],
                                        ident[:, :])
                    return r
                ph.on("tensor", trp)
                ph.bar()

                def evt(e, dt=dt, W=W):
                    return e.tensor_scalar(xn_fm[:, dt * 512:dt * 512 + W], pT[:, 0:W],
                                           n1g_c[:, dt:dt + 1], n1b_c[:, dt:dt + 1], ALU.mult, ALU.add)
                ph.on("vector", evt)
                ph.bar()

            def mmb(e, W=W):
                r = None
                for i in range(8):
                    r = e.matmul(pMM[:, 0:W], ipw_bf[:, i * 128:(i + 1) * 128],
                                 xn_fm[:, i * 512:i * 512 + W], start=(i == 0), stop=(i == 7))
                return r
            ph.on("tensor", mmb)
            ph.bar()
            if halo:
                def evb(e, W=W):
                    e.tensor_scalar(betar[:, 0:W], pMM[0:64, 0:W], hl_c[0:64, 0:1], None, ALU.mult)
                    return e.tensor_scalar(betai[:, 0:W], pMM[64:128, 0:W], hl_c[0:64, 0:1], None, ALU.mult)
            else:
                def evb(e, W=W):
                    e.tensor_copy(betar[:, 0:W], pMM[0:64, 0:W])
                    return e.tensor_copy(betai[:, 0:W], pMM[64:128, 0:W])
            ph.on("vector", evb)
            ph.bar()

            def mkrc(e, W=W):
                e.tensor_scalar(Rcr[:, 0:W], Rr[:, 0:W], Bsr[:, 0:1], None, ALU.mult)
                e.tensor_scalar(t1[:, 0:W], Ri[:, 0:W], Bsi[:, 0:1], None, ALU.mult)
                e.tensor_scalar(Rci[:, 0:W], Rr[:, 0:W], Bsi[:, 0:1], None, ALU.mult)
                return e.tensor_scalar(t2[:, 0:W], Ri[:, 0:W], Bsr[:, 0:1], None, ALU.mult)
            ph.on("vector", mkrc)
            ph.bar()

            def mkrc2(e, W=W):
                e.tensor_sub(Rcr[:, 0:W], Rcr[:, 0:W], t1[:, 0:W])
                return e.tensor_add(Rci[:, 0:W], Rci[:, 0:W], t2[:, 0:W])
            ph.on("vector", mkrc2)
            ph.bar()
            if W == 512:
                def adv(e):
                    e.tensor_mul(tA[:, :], Bsr[:, :], R5r[:, :])
                    e.tensor_mul(tB[:, :], Bsi[:, :], R5i[:, :])
                    e.tensor_mul(tC[:, :], Bsr[:, :], R5i[:, :])
                    return e.tensor_mul(csn[:, 2:3], Bsi[:, :], R5r[:, :])
                ph.on("vector", adv)
                ph.bar()

                def adv2(e):
                    e.tensor_sub(Bsr[:, :], tA[:, :], tB[:, :])
                    return e.tensor_add(Bsi[:, :], tC[:, :], csn[:, 2:3])
                ph.on("vector", adv2)
                ph.bar()
            else:
                def adv(e):
                    e.tensor_scalar(tA[:, :], Bsr[:, :], Rr[:, 256:257], None, ALU.mult)
                    e.tensor_scalar(tB[:, :], Bsi[:, :], Ri[:, 256:257], None, ALU.mult)
                    e.tensor_scalar(tC[:, :], Bsr[:, :], Ri[:, 256:257], None, ALU.mult)
                    return e.tensor_scalar(csn[:, 2:3], Bsi[:, :], Rr[:, 256:257], None, ALU.mult)
                ph.on("vector", adv)
                ph.bar()

                def adv2(e):
                    e.tensor_sub(Bsr[:, :], tA[:, :], tB[:, :])
                    return e.tensor_add(Bsi[:, :], tC[:, :], csn[:, 2:3])
                ph.on("vector", adv2)
                ph.bar()

            def mkbp(e, W=W):
                e.tensor_mul(bpr[:, 0:W], betar[:, 0:W], Rcr[:, 0:W])
                e.tensor_mul(t1[:, 0:W], betai[:, 0:W], Rci[:, 0:W])
                e.tensor_mul(bpi[:, 0:W], betar[:, 0:W], Rci[:, 0:W])
                return e.tensor_mul(t2[:, 0:W], betai[:, 0:W], Rcr[:, 0:W])
            ph.on("vector", mkbp)
            ph.bar()

            def mkbp2(e, W=W):
                e.tensor_sub(bpr[:, 0:W], bpr[:, 0:W], t1[:, 0:W])
                return e.tensor_add(bpi[:, 0:W], bpi[:, 0:W], t2[:, 0:W])
            ph.on("vector", mkbp2)
            ph.bar()

            def sc1(e, W=W):
                return e.tensor_tensor_scan(gsr[:, 0:W], mbc[:, 0:W], bpr[:, 0:W],
                                            carr[:, 0:1], ALU.mult, ALU.add)
            ph.on("vector", sc1)
            ph.bar()

            def sc2(e, W=W):
                return e.tensor_tensor_scan(gsi[:, 0:W], mbc[:, 0:W], bpi[:, 0:W],
                                            cari[:, 0:1], ALU.mult, ALU.add)
            ph.on("vector", sc2)
            ph.bar()

            def upcar(e, W=W):
                e.tensor_copy(carr[:, 0:1], gsr[:, W - 1:W])
                return e.tensor_copy(cari[:, 0:1], gsi[:, W - 1:W])
            ph.on("vector", upcar)
            ph.bar()

            if not halo:
                def mkc(e, W=W):
                    e.tensor_mul(t1[:, 0:W], gsr[:, 0:W], Rcr[:, 0:W])
                    e.tensor_mul(t2[:, 0:W], gsi[:, 0:W], Rci[:, 0:W])
                    e.tensor_mul(bpr[:, 0:W], gsi[:, 0:W], Rcr[:, 0:W])
                    return e.tensor_mul(bpi[:, 0:W], gsr[:, 0:W], Rci[:, 0:W])
                ph.on("vector", mkc)
                ph.bar()

                def mkc2(e, W=W):
                    e.tensor_add(cc[0:64, 0:W], t1[:, 0:W], t2[:, 0:W])
                    return e.tensor_sub(cc[64:128, 0:W], bpr[:, 0:W], bpi[:, 0:W])
                ph.on("vector", mkc2)
                ph.bar()

                def mmcoup(e, W=W):
                    return e.matmul(pMM[:, 0:W], bigc[:, :], cc[:, 0:W], start=True, stop=True)
                ph.on("tensor", mmcoup)
                ph.bar()
                ph.on("vector", lambda e, W=W: e.tensor_copy(eig_bf[:, 0:W], pMM[:, 0:W]))
                ph.bar()

                for ot in range(8):
                    def mmop(e, ot=ot, W=W):
                        return e.matmul(pMM[:, 0:W], opw_bf[:, ot * 128:(ot + 1) * 128],
                                        eig_bf[:, 0:W], start=True, stop=True)
                    ph.on("tensor", mmop)
                    ph.bar()
                    ph.on("vector", lambda e, ot=ot, W=W: e.tensor_copy(o_fm[:, ot * 512:ot * 512 + W], pMM[:, 0:W]))
                    ph.bar()

                for ot in range(8):
                    def mmg(e, ot=ot, W=W):
                        r = None
                        for i in range(8):
                            r = e.matmul(pMM[:, 0:W], gw_bf[:, i * D + ot * 128:i * D + ot * 128 + 128],
                                         xn_fm[:, i * 512:i * 512 + W], start=(i == 0), stop=(i == 7))
                        return r
                    ph.on("tensor", mmg)
                    ph.bar()

                    def evg(e, ot=ot, W=W):
                        return e.activation(x7_fm[:, ot * 512:ot * 512 + W], pMM[:, 0:W],
                                            AF.Sigmoid, bias=gb_c[:, ot:ot + 1])
                    ph.on("scalar", evg)
                    ph.bar()
                def mkx7(e, W=W):
                    r = None
                    for dt in range(8):
                        sl = slice(dt * 512, dt * 512 + W)
                        r = e.tensor_mul(x7_fm[:, sl], x7_fm[:, sl], o_fm[:, sl])
                    return r
                ph.on("vector", mkx7)
                ph.bar()

                def mkx72(e, W=W):
                    r = None
                    for dt in range(8):
                        sl = slice(dt * 512, dt * 512 + W)
                        r = e.tensor_add(x7_fm[:, sl], x7_fm[:, sl], x_fm[:, sl])
                    return r
                ph.on("vector", mkx72)
                ph.bar()

                def stx7(e, c0=c0, W=W):
                    ph.tag(e.dma_start(out=o_x7.rearrange("(a p) t -> p a t", p=128)[:, :, c0 - 256:c0 - 256 + W],
                                       in_=x7_fm.ap().rearrange("p (a t) -> p a t", a=8)[:, :, 0:W]))
                ph.on("sync", stx7, ndma=1)
                ph.bar()

        ph.emit(S)
    return nc


def l13a_inmaps(x6_fm_full, inp):
    coupT = cont(inp["L3_attn_coupling"]).transpose(0, 2, 1)
    coupT = np.ascontiguousarray(np.concatenate([coupT, coupT], axis=0))
    args = dict(
        ipwT=fmT(inp["L3_attn_in_proj_w"]),
        ldec=cont(inp["L3_attn_log_decay"]).reshape(64, 1),
        freq=cont(inp["L3_attn_frequency"]).reshape(64, 1),
        coupT=coupT,
        opwT=fmT(inp["L3_attn_out_proj_w"]),
        gwT=fmT(inp["L3_attn_gate_w"]),
        gb=col128(inp["L3_attn_gate_b"]),
        n1g=col128(inp["L3_attn_n1_g"]), n1b=col128(inp["L3_attn_n1_b"]),
    )
    ins = []
    for c in range(NCORES):
        b, q = c // 4, c % 4
        t0 = q * TLOC
        xb = x6_fm_full[b]
        ext = np.zeros((TE3, D), np.float32)
        lo = t0 - 256
        slo = max(lo, 0)
        ext[slo - lo:256] = xb[:, slo:t0].T
        ext[256:] = xb[:, t0:t0 + TLOC].T
        extf = np.zeros((D, TE3), np.float32)
        extf[:, slo - lo:256] = xb[:, slo:t0]
        extf[:, 256:] = xb[:, t0:t0 + TLOC]
        m = dict(args)
        m["x_tme"] = ext
        m["x_fme"] = extf
        m["hlv"] = np.full((128, 1), 0.0 if q == 0 else 1.0, np.float32)
        ins.append(m)
    return ins


def build_l13b():
    nc = bass.Bass("TRN2", target_bir_lowering=False, num_devices=NCORES)
    x7_tm = nc.dram_tensor("x7_tm", [TLOC, D], F32, kind="ExternalInput").ap()
    x6_tm = nc.dram_tensor("x6_tm", [TLOC, D], F32, kind="ExternalInput").ap()
    n2g = nc.dram_tensor("n2g", [128, 8], F32, kind="ExternalInput").ap()
    n2b = nc.dram_tensor("n2b", [128, 8], F32, kind="ExternalInput").ap()
    w1T = nc.dram_tensor("w1T", [D, 4 * D], F32, kind="ExternalInput").ap()
    b1c = nc.dram_tensor("b1c", [128, 32], F32, kind="ExternalInput").ap()
    w2n = nc.dram_tensor("w2n", [4 * D, D], F32, kind="ExternalInput").ap()
    b2r = nc.dram_tensor("b2r", [1, D], F32, kind="ExternalInput").ap()
    wr8 = nc.dram_tensor("wr8", [D, 8], F32, kind="ExternalInput").ap()
    br8 = nc.dram_tensor("br8", [1, 8], F32, kind="ExternalInput").ap()
    out = nc.dram_tensor("y_tm", [TLOC, D], F32, kind="ExternalOutput").ap()
    o_lg8 = nc.dram_tensor("lg8", [8, TLOC], F32, kind="ExternalOutput").ap()

    with contextlib.ExitStack() as es:
        S = es.enter_context
        w1_bf = S(nc.sbuf_tensor("w1b", [128, 8 * 4 * D], BF16))
        w2_bf = S(nc.sbuf_tensor("w2b", [128, 32 * D], BF16))
        wr_bf = S(nc.sbuf_tensor("wrb", [128, 8 * 8], BF16))
        br_row = S(nc.sbuf_tensor("brr", [1, 8], BF16))
        b2_row = S(nc.sbuf_tensor("b2rw", [1, D], BF16))
        b1_c = S(nc.sbuf_tensor("b1cc", [128, 32], F32))
        n2g_c = S(nc.sbuf_tensor("n2gc", [128, 8], F32))
        n2b_c = S(nc.sbuf_tensor("n2bc", [128, 8], F32))
        onesr = S(nc.sbuf_tensor("onesrB", [1, 512], BF16))
        x7t = S(nc.sbuf_tensor("x7t", [128, 4 * D], F32))
        res6 = S(nc.sbuf_tensor("res6", [128, 4 * D], F32))
        n2_bf = S(nc.sbuf_tensor("n2bf", [128, 4 * D], BF16))
        n2_fm = S(nc.sbuf_tensor("n2fm", [128, 8 * 512], BF16))
        mid_bf = S(nc.sbuf_tensor("midb", [128, 32 * 128], BF16))
        o_tm = S(nc.sbuf_tensor("otmB", [128, 4 * D], F32))
        lg8_sb = S(nc.sbuf_tensor("lg8sb", [8, 512], F32))
        ident_f = S(nc.sbuf_tensor("idfB", [128, 128], F32))
        ident = S(nc.sbuf_tensor("idB", [128, 128], BF16))
        ln_m = S(nc.sbuf_tensor("lnmB", [128, 4], F32))
        ln_v = S(nc.sbuf_tensor("lnvB", [128, 4], F32))
        ln_r = S(nc.sbuf_tensor("lnrB", [128, 4], F32))
        pMM = S(nc.psum_tensor("pMMB", [128, 512], F32))
        pY = S(nc.psum_tensor("pYB", [128, 1024], F32))
        pT = S(nc.psum_tensor("pTB", [128, 512], BF16))

        ph = Ph(nc, S)

        def ldw(e):
            ph.tag(e.dma_start(out=w1_bf.ap().rearrange("p (a o) -> p a o", a=8),
                               in_=w1T.rearrange("(a p) o -> p a o", p=128)))
            ph.tag(e.dma_start(out=w2_bf.ap().rearrange("p (a o) -> p a o", a=32),
                               in_=w2n.rearrange("(a p) o -> p a o", p=128)))
            ph.tag(e.dma_start(out=wr_bf.ap().rearrange("p (a o) -> p a o", a=8),
                               in_=wr8.rearrange("(a p) o -> p a o", p=128)))
            ph.tag(e.dma_start(out=br_row[:, :], in_=br8))
            ph.tag(e.dma_start(out=b2_row[:, :], in_=b2r))
        ph.on("gpsimd", ldw, ndma=5)

        def ldc(e):
            for dst, srcd in ((b1_c, b1c), (n2g_c, n2g), (n2b_c, n2b)):
                ph.tag(e.dma_start(out=dst[:, :], in_=srcd))
        ph.on("sync", ldc, ndma=3)

        def mkid(e):
            return e.iota(ident_f[:, :], pattern=[[-1, 128]], base=0,
                          channel_multiplier=1, allow_small_or_imprecise_dtypes=True)
        ph.on("gpsimd", mkid)
        ph.on("vector", lambda e: e.memset(onesr[:, :], 1.0))
        ph.bar()
        ph.on("vector", lambda e: e.tensor_scalar(ident_f[:, :], ident_f[:, :], 0.0, None, ALU.is_equal))
        ph.bar()
        ph.on("vector", lambda e: e.tensor_copy(ident[:, :], ident_f[:, :]))
        ph.bar()

        CH = 512
        for ci in range(TLOC // CH):
            c0 = ci * CH

            def ldx(e, c0=c0):
                ph.tag(e.dma_start(out=x7t.ap().rearrange("p (a d) -> p a d", a=4),
                                   in_=x7_tm[c0:c0 + CH, :].rearrange("(a p) d -> p a d", p=128)))
                ph.tag(e.dma_start(out=res6.ap().rearrange("p (a d) -> p a d", a=4),
                                   in_=x6_tm[c0:c0 + CH, :].rearrange("(a p) d -> p a d", p=128)))
            ph.on("sync", ldx, ndma=2)
            ph.bar()
            ln_tm_batch(ph, lambda i: x7t[:, i * D:(i + 1) * D],
                        lambda i: n2_bf[:, i * D:(i + 1) * D],
                        o_tm, ln_m, ln_v, ln_r, 4, D)
            for dt in range(8):
                def trp(e, dt=dt):
                    r = None
                    for tt in range(4):
                        r = e.transpose(pT[:, tt * 128:(tt + 1) * 128],
                                        n2_bf[:, tt * D + dt * 128:tt * D + dt * 128 + 128],
                                        ident[:, :])
                    return r
                ph.on("tensor", trp)
                ph.bar()

                def evt(e, dt=dt):
                    return e.tensor_scalar(n2_fm[:, dt * 512:(dt + 1) * 512], pT[:, :],
                                           n2g_c[:, dt:dt + 1], n2b_c[:, dt:dt + 1], ALU.mult, ALU.add)
                ph.on("vector", evt)
                ph.bar()

            for tt in range(4):
                for ft in range(32):
                    def mm1(e, tt=tt, ft=ft):
                        r = None
                        for i in range(8):
                            r = e.matmul(pMM[:, 0:128], w1_bf[:, i * 4 * D + ft * 128:i * 4 * D + ft * 128 + 128],
                                         n2_fm[:, i * 512 + tt * 128:i * 512 + tt * 128 + 128],
                                         start=(i == 0), stop=(i == 7))
                        return r
                    ph.on("tensor", mm1)
                    ph.bar()

                    def ev1(e, ft=ft):
                        return e.activation(mid_bf[:, ft * 128:(ft + 1) * 128], pMM[:, 0:128],
                                            AF.Silu, bias=b1_c[:, ft:ft + 1])
                    ph.on("scalar", ev1)
                    ph.bar()

                def mm2(e, tt=tt):
                    r = None
                    for half in range(2):
                        psl = pY[:, half * 512:(half + 1) * 512]
                        for i in range(32):
                            e.matmul(psl, mid_bf[:, i * 128:(i + 1) * 128],
                                     w2_bf[:, i * D + half * 512:i * D + half * 512 + 512],
                                     start=(i == 0), stop=False)
                        r = e.matmul(psl, onesr[:, 0:128], b2_row[:, half * 512:(half + 1) * 512],
                                     start=False, stop=True)
                    return r
                ph.on("tensor", mm2)
                ph.bar()

                def evy(e, tt=tt):
                    return e.tensor_add(o_tm[:, tt * D:(tt + 1) * D], x7t[:, tt * D:(tt + 1) * D], pY[:, :])
                ph.on("vector", evy)
                ph.bar()
            ph.on("vector", lambda e: e.tensor_add(o_tm[:, :], o_tm[:, :], res6[:, :]))
            ph.bar()
            ln_tm_batch(ph, lambda i: o_tm[:, i * D:(i + 1) * D], lambda i: o_tm[:, i * D:(i + 1) * D],
                        x7t, ln_m, ln_v, ln_r, 4, D)

            def store(e, c0=c0):
                ph.tag(e.dma_start(out=out[c0:c0 + CH, :].rearrange("(a p) d -> p a d", p=128),
                                   in_=o_tm.ap().rearrange("p (a d) -> p a d", a=4)))
            ph.on("sync", store, ndma=1)
            ph.on("vector", lambda e: e.tensor_copy(n2_bf[:, :], o_tm[:, :]))
            ph.bar()

            for dt in range(8):
                def trp8(e, dt=dt):
                    r = None
                    for tt in range(4):
                        r = e.transpose(pT[:, tt * 128:(tt + 1) * 128],
                                        n2_bf[:, tt * D + dt * 128:tt * D + dt * 128 + 128],
                                        ident[:, :])
                    return r
                ph.on("tensor", trp8)
                ph.bar()
                ph.on("vector", lambda e, dt=dt: e.tensor_copy(n2_fm[:, dt * 512:(dt + 1) * 512], pT[:, :]))
                ph.bar()

            def mmlg(e):
                r = None
                for i in range(8):
                    e.matmul(pMM[0:8, :], wr_bf[:, i * 8:(i + 1) * 8],
                             n2_fm[:, i * 512:(i + 1) * 512], start=(i == 0), stop=False)
                r = e.matmul(pMM[0:8, :], onesr[0:1, 0:8].rearrange("a b -> a b") if False else br_row[:, :],
                             onesr[0:1, 0:512], start=False, stop=True)
                return r
            ph.on("tensor", mmlg)
            ph.bar()
            ph.on("vector", lambda e: e.tensor_copy(lg8_sb[:, :], pMM[0:8, :]))
            ph.bar()

            def stlg(e, c0=c0):
                ph.tag(e.dma_start(out=o_lg8[:, c0:c0 + CH], in_=lg8_sb[:, :]))
            ph.on("sync", stlg, ndma=1)
            ph.bar()

        ph.emit(S)
    return nc


def l13b_inmaps(x7_fm_full, x6_fm_full, inp):
    args = dict(
        n2g=col128(inp["L3_attn_n2_g"]), n2b=col128(inp["L3_attn_n2_b"]),
        w1T=fmT(inp["L3_attn_mlp_w1"]),
        b1c=col128(inp["L3_attn_mlp_b1"]),
        w2n=fmT(inp["L3_attn_mlp_w2"]),
        b2r=cont(inp["L3_attn_mlp_b2"]).reshape(1, D),
        wr8=cont(inp["L3_ffn_Wr"]),
        br8=cont(inp["L3_ffn_br"]).reshape(1, 8),
    )
    ins = []
    for c in range(NCORES):
        b, q = c // 4, c % 4
        m = dict(args)
        m["x7_tm"] = cont(x7_fm_full[b][:, q * TLOC:(q + 1) * TLOC].T)
        m["x6_tm"] = cont(x6_fm_full[b][:, q * TLOC:(q + 1) * TLOC].T)
        ins.append(m)
    return ins


L3F_ROWS = 9 * 512 * NCORES


def build_l14():
    nc = bass.Bass("TRN2", target_bir_lowering=False, num_devices=NCORES)
    TTK = NB * T
    NCH = TTK // 512
    lg = nc.dram_tensor("lg8", [8, TTK], F32, kind="ExternalInput").ap()
    x_tm = nc.dram_tensor("x8_tm", [TTK, D], F32, kind="ExternalInput").ap()
    w1d = nc.dram_tensor("w1d", [8 * D, 4 * D], F32, kind="ExternalInput").ap()
    b1d = nc.dram_tensor("b1d", [8, 4 * D], F32, kind="ExternalInput").ap()
    w2d = nc.dram_tensor("w2d", [8 * 4 * D, D], F32, kind="ExternalInput").ap()
    b2d = nc.dram_tensor("b2d", [8, D], F32, kind="ExternalInput").ap()
    eoff3 = nc.dram_tensor("eoff3", [1, 1], F32, kind="ExternalInput").ap()
    ybuf = nc.dram_tensor("ybuf3_sl", [9 * 512, D], BF16, kind="ExternalOutput").ap()
    o_slotc = nc.dram_tensor("slotc3", [128, 256], U32, kind="ExternalOutput").ap()
    iv2 = nc.dram_tensor("iv2", [9 * 512 + 128, 4], U32).ap()
    gate4 = nc.dram_tensor("gate4", [TTK, 4], F32).ap()

    with contextlib.ExitStack() as es:
        S = es.enter_context
        lgT = S(nc.sbuf_tensor("lgTF", [128, 32], F32))
        mx8 = S(nc.sbuf_tensor("mx8F", [128, 8], F32))
        zeq = S(nc.sbuf_tensor("zeqF", [128, 8], F32))
        zin = S(nc.sbuf_tensor("zinF", [128, 8], F32))
        ones8r = S(nc.sbuf_tensor("ones8rF", [128, 8], F32))
        iota8r = S(nc.sbuf_tensor("iota8rF", [128, 8], F32))
        ecol = S(nc.sbuf_tensor("ecolF", [128, 4], F32))
        gcol = S(nc.sbuf_tensor("gcolF", [128, 4], F32))
        den = S(nc.sbuf_tensor("denF", [128, 4], F32))
        et8 = S(nc.sbuf_tensor("et8F", [128, 8], F32))
        car8 = S(nc.sbuf_tensor("car8F", [8, 1], F32))
        ones8c = S(nc.sbuf_tensor("ones8cF", [8, 512], F32))
        one8col = S(nc.sbuf_tensor("one8colF", [8, 1], F32))
        posc = S(nc.sbuf_tensor("poscF", [128, 256], F32))
        ecc = S(nc.sbuf_tensor("eccF", [128, 256], F32))
        gatec = S(nc.sbuf_tensor("gatecF", [128, 256], F32))
        slotf = S(nc.sbuf_tensor("slotfF", [128, 256], F32))
        slocf = S(nc.sbuf_tensor("slocfF", [128, 256], F32))
        slotu = S(nc.sbuf_tensor("slotuF", [128, 256], U32))
        slocu = S(nc.sbuf_tensor("slocuF", [128, 256], U32))
        mskc = S(nc.sbuf_tensor("mskcF", [128, 256], U8))
        trashc = S(nc.sbuf_tensor("trashcF", [128, 256], F32))
        iotaS = S(nc.sbuf_tensor("iotaSF", [8, 64], F32))
        nco = S(nc.sbuf_tensor("ncoF", [8, 4], F32))
        orow = S(nc.sbuf_tensor("orowF", [1, 8], F32))
        prow = S(nc.sbuf_tensor("prowF", [1, 8], F32))
        obc = S(nc.sbuf_tensor("obcF", [128, 8], F32))
        i72 = S(nc.sbuf_tensor("i72F", [8, 9], F32))
        ecloc = S(nc.sbuf_tensor("eclocF", [1, 9], F32))
        eoffc = S(nc.sbuf_tensor("eoffcF", [1, 1], F32))
        eoffb = S(nc.sbuf_tensor("eoffbF", [128, 1], F32))
        ids4 = S(nc.sbuf_tensor("ids4F", [128, 4 * 256], U32))
        iotac8 = S(nc.sbuf_tensor("iotac8F", [128, 8], F32))
        iotac32 = S(nc.sbuf_tensor("iotac32F", [128, 32], F32))
        ecb = S(nc.sbuf_tensor("ecbF", [128, 1], F32))
        idxw = S(nc.sbuf_tensor("idxwF", [128, 8], F32))
        idxw2 = S(nc.sbuf_tensor("idxw2F", [128, 32], F32))
        idxwu = S(nc.sbuf_tensor("idxwuF", [128, 8], U32))
        idxw2u = S(nc.sbuf_tensor("idxw2uF", [128, 32], U32))
        oh8c = S(nc.sbuf_tensor("oh8cF", [8, 1], F32))
        b2row = S(nc.sbuf_tensor("b2rowF", [1, D], BF16))
        b1cols = S(nc.sbuf_tensor("b1colsF", [128, 32], F32))
        ivcols = S(nc.sbuf_tensor("ivcolsF", [128, 16], U32))
        xg_tm = S(nc.sbuf_tensor("xg_tmF", [128, 4 * D], F32))
        lgch = xg_tm[0:8, 0:512]
        oh8 = xg_tm[0:8, 512:1024]
        rk8 = xg_tm[0:8, 1024:1536]
        posr = xg_tm[0:1, 1536:2048]
        x_fm = S(nc.sbuf_tensor("x_fmF", [128, 8 * 512], BF16))
        w1g = S(nc.sbuf_tensor("w1gF", [128, 8 * 4 * D], BF16))
        w2g = S(nc.sbuf_tensor("w2gF", [128, 32 * D], BF16))
        h_bf = S(nc.sbuf_tensor("h_bfF", [128, 32 * 512], BF16))
        y_tmo = S(nc.sbuf_tensor("y_tmoF", [128, D], BF16))
        ggt = S(nc.sbuf_tensor("ggtF", [128, 16], F32))
        onesr = S(nc.sbuf_tensor("onesrF", [1, 512], BF16))
        zerou = S(nc.sbuf_tensor("zerouF", [128, 37 * 4], U32))
        ident_f = S(nc.sbuf_tensor("idfF", [128, 128], F32))
        identb = S(nc.sbuf_tensor("idbF", [128, 128], BF16))
        pMM = S(nc.psum_tensor("pMMF", [128, 512], F32))
        pT = S(nc.psum_tensor("pTF", [128, 512], F32))
        pY = S(nc.psum_tensor("pYF", [128, 1024], F32))
        pTb = S(nc.psum_tensor("pTbF", [128, 512], BF16))

        ph = Ph(nc, S)

        def setup_io(e):
            e.iota(iota8r[:, :], pattern=[[1, 8]], base=0, channel_multiplier=0,
                   allow_small_or_imprecise_dtypes=True)
            e.iota(ids4[:, :], pattern=[[128, 256], [0, 4]], base=0, channel_multiplier=1)
            e.iota(trashc[:, :], pattern=[[0, 256]], base=9 * 512, channel_multiplier=1,
                   allow_small_or_imprecise_dtypes=True)
            e.iota(iotaS[:, :], pattern=[[512, 64]], base=512, channel_multiplier=0,
                   allow_small_or_imprecise_dtypes=True)
            e.iota(i72[:, :], pattern=[[512, 9]], base=0, channel_multiplier=0,
                   allow_small_or_imprecise_dtypes=True)
            e.iota(iotac8[:, :], pattern=[[128, 8]], base=0, channel_multiplier=1,
                   allow_small_or_imprecise_dtypes=True)
            e.iota(iotac32[:, :], pattern=[[128, 32]], base=0, channel_multiplier=1,
                   allow_small_or_imprecise_dtypes=True)
            return e.iota(ident_f[:, :], pattern=[[-1, 128]], base=0, channel_multiplier=1,
                          allow_small_or_imprecise_dtypes=True)
        ph.on("gpsimd", setup_io)

        def setup_ms(e):
            e.memset(ones8r[:, :], 1.0)
            e.memset(ones8c[:, :], 1.0)
            e.memset(one8col[:, :], 1.0)
            e.memset(car8[:, :], 0.0)
            e.memset(onesr[:, :], 1.0)
            return e.memset(zerou[:, :], 0)
        ph.on("vector", setup_ms)

        def ld0(e):
            ph.tag(e.dma_start(out=eoffc[:, :], in_=eoff3))
        ph.on("sync", ld0, ndma=1)
        ph.bar()
        ph.on("vector", lambda e: e.tensor_scalar(ident_f[:, :], ident_f[:, :], 0.0, None, ALU.is_equal))
        ph.bar()
        ph.on("vector", lambda e: e.tensor_copy(identb[:, :], ident_f[:, :]))
        ph.bar()

        def iv_init(e):
            ph.tag(e.dma_start(out=iv2.rearrange("(p a) c -> p (a c)", p=128), in_=zerou[:, :]))
        ph.on("sync", iv_init, ndma=1)
        ph.bar()

        for ci in range(NCH):
            c0 = ci * 512

            def ldlg(e, c0=c0):
                ph.tag(e.dma_start(out=lgch[:, :], in_=lg[:, c0:c0 + 512]))
            ph.on("sync", ldlg, ndma=1)
            ph.bar()

            def trlg(e):
                r = None
                for tt in range(4):
                    r = e.transpose(pT[:, tt * 8:(tt + 1) * 8],
                                    lgch[:, tt * 128:(tt + 1) * 128], ident_f[0:8, 0:8])
                return r
            ph.on("tensor", trlg)
            ph.bar()
            ph.on("vector", lambda e: e.tensor_copy(lgT[:, :], pT[:, 0:32]))
            ph.bar()
            for tt in range(4):
                sl8 = slice(tt * 8, (tt + 1) * 8)
                ph.on("vector", lambda e, sl8=sl8: e.max(mx8[:, :], lgT[:, sl8]))
                ph.bar()
                ph.on("vector", lambda e, sl8=sl8: e.tensor_scalar(zeq[:, :], lgT[:, sl8],
                                                                   mx8[:, 0:1], None, ALU.is_equal))
                ph.bar()
                ph.on("vector", lambda e: e.tensor_tensor_scan(zin[:, :], ones8r[:, :], zeq[:, :],
                                                               0.0, ALU.mult, ALU.add))
                ph.bar()
                ph.on("vector", lambda e: e.tensor_scalar(zin[:, :], zin[:, :], 1.0, None, ALU.is_equal))
                ph.bar()
                ph.on("vector", lambda e: e.tensor_mul(zeq[:, :], zeq[:, :], zin[:, :]))
                ph.bar()
                def troh(e, tt=tt):
                    return e.transpose(pT[0:8, 256 + tt * 128:256 + (tt + 1) * 128] if False
                                       else pT[0:8, tt * 128:(tt + 1) * 128],
                                       zeq[:, :], ident_f[:, :])
                ph.on("tensor", troh)
                ph.bar()
                ph.on("vector", lambda e, tt=tt: e.tensor_mul(et8[:, :], zeq[:, :], iota8r[:, :]))
                ph.bar()
                ph.on("vector", lambda e, tt=tt: e.reduce_sum(ecol[:, tt:tt + 1], et8[:, :], axis=AX.X))
                ph.on("vector", lambda e, tt=tt: e.tensor_copy(den[:, tt:tt + 1], mx8[:, 0:1]))
                ph.bar()

                def ex(e, tt=tt, sl8=sl8):
                    return e.tensor_scalar(et8[:, :], lgT[:, sl8], den[:, tt:tt + 1], None, ALU.subtract)
                ph.on("vector", ex)
                ph.bar()
                ph.on("scalar", lambda e: e.activation(et8[:, :], et8[:, :], AF.Exp))
                ph.bar()
                ph.on("vector", lambda e, tt=tt: e.reduce_sum(den[:, tt:tt + 1], et8[:, :], axis=AX.X))
                ph.bar()
            ph.on("vector", lambda e: e.reciprocal(gcol[:, :], den[:, :]))
            ph.bar()
            ph.on("vector", lambda e: e.tensor_copy(oh8[:, :], pT[0:8, 0:512]))
            ph.bar()
            ph.on("vector", lambda e: e.tensor_tensor_scan(rk8[:, :], ones8c[:, :], oh8[:, :],
                                                           car8[:, 0:1], ALU.mult, ALU.add))
            ph.bar()
            ph.on("vector", lambda e: e.tensor_copy(car8[:, 0:1], rk8[:, 511:512]))
            ph.on("vector", lambda e: e.tensor_mul(oh8[:, :], oh8[:, :], rk8[:, :]))
            ph.bar()
            ph.on("tensor", lambda e: e.matmul(pMM[0:1, :], one8col[:, :], oh8[:, :], start=True, stop=True))
            ph.bar()
            ph.on("vector", lambda e: e.tensor_scalar(posr[:, :], pMM[0:1, :], 1.0, None, ALU.subtract))
            ph.bar()

            def trcols(e):
                r = None
                for tt in range(4):
                    r = e.transpose(pT[:, tt:tt + 1], posr[0:1, tt * 128:(tt + 1) * 128], ident_f[0:1, 0:1])
                return r
            ph.on("tensor", trcols)
            ph.bar()

            def evcols(e, ci=ci):
                e.tensor_copy(posc[:, ci * 4:(ci + 1) * 4], pT[:, 0:4])
                e.tensor_copy(ecc[:, ci * 4:(ci + 1) * 4], ecol[:, :])
                return e.tensor_copy(gatec[:, ci * 4:(ci + 1) * 4], gcol[:, :])
            ph.on("vector", evcols)
            ph.bar()

        ph.on("vector", lambda e: e.tensor_copy(nco[:, 0:1], car8[:, 0:1]))
        ph.bar()
        ph.on("vector", lambda e: e.tensor_scalar(iotaS[:, :], iotaS[:, :], nco[:, 0:1], None, ALU.is_le))
        ph.bar()
        ph.on("vector", lambda e: e.reduce_sum(nco[:, 1:2], iotaS[:, :], axis=AX.X))
        ph.bar()

        def mkpad(e):
            e.scalar_tensor_tensor(nco[:, 2:3], nco[:, 1:2], -512.0, nco[:, 0:1], ALU.mult, ALU.add)
            return None
        def mkpad1(e):
            return e.scalar_tensor_tensor(nco[:, 2:3], nco[:, 1:2], -512.0, nco[:, 0:1], ALU.mult, ALU.add)
        ph.on("vector", mkpad1)
        ph.bar()
        ph.on("vector", lambda e: e.tensor_scalar(nco[:, 2:3], nco[:, 2:3], 0.0, None, ALU.is_gt))
        ph.bar()
        ph.on("vector", lambda e: e.tensor_add(nco[:, 3:4], nco[:, 1:2], nco[:, 2:3]))
        ph.bar()
        ph.on("vector", lambda e: e.tensor_scalar(nco[:, 3:4], nco[:, 3:4], 512.0, None, ALU.mult))
        ph.bar()
        ph.on("tensor", lambda e: e.transpose(pT[0:1, 0:8], nco[:, 3:4], ident_f[0:8, 0:8]))
        ph.bar()
        ph.on("vector", lambda e: e.tensor_copy(prow[:, :], pT[0:1, 0:8]))
        ph.bar()
        ph.on("vector", lambda e: e.tensor_tensor_scan(orow[:, :], ones8c[0:1, 0:8], prow[:, :],
                                                       0.0, ALU.mult, ALU.add))
        ph.bar()
        ph.on("vector", lambda e: e.tensor_sub(orow[:, :], orow[:, :], prow[:, :]))
        ph.bar()
        ph.on("tensor", lambda e: e.matmul(pT[:, 0:8], ones8c[0:1, 0:128], orow[:, :], start=True, stop=True))
        ph.bar()
        ph.on("vector", lambda e: e.tensor_copy(obc[:, :], pT[:, 0:8]))
        ph.bar()
        ph.on("vector", lambda e: e.tensor_copy(slotf[:, :], posc[:, :]))
        ph.bar()
        for ex in range(8):
            def addo(e, ex=ex):
                e.tensor_scalar(slocf[:, :], ecc[:, :], float(ex), None, ALU.is_equal)
                return None
            def addo1(e, ex=ex):
                return e.tensor_scalar(slocf[:, :], ecc[:, :], float(ex), None, ALU.is_equal)
            ph.on("vector", addo1)
            ph.bar()
            ph.on("vector", lambda e, ex=ex: e.scalar_tensor_tensor(slotf[:, :], slocf[:, :],
                                                                    obc[0:128, ex:ex + 1], slotf[:, :],
                                                                    ALU.mult, ALU.add))
            ph.bar()
        ph.on("vector", lambda e: e.tensor_copy(slotu[:, :], slotf[:, :]))
        ph.bar()

        def stslot(e):
            ph.tag(e.dma_start(out=o_slotc, in_=slotu[:, :]))
        ph.on("sync", stslot, ndma=1)
        ph.bar()
        ph.on("tensor", lambda e: e.matmul(pT[:, 100:101], ones8c[0:1, 0:128], eoffc[0:1, 0:1],
                                           start=True, stop=True))
        ph.bar()
        ph.on("vector", lambda e: e.tensor_copy(eoffb[:, :], pT[:, 100:101]))
        ph.bar()
        ph.on("vector", lambda e: e.tensor_scalar(slocf[:, :], slotf[:, :], eoffb[:, 0:1], None, ALU.subtract))
        ph.bar()
        ph.on("vector", lambda e: e.tensor_scalar(mskc[:, :], slocf[:, :], 0.0, None, ALU.is_ge))
        ph.bar()
        ph.on("vector", lambda e: e.select(slotf[:, :], mskc[:, :], slocf[:, :], trashc[:, :]))
        ph.bar()
        ph.on("vector", lambda e: e.tensor_scalar(mskc[:, :], slotf[:, :], float(9 * 512), None, ALU.is_lt))
        ph.bar()
        ph.on("vector", lambda e: e.select(slocf[:, :], mskc[:, :], slotf[:, :], trashc[:, :]))
        ph.bar()
        ph.on("vector", lambda e: e.tensor_copy(slocu[:, :], slocf[:, :]))
        ph.bar()

        for g0 in range(0, 256, 64):
            def sct_b(e, g0=g0):
                for g in range(g0, g0 + 64):
                    ph.tag(e.indirect_dma_start(
                        out=iv2, out_offset=bass.IndirectOffsetOnAxis(ap=slocu[:, g:g + 1], axis=0),
                        in_=ids4[:, g * 4:(g + 1) * 4], in_offset=None))
            ph.on("gpsimd", sct_b, ndma=64)
            ph.bar()

        def repg(e):
            r = None
            for i in range(4):
                r = e.tensor_copy(xg_tm.ap().rearrange("p (a c) -> p a c", c=4)[:, 0:256, i:i + 1],
                                  gatec[:, :].rearrange("p (a c) -> p a c", c=1))
            return r
        ph.on("vector", repg)
        ph.bar()

        def stg4(e):
            ph.tag(e.dma_start(out=gate4.rearrange("(a p) c -> p a c", p=128),
                               in_=xg_tm[:, 0:1024].rearrange("p (a c) -> p a c", c=4)))
        ph.on("sync", stg4, ndma=1)
        ph.bar()

        ph.on("vector", lambda e: e.tensor_scalar(i72[:, :], i72[:, :], eoffb[0:8, 0:1], None, ALU.add))
        ph.bar()
        ph.on("tensor", lambda e: e.transpose(pT[0:8, 8:9], orow[0:1, 0:8], ident_f[0:1, 0:1]))
        ph.bar()
        ph.on("vector", lambda e: e.tensor_copy(nco[0:8, 0:1], pT[0:8, 8:9]))
        ph.bar()
        ph.on("vector", lambda e: e.tensor_scalar(i72[:, :], i72[:, :], nco[0:8, 0:1], None, ALU.is_ge))
        ph.bar()
        ph.on("tensor", lambda e: e.matmul(pMM[0:1, 0:9], one8col[:, :], i72[:, :], start=True, stop=True))
        ph.bar()
        ph.on("vector", lambda e: e.tensor_scalar(ecloc[:, :], pMM[0:1, 0:9], 1.0, None, ALU.subtract))
        ph.bar()

        for lc in range(9):
            r0 = lc * 512

            def ecb_mm(e, lc=lc):
                return e.matmul(pT[:, 100:101], ones8c[0:1, 0:128], ecloc[0:1, lc:lc + 1],
                                start=True, stop=True)
            ph.on("tensor", ecb_mm)
            ph.bar()
            ph.on("vector", lambda e: e.tensor_copy(ecb[:, :], pT[:, 100:101]))
            ph.bar()
            ph.on("vector", lambda e: e.tensor_scalar(oh8c[:, :], iota8r[0:8, 0:1], ecb[0:8, 0:1],
                                                      None, ALU.is_equal))
            ph.bar()

            def mkidx0(e):
                e.tensor_scalar(ggt[:, 8:9], ecb[:, 0:1], 1024.0, None, ALU.mult)
                return e.tensor_scalar(ggt[:, 9:10], ecb[:, 0:1], 4096.0, None, ALU.mult)
            ph.on("vector", mkidx0)
            ph.bar()

            def mkidx1(e):
                e.tensor_scalar(idxw[:, :], iotac8[:, :], ggt[:, 8:9], None, ALU.add)
                return e.tensor_scalar(idxw2[:, :], iotac32[:, :], ggt[:, 9:10], None, ALU.add)
            ph.on("vector", mkidx1)
            ph.bar()

            def mkidx3(e):
                e.tensor_copy(idxwu[:, :], idxw[:, :])
                return e.tensor_copy(idxw2u[:, :], idxw2[:, :])
            ph.on("vector", mkidx3)
            ph.bar()

            b1_sb = xg_tm[0:8, 0:4 * D]
            b1row = xg_tm[0:1, 0:4 * D]

            def ldb1(e):
                ph.tag(e.dma_start(out=b1_sb, in_=b1d))
            ph.on("sync", ldb1, ndma=1)
            ph.bar()
            for pc in range(8):
                def mmb1p(e, pc=pc):
                    return e.matmul(pMM[0:1, :], oh8c[:, :], b1_sb[:, pc * 512:(pc + 1) * 512],
                                    start=True, stop=True)
                ph.on("tensor", mmb1p)
                ph.bar()
                ph.on("vector", lambda e, pc=pc: e.tensor_copy(b1row[:, pc * 512:(pc + 1) * 512], pMM[0:1, :]))
                ph.bar()

            def trb1(e):
                r = None
                for j in range(32):
                    r = e.transpose(pT[:, j:j + 1], b1row[:, j * 128:(j + 1) * 128], ident_f[0:1, 0:1])
                return r
            ph.on("tensor", trb1)
            ph.bar()
            ph.on("vector", lambda e: e.tensor_copy(b1cols[:, :], pT[:, 0:32]))
            ph.bar()
            b2_sb = xg_tm[0:8, 0:D]

            def ldb2(e):
                ph.tag(e.dma_start(out=b2_sb, in_=b2d))
            ph.on("sync", ldb2, ndma=1)
            ph.bar()
            for pc in range(2):
                def mmb2p(e, pc=pc):
                    return e.matmul(pMM[0:1, :], oh8c[:, :], b2_sb[:, pc * 512:(pc + 1) * 512],
                                    start=True, stop=True)
                ph.on("tensor", mmb2p)
                ph.bar()
                ph.on("vector", lambda e, pc=pc: e.tensor_copy(b2row[:, pc * 512:(pc + 1) * 512], pMM[0:1, :]))
                ph.bar()

            def ldiv(e, r0=r0):
                ph.tag(e.dma_start(out=ivcols[:, :],
                                   in_=iv2.rearrange("(r a) c -> r (a c)", a=4)[r0 // 4:r0 // 4 + 128, :]))
            ph.on("sync", ldiv, ndma=1)
            ph.bar()

            def gat(e):
                for a in range(4):
                    ph.tag(e.indirect_dma_start(
                        out=xg_tm[:, a * D:(a + 1) * D], out_offset=None,
                        in_=x_tm, in_offset=bass.IndirectOffsetOnAxis(ap=ivcols[:, a * 4:a * 4 + 1], axis=0)))
                for a in range(4):
                    ph.tag(e.indirect_dma_start(
                        out=ggt[:, a * 4:(a + 1) * 4], out_offset=None,
                        in_=gate4, in_offset=bass.IndirectOffsetOnAxis(ap=ivcols[:, a * 4:a * 4 + 1], axis=0)))
            ph.on("gpsimd", gat, ndma=8)
            ph.bar()
            for dt in range(8):
                def trp(e, dt=dt):
                    r = None
                    for a in range(4):
                        r = e.transpose(pT[:, a * 128:(a + 1) * 128],
                                        xg_tm[:, a * D + dt * 128:a * D + dt * 128 + 128], ident_f[:, :])
                    return r
                ph.on("tensor", trp)
                ph.bar()
                ph.on("vector", lambda e, dt=dt: e.tensor_copy(x_fm[:, dt * 512:(dt + 1) * 512], pT[:, :]))
                ph.bar()

            wst = xg_tm
            for a in range(8):
                def gw1(e, a=a):
                    ph.tag(e.indirect_dma_start(
                        out=wst[:, :], out_offset=None,
                        in_=w1d, in_offset=bass.IndirectOffsetOnAxis(ap=idxwu[:, a:a + 1], axis=0)))
                ph.on("gpsimd", gw1, ndma=1)
                ph.bar()

                def cw1(e, a=a):
                    ph.tag(e.dma_start(out=w1g[:, a * 4 * D:(a + 1) * 4 * D], in_=wst[:, :]))
                ph.on("gpsimd", cw1, ndma=1)
                ph.bar()

            for ft in range(32):
                def mm1(e, ft=ft):
                    r = None
                    for a in range(8):
                        r = e.matmul(pMM[:, :], w1g[:, a * 4 * D + ft * 128:a * 4 * D + ft * 128 + 128],
                                     x_fm[:, a * 512:(a + 1) * 512], start=(a == 0), stop=(a == 7))
                    return r
                ph.on("tensor", mm1)
                ph.bar()
                ph.on("scalar", lambda e, ft=ft: e.activation(h_bf[:, ft * 512:(ft + 1) * 512], pMM[:, :],
                                                              AF.Relu, bias=b1cols[:, ft:ft + 1]))
                ph.bar()

            for a in range(32):
                def gw2(e, a=a):
                    ph.tag(e.indirect_dma_start(
                        out=wst[:, 0:D], out_offset=None,
                        in_=w2d, in_offset=bass.IndirectOffsetOnAxis(ap=idxw2u[:, a:a + 1], axis=0)))
                ph.on("gpsimd", gw2, ndma=1)
                ph.bar()

                def cw2(e, a=a):
                    ph.tag(e.dma_start(out=w2g[:, a * D:(a + 1) * D], in_=wst[:, 0:D]))
                ph.on("gpsimd", cw2, ndma=1)
                ph.bar()

            for tt in range(4):
                def mm2(e, tt=tt):
                    r = None
                    for half in range(2):
                        psl = pY[:, half * 512:(half + 1) * 512]
                        for i in range(32):
                            e.matmul(psl, h_bf[:, i * 512 + tt * 128:i * 512 + tt * 128 + 128],
                                     w2g[:, i * D + half * 512:i * D + half * 512 + 512],
                                     start=(i == 0), stop=False)
                        r = e.matmul(psl, onesr[0:1, 0:128], b2row[:, half * 512:(half + 1) * 512],
                                     start=False, stop=True)
                    return r
                ph.on("tensor", mm2)
                ph.bar()

                def evy(e, tt=tt):
                    return e.tensor_scalar(y_tmo[:, :], pY[:, :],
                                           ggt[:, tt * 4:tt * 4 + 1], None, ALU.mult)
                ph.on("vector", evy)
                ph.bar()

                def sty(e, r0=r0, tt=tt):
                    ph.tag(e.dma_start(
                        out=ybuf[r0:r0 + 512, :].rearrange("(p a) d -> p (a d)", a=4)[:, tt * D:(tt + 1) * D],
                        in_=y_tmo[:, :]))
                ph.on("sync", sty, ndma=1)
                ph.bar()

        ph.emit(S)
    return nc


def l14_inmaps(lg8_g, x8_tm_g, inp):
    args = dict(
        lg8=cont(lg8_g),
        x8_tm=cont(x8_tm_g),
        w1d=cont(inp["L3_ffn_W1"]).reshape(8 * D, 4 * D),
        b1d=cont(inp["L3_ffn_b1"]),
        w2d=cont(inp["L3_ffn_W2"]).reshape(8 * 4 * D, D),
        b2d=cont(inp["L3_ffn_b2"]),
    )
    ins = []
    for c in range(NCORES):
        m = dict(args)
        m["eoff3"] = np.full((1, 1), 9 * c * 512, np.float32)
        ins.append(m)
    return ins


def build_l15():
    nc = bass.Bass("TRN2", target_bir_lowering=False, num_devices=NCORES)
    CH = 512
    x_tm = nc.dram_tensor("x_tm", [TLOC, D], F32, kind="ExternalInput").ap()
    ybuf = nc.dram_tensor("ybuf", [L3F_ROWS, D], BF16, kind="ExternalInput").ap()
    slot = nc.dram_tensor("slot", [128, TLOC // 128], U32, kind="ExternalInput").ap()
    out = nc.dram_tensor("y_tm", [TLOC, D], F32, kind="ExternalOutput").ap()

    with contextlib.ExitStack() as es:
        S = es.enter_context
        slc = S(nc.sbuf_tensor("slcL", [128, 32], U32))
        yg = S(nc.sbuf_tensor("ygL", [128, 4 * D], BF16))
        o_tm = S(nc.sbuf_tensor("o_tmL", [128, 4 * D], F32))
        res_tm = S(nc.sbuf_tensor("res_tmL", [128, 4 * D], F32))
        ln_sq = S(nc.sbuf_tensor("ln_sqL", [128, 4 * D], F32))
        ln_m = S(nc.sbuf_tensor("ln_mL", [128, 4], F32))
        ln_v = S(nc.sbuf_tensor("ln_vL", [128, 4], F32))
        ln_r = S(nc.sbuf_tensor("ln_rL", [128, 4], F32))

        ph = Ph(nc, S)

        def ld0(e):
            ph.tag(e.dma_start(out=slc[:, :], in_=slot))
        ph.on("sync", ld0, ndma=1)
        ph.bar()

        for ci in range(TLOC // CH):
            c0 = ci * CH

            def ldres(e, c0=c0):
                ph.tag(e.dma_start(out=res_tm.ap().rearrange("p (a d) -> p a d", a=4),
                                   in_=x_tm[c0:c0 + CH, :].rearrange("(a p) d -> p a d", p=128)))
            ph.on("sync", ldres, ndma=1)
            ph.bar()

            def gat(e, ci=ci):
                for a in range(4):
                    ph.tag(e.indirect_dma_start(
                        out=yg[:, a * D:(a + 1) * D], out_offset=None,
                        in_=ybuf, in_offset=bass.IndirectOffsetOnAxis(ap=slc[:, ci * 4 + a:ci * 4 + a + 1], axis=0)))
            ph.on("gpsimd", gat, ndma=4)
            ph.bar()
            ph.on("vector", lambda e: e.tensor_add(o_tm[:, :], res_tm[:, :], yg[:, :]))
            ph.bar()
            ln_tm_batch(ph, lambda i: o_tm[:, i * D:(i + 1) * D], lambda i: o_tm[:, i * D:(i + 1) * D],
                        ln_sq, ln_m, ln_v, ln_r, 4, D)

            def store(e, c0=c0):
                ph.tag(e.dma_start(out=out[c0:c0 + CH, :].rearrange("(a p) d -> p a d", p=128),
                                   in_=o_tm.ap().rearrange("p (a d) -> p a d", a=4)))
            ph.on("sync", store, ndma=1)
            ph.bar()

        ph.emit(S)
    return nc


def l15_inmaps(x8_fm_full, ybuf_g, slotc_g):
    ins = []
    for c in range(NCORES):
        b, q = c // 4, c % 4
        g0 = (b * T + q * TLOC) // 128
        ins.append({
            "x_tm": cont(x8_fm_full[b][:, q * TLOC:(q + 1) * TLOC].T),
            "ybuf": np.ascontiguousarray(ybuf_g),
            "slot": np.ascontiguousarray(slotc_g[:, g0:g0 + TLOC // 128]),
        })
    return ins
